# version 01
# baseline (speedup 1.0000x reference)
import numpy as np
import concourse.bass as bass
import concourse.mybir as mybir
from concourse.bass_utils import run_bass_kernel_spmd

F32 = mybir.dt.float32
BF16 = mybir.dt.bfloat16
AF = mybir.ActivationFunctionType
ALU = mybir.AluOpType
AX = mybir.AxisListType.X

NCORES = 8
T = 2048
D = 1024
H = 16
DH = 64
NT = T // 128
NKC = D // 128
EPS = 1e-5

N_SUB = 4
DEBUG_DUMPS = False
DEV_PHASE = None
OP_LVL = 4


_PENDING = {}
_SEEN = {}


def traceback_stack():
    import traceback
    return traceback.format_stack()[-6:-2]



class Sem:
    def __init__(self, nc, name):
        self.h = nc.alloc_semaphore(name)
        self.n = 0

    def inc(self, instr, k=1):
        key = id(instr)
        if key in _SEEN and _SEEN[key][0] is self and _SEEN[key][2] is instr:
            import traceback
            raise RuntimeError("double Sem.inc on same instruction: first at\n"
                               + "".join(_SEEN[key][1]))
        _SEEN[key] = (self, traceback_stack(), instr)
        instr.then_inc(self.h, k)
        self.n += k
        w = _PENDING.pop(id(instr), None)
        if w is not None:
            w._last = (self, self.n)
            w._pending = None
        return instr

    def wait(self, eng):
        if self.n:
            eng.wait_ge(self.h, self.n)


class Ctx:
    pass


def build_nc():
    nc = bass.Bass("TRN2", target_bir_lowering=False, debug=False)
    c = Ctx()
    c.nc = nc

    dr = {}

    def din(name, shape, dt=F32):
        dr[name] = nc.dram_tensor(name, list(shape), dt, kind="ExternalInput").ap()
        return dr[name]

    din("y", (T, D))
    din("l0_WqT", (D, D)); din("l0_WkT", (D, D)); din("l0_WvT", (D, D)); din("l0_WoT", (D, D))
    din("l0_bq", (D,)); din("l0_bk", (D,)); din("l0_bv", (D,)); din("l0_bo", (D,))
    din("sq", (16, D))
    for p in ("d", "c"):
        for m in ("Wq", "Wk", "Wv", "Wo"):
            din(f"{p}_{m}", (D, D))
        for m in ("bq", "bk", "bv", "bo"):
            din(f"{p}_{m}", (D,))
    din("e_ln_g", (2, D)); din("e_ln_b", (2, D))
    din("eW1", (2, D, 4096)); din("eWg", (2, D, 4096)); din("eW2", (2, 4096, D))
    din("eb1", (2, 4096)); din("ebg", (2, 4096)); din("eb2", (16, D))
    din("l1_qkvT", (D, 3 * D)); din("l1_qkvmT", (D, 3 * D))
    din("l1_outT", (D, D)); din("l1_outmT", (D, D))
    din("l1_ob", (D,)); din("l1_obm", (D,))
    din("f_Win", (D, 64)); din("f_bin", (64,))
    din("f_W1", (64, 16)); din("f_b1", (64, 16)); din("f_W2", (64, 16)); din("f_b2", (64,))
    din("f_Wout", (64, D)); din("f_bout", (D,))
    c.dr = dr

    c.out = nc.dram_tensor("out", [T, D], F32, kind="ExternalOutput").ap()
    if DEBUG_DUMPS:
        c.dbg_o = nc.dram_tensor("dbg_o", [16, D], F32, kind="ExternalOutput").ap()
        c.dbg_of = nc.dram_tensor("dbg_of", [16, D], F32, kind="ExternalOutput").ap()
        c.dbg_h = nc.dram_tensor("dbg_h", [16, D], F32, kind="ExternalOutput").ap()
        c.dbg_a2 = nc.dram_tensor("dbg_a2", [128, 16384], BF16, kind="ExternalOutput").ap()
        c.dbg_ks = nc.dram_tensor("dbg_ks", [128, 256], BF16, kind="ExternalOutput").ap()

    c.ag1_in = nc.dram_tensor("ag1_in", [16, D], F32)
    c.ag1_out = nc.dram_tensor("ag1_out", [16, D], F32)
    c.agz_in = nc.dram_tensor("agz_in", [16, D], F32)
    c.agz_out = nc.dram_tensor("agz_out", [16, D], F32)

    A = nc.alloc_sbuf_tensor
    c.X = A("X", [128, NT * D], F32)
    c.XN = A("XN", [128, NKC * T], BF16)
    c.A2 = A("A2", [128, NKC * T], BF16)
    c.A3 = A("A3", [128, 12352], BF16)
    c.WS = A("WS", [128, 6144], BF16)
    c.WF = A("WF", [128, 2048], F32)
    c.PT = A("PT", [128, 2048], BF16)
    c.EPI = A("EPI", [128, 2048], F32)
    c.STATS = A("STATS", [128, 128], F32)
    c.RB = A("RB", [1, 1024], F32)
    c.RBb = A("RBb", [1, 2048], BF16)
    c.ID32 = A("ID32", [16, 16], F32)
    c.IDb = A("IDb", [128, 128], BF16)
    c.ONESb = A("ONESb", [1, 128], BF16)
    c.ONES32 = A("ONES32", [1, 64], F32)
    c.SL = A("SL", [16, 3072], F32)
    c.HT = A("HT", [128, 128], F32)
    c.BSM = A("BSM", [128, 32], F32)
    c.QSL = A("QSL", [128, 256], BF16)
    c.FSM = A("FSM", [64, 96], F32)
    c.EPSC = A("EPSC", [128, 1], F32)
    c.ZSB = A("ZSB", [1, 128], BF16)
    c.ZS32 = A("ZS32", [1, 16], F32)
    c.AMN = c.EPI[:].bitcast(BF16)

    c.PS = nc.alloc_psum_tensor("PS", [128, 4096], F32)
    c.PSb = c.PS[:].bitcast(BF16)

    c.s_ld = Sem(nc, "s_ld")
    c.s_wt = Sem(nc, "s_wt")
    c.s_wu = Sem(nc, "s_wu")
    c.s_te = Sem(nc, "s_te")
    c.s_ve = Sem(nc, "s_ve")
    c.s_se = Sem(nc, "s_se")
    c.s_ge = Sem(nc, "s_ge")
    c.s_st = Sem(nc, "s_st")
    c.s_cc = Sem(nc, "s_cc")
    c.s_vv = Sem(nc, "s_vv")
    c.s_ss = Sem(nc, "s_ss")
    c.s_gg = Sem(nc, "s_gg")
    c.wv = EngWrap(c.s_vv)
    c.ws_ = EngWrap(c.s_ss)
    c.wg = EngWrap(c.s_gg)
    return nc, c


_WRAP_OPS = {
    "tensor_copy", "tensor_tensor", "tensor_add", "tensor_sub", "tensor_mul",
    "tensor_max", "tensor_scalar", "tensor_single_scalar", "tensor_scalar_add",
    "tensor_scalar_sub", "tensor_scalar_mul", "tensor_scalar_max",
    "tensor_scalar_min", "tensor_relu", "tensor_reduce", "reduce_sum",
    "reduce_max", "scalar_tensor_tensor", "tensor_tensor_scan", "memset",
    "memzero", "reciprocal", "select", "bn_stats", "bn_aggr", "iota",
    "affine_select", "activation", "sign", "square", "sqrt", "mul", "add",
    "copy", "transpose",
}


class EngWrap:

    def __init__(self, sem):
        self._e = None
        self._s = sem
        self._pending = None
        self._last = None

    def _flush(self):
        p = self._pending
        if p is not None and id(p) in _PENDING:
            _PENDING.pop(id(p))
            p.then_inc(self._s.h, 1)
            self._s.n += 1
            self._last = (self._s, self._s.n)
        self._pending = None

    def __getattr__(self, name):
        fn = getattr(self._e, name)
        if name not in _WRAP_OPS:
            return fn

        def call(*a, **k):
            self._flush()
            if self._last is not None:
                self._e.wait_ge(self._last[0].h, self._last[1])
            ins = fn(*a, **k)
            self._pending = ins
            _PENDING[id(ins)] = self
            return ins
        return call


def wrap_block(block, c):
    class B:
        pass
    b = B()

    def mk(section, w):
        def deco(f):
            def g(eng):
                w._e = eng
                return f(w)
            return section(g)
        return deco
    b.vector = mk(block.vector, c.wv)
    b.scalar = mk(block.scalar, c.ws_)
    b.gpsimd = mk(block.gpsimd, c.wg)
    b.tensor = block.tensor
    b.sync = block.sync
    return b


def gate_wf(eng, c):
    if c.s_ve.n:
        eng.wait_ge(c.s_ve.h, c.s_ve.n)
    if c.s_te.n:
        eng.wait_ge(c.s_te.h, c.s_te.n)
    if c.s_ge.n:
        eng.wait_ge(c.s_ge.h, c.s_ge.n)


def dma_ser(eng, sem, out, in_, **kw):
    if sem.n:
        eng.wait_ge(sem.h, sem.n)
    return sem.inc(eng.dma_start(out=out, in_=in_, **kw), 16)


def bank(c, b, n=1):
    return c.PS[:, 512 * b: 512 * (b + n)]


def xtile(c, j):
    return c.X[:, 1024 * j: 1024 * (j + 1)]


def emit_init(c, block):
    nc = c.nc

    @block.gpsimd
    def _(g):
        for j in range(NT):
            dma_ser(g, c.s_ld, xtile(c, j), c.dr["y"][128 * j:128 * (j + 1), :])
        c.s_ge.inc(g.memset(c.ID32[:], 0.0))
        c.s_ge.inc(g.memset(c.IDb[:], 0.0))
        g.wait_ge(c.s_ge.h, c.s_ge.n)
        c.s_ge.inc(g.affine_select(out=c.ID32[:], in_=c.ID32[:], compare_op=ALU.not_equal,
                                   fill=1.0, base=0, pattern=[[-1, 16]], channel_multiplier=1))
        c.s_ge.inc(g.affine_select(out=c.IDb[:], in_=c.IDb[:], compare_op=ALU.not_equal,
                                   fill=1.0, base=0, pattern=[[-1, 128]], channel_multiplier=1))
        g.wait_ge(c.s_ge.h, c.s_ge.n)
        c.s_ge.inc(g.nop())

    @block.vector
    def _(v):
        v.memset(c.ONESb[:], 1.0)
        v.memset(c.ONES32[0:1, 0:64], 1.0)
        v.memset(c.EPSC[:], EPS)
        v.memset(c.ZSB[:], 0.0)
        v.memset(c.ZS32[:], 0.0)
        v.memset(c.RBb[0:1, :], 0.0)
        v.memset(c.RB[0:1, :], 0.0)
        c.s_ve.inc(v.nop())

    @block.scalar
    def _(s):
        s.add_instruction(mybir.InstLoadActFuncSet(
            name=nc.get_next_instruction_name(), act_func_set_id=6, ins=[], outs=[]))
        c.s_se.inc(s.nop())

    @block.tensor
    def _(t):
        t.wait_ge(c.s_ge.h, c.s_ge.n)


def emit_ln_stats(c, block, src_tiles, have_sums=None, wait_sems=()):
    nc = c.nc
    ssum = c.STATS[:, 32:48]
    ssq = c.STATS[:, 48:64]
    negm = c.STATS[:, 0:16]
    r = c.STATS[:, 16:32]
    t1 = c.STATS[:, 64:80]
    t2 = c.STATS[:, 80:96]
    junk = c.EPI[:, 0:1024]

    if have_sums is None:
        @block.vector
        def _(v):
            for s in wait_sems:
                s.wait(v)
            for j in range(NT):
                ins = v.reduce_sum(ssum[:, j:j + 1], src_tiles(j), axis=AX)
            c.s_ve.inc(ins)

    @block.scalar
    def _(s):
        for sm in wait_sems:
            sm.wait(s)
        for j in range(NT):
            if j:
                s.wait_ge(c.s_se.h, c.s_se.n)
            c.s_se.inc(s.activation(junk[:], src_tiles(j), AF.Square, accum_out=ssq[:, j:j + 1]))

    sums = have_sums if have_sums is not None else ssum

    @block.vector
    def _(v):
        v.wait_ge(c.s_ve.h, c.s_ve.n)
        c.s_se.wait(v)
        v.tensor_scalar(negm[:], sums, -1.0 / D, None, ALU.mult)
        v.tensor_mul(t2[:], negm[:], negm[:])
        v.tensor_scalar(t1[:], ssq[:], 1.0 / D, None, ALU.mult)
        c.s_ve.inc(v.tensor_sub(t1[:], t1[:], t2[:]))

    @block.scalar
    def _(s):
        c.s_ve.wait(s)
        s.activation(t1[:], t1[:], AF.Ln, bias=c.EPSC[:])
        c.s_se.inc(s.activation(r[:], t1[:], AF.Exp, scale=-0.5))
    return negm, r


def emit_ln_apply_transpose(c, block, src_tiles, negm, r, dst_fm, stage):
    nc = c.nc
    vb = c.s_ve.n
    tb = c.s_te.n

    @block.vector
    def _(v):
        c.s_se.wait(v)
        for j in range(NT):
            ins = v.tensor_scalar(stage[:, 1024 * j:1024 * (j + 1)], src_tiles(j),
                                  negm[:, j:j + 1], r[:, j:j + 1], ALU.add, ALU.mult)
            c.s_ve.inc(ins)

    @block.tensor
    def _(t):
        gi = 0
        for j in range(NT):
            t.wait_ge(c.s_ve.h, vb + j + 1)
            for g in range(2):
                b = 6 + gi % 2
                if gi >= 2:
                    t.wait_ge(c.s_ve.h, vb + 16 + (gi - 1))
                for kk in range(4):
                    k = 4 * g + kk
                    ins = t.transpose(c.PSb[:, 1024 * b + 128 * kk:1024 * b + 128 * (kk + 1)],
                                      stage[:, 1024 * j + 128 * k: 1024 * j + 128 * (k + 1)],
                                      c.IDb[:])
                c.s_te.inc(ins)
                gi += 1

    @block.vector
    def _(v):
        gi = 0
        vw = dst_fm.rearrange("p (k t) -> p k t", t=T)
        for j in range(NT):
            for g in range(2):
                b = 6 + gi % 2
                v.wait_ge(c.s_te.h, tb + gi + 1)
                dst = vw[:, 4 * g:4 * g + 4, 128 * j:128 * (j + 1)]
                c.s_ve.inc(v.tensor_copy(dst, c.PSb[:, 1024 * b:1024 * b + 512]
                                         .rearrange("p (a t) -> p a t", a=4)))
                gi += 1


def stream_w_bf16(c, block, dram_ap, n_chunks, ws_off, wc=1024, gate0=None, dst=None):
    nc = c.nc
    vb = c.s_ve.n
    wsem = [c.s_wt, c.s_wu]
    dvals = []

    @block.sync
    def _(sy):
        for k in range(n_chunks):
            if k >= 2:
                sy.wait_ge(c.s_ve.h, vb + (k - 1))
            else:
                gate_wf(sy, c)
            dma_ser(sy, wsem[k % 2], c.WF[:, 1024 * (k % 2):1024 * (k % 2) + wc],
                    dram_ap[128 * k:128 * (k + 1), :])
            dvals.append((wsem[k % 2], wsem[k % 2].n))

    ready = []

    @block.vector
    def _(v):
        for k in range(n_chunks):
            v.wait_ge(dvals[k][0].h, dvals[k][1])
            if k == 0 and gate0 is not None:
                v.wait_ge(gate0[0].h, gate0[1])
            dd = dst if dst is not None else c.WS
            ins = v.tensor_copy(dd[:, ws_off + wc * k: ws_off + wc * (k + 1)],
                                c.WF[:, 1024 * (k % 2):1024 * (k % 2) + wc])
            c.s_ve.inc(ins)
            ready.append((c.s_ve, c.s_ve.n))
    return ready


def load_bias_fm(c, block, dram_ap, col):
    @block.gpsimd
    def _(g):
        vw = dram_ap.rearrange("(k p) -> p k", p=128)
        dma_ser(g, c.s_ld, c.BSM[:, 8 * col:8 * col + 8], vw,
                allow_slow_non_contiguous=True)


def load_bias_row(c, block, dram_ap, slot=0, length=D, make_bf=True, gates=()):
    vb = c.s_ve.n

    tb = c.s_te.n

    @block.gpsimd
    def _(g):
        if vb:
            g.wait_ge(c.s_ve.h, vb)
        if tb:
            g.wait_ge(c.s_te.h, tb)
        for sm, val in gates:
            if val:
                g.wait_ge(sm.h, val)
        src = dram_ap.rearrange("(a d) -> a d", a=1) if len(dram_ap.shape) == 1 else dram_ap
        dma_ser(g, c.s_ld, c.RB[0:1, 0:length], src)

    if make_bf:
        @block.vector
        def _(v):
            c.s_ld.wait(v)
            for sm, val in gates:
                if val:
                    v.wait_ge(sm.h, val)
            c.s_ve.inc(v.tensor_copy(c.RBb[0:1, 0:length], c.RB[0:1, 0:length]))



def emit_proj_fm(c, block, dst_fm, dst_chunk0, ws_off, wt_ready, src_chunk, e_chunks,
                 bias_col, wslice=128):
    nc = c.nc
    tb = c.s_te.n
    vb = c.s_ve.n
    wtot = 128 * len(e_chunks)

    @block.tensor
    def _(t):
        if vb:
            t.wait_ge(c.s_ve.h, vb)
        for jj, j in enumerate(e_chunks):
            b0 = 0 if jj % 2 == 0 else 4
            if jj >= 2:
                t.wait_ge(c.s_ve.h, vb + jj - 1)
            for k in range(NKC):
                if wt_ready is not None and jj == 0:
                    t.wait_ge(wt_ready[k][0].h, wt_ready[k][1])
                lhsT = c.WS[:, ws_off + wtot * k + 128 * jj: ws_off + wtot * k + 128 * (jj + 1)]
                for w in range(4):
                    t.matmul(bank(c, b0 + w)[:, 0:512], lhsT,
                             src_chunk(k, 512 * w, 512 * (w + 1)),
                             start=(k == 0), stop=False)
            for w in range(4):
                ins = t.matmul(bank(c, b0 + w)[:, 0:512], c.ZSB[0:1, 0:128],
                               c.RBb[0:1, 0:512], start=False, stop=True)
            c.s_te.inc(ins)

    ldn = c.s_ld.n

    @block.vector
    def _(v):
        if bias_col is not None and ldn:
            v.wait_ge(c.s_ld.h, ldn)
        vw = dst_fm.rearrange("p (k t) -> p k t", t=T)
        for jj, j in enumerate(e_chunks):
            b0 = 0 if jj % 2 == 0 else 4
            v.wait_ge(c.s_te.h, tb + jj + 1)
            pv = c.PS[:, 512 * b0:512 * b0 + 2048]
            dst = vw[:, dst_chunk0 + jj, :]
            if bias_col is not None:
                ins = v.tensor_scalar(dst, pv, c.BSM[:, 8 * bias_col + j:8 * bias_col + j + 1],
                                      None, ALU.add)
            else:
                ins = v.tensor_copy(dst, pv)
            c.s_ve.inc(ins)


def emit_vprime_proj(c, block, dst, nheads, ws_off, wt_ready, src_chunk, wc,
                     bias_slot, bias_off, zb=0):
    nc = c.nc
    tb = c.s_te.n
    vb = c.s_ve.n
    CW = 65 * nheads

    @block.tensor
    def _(t):
        if vb:
            t.wait_ge(c.s_ve.h, vb)
        for j in range(NT):
            b = zb + (j % 2)
            if j >= 2:
                t.wait_ge(c.s_ve.h, vb + j - 1)
            for k in range(NKC):
                if wt_ready is not None and j == 0:
                    t.wait_ge(wt_ready[k][0].h, wt_ready[k][1])
                t.matmul(bank(c, b)[:, 0:wc], src_chunk(k, 128 * j, 128 * (j + 1)),
                         c.WS[:, ws_off + wc * k: ws_off + wc * (k + 1)],
                         start=(k == 0), stop=False)
            ins = t.matmul(bank(c, b)[:, 0:wc], c.ONESb[:, 0:128],
                           c.RBb[0:1, bias_off:bias_off + wc], start=False, stop=True)
            c.s_te.inc(ins)

    @block.vector
    def _(v):
        dv = dst.rearrange("p (j h e) -> p j h e", h=nheads, e=65)
        for j in range(NT):
            b = zb + (j % 2)
            v.wait_ge(c.s_te.h, tb + j + 1)
            v.tensor_copy(dv[:, j, :, 0:64],
                          bank(c, b)[:, 0:wc].rearrange("p (h e) -> p h e", e=64))
            c.s_ve.inc(v.memset(dv[:, j, :, 64:65], 1.0))



def emit_attention_quarter(c, block, q, scale, attn_dst):
    nc = c.nc
    NKT = 16

    for jj in range(2):
        cglob = 2 * q + jj
        tb = c.s_te.n
        vb = c.s_ve.n
        sb = c.s_se.n

        @block.tensor
        def _(t):
            if vb:
                t.wait_ge(c.s_ve.h, vb)
            for w in range(4):
                for kt in range(NKT):
                    sc = 2 * (kt % 2)
                    if not (w == 0 and kt < 2):
                        t.wait_ge(c.s_se.h, sb + 16 * w + kt - 1)
                    if w > 0 and kt == 0:
                        t.wait_ge(c.s_ve.h, vb + 3 * w)
                    lA = c.A3[0 + 64 * 0:64, 4096 + 2048 * jj + 128 * kt: 4096 + 2048 * jj + 128 * (kt + 1)]
                    lB = c.A3[64:128, 4096 + 2048 * jj + 128 * kt: 4096 + 2048 * jj + 128 * (kt + 1)]
                    rA = c.A3[0:64, 2048 * jj + 512 * w: 2048 * jj + 512 * (w + 1)]
                    rB = c.A3[64:128, 2048 * jj + 512 * w: 2048 * jj + 512 * (w + 1)]
                    t.matmul(bank(c, sc)[:, 0:512], lA, rA, start=True, stop=True)
                    ins = t.matmul(bank(c, sc + 1)[:, 0:512], lB, rB, start=True, stop=True)
                    c.s_te.inc(ins)
                    t.wait_ge(c.s_se.h, sb + 16 * w + kt + 1)
                    vA = c.A3[:, 8192 + 260 * kt + 65 * (2 * jj): 8192 + 260 * kt + 65 * (2 * jj) + 65]
                    vB = c.A3[:, 8192 + 260 * kt + 65 * (2 * jj + 1): 8192 + 260 * kt + 65 * (2 * jj + 1) + 65]
                    pA = c.PT[:, 1024 * (kt % 2): 1024 * (kt % 2) + 512]
                    pB = c.PT[:, 1024 * (kt % 2) + 512: 1024 * (kt % 2) + 1024]
                    t.matmul(bank(c, 4)[0:65, 0:512], vA, pA, start=(kt == 0), stop=False)
                    ins = t.matmul(bank(c, 5)[0:65, 0:512], vB, pB, start=(kt == 0), stop=False)
                    if kt < NKT - 1:
                        c.s_te.inc(ins)
                t.matmul(bank(c, 4)[0:65, 0:512], c.ZSB[0:1, 0:65],
                         c.RBb[0:1, 0:512], start=False, stop=True)
                ins = t.matmul(bank(c, 5)[0:65, 0:512], c.ZSB[0:1, 0:65],
                               c.RBb[0:1, 0:512], start=False, stop=True)
                c.s_te.inc(ins)
                t.wait_ge(c.s_ve.h, vb + 3 * w + 1)
                rr = c.EPI[0:1, 1024:2048]
                t.matmul(bank(c, 6)[0:64, 0:512], c.ONES32[:, 0:64], rr[:, 0:512],
                         start=True, stop=True)
                ins = t.matmul(bank(c, 7)[0:64, 0:512], c.ONES32[:, 0:64], rr[:, 512:1024],
                               start=True, stop=True)
                c.s_te.inc(ins)

        @block.scalar
        def _(s):
            for w in range(4):
                for kt in range(NKT):
                    s.wait_ge(c.s_te.h, tb + 33 * w + 2 * kt + 1)
                    ins = s.activation(c.PT[:, 1024 * (kt % 2):1024 * (kt % 2 + 1)],
                                       c.PS[:, 512 * (2 * (kt % 2)): 512 * (2 * (kt % 2)) + 1024],
                                       AF.Exp, scale=scale)
                    c.s_se.inc(ins)

        gb = c.s_ge.n

        @block.vector
        def _(v):
            for w in range(4):
                v.wait_ge(c.s_te.h, tb + 33 * w + 32)
                rr = c.EPI[0:1, 1024:2048]
                c.s_ve.inc(v.reciprocal(rr, c.PS[64:65, 512 * 4:512 * 4 + 1024]))
                v.wait_ge(c.s_te.h, tb + 33 * w + 33)
                bc = c.EPI[0:64, 1024:2048]
                c.s_ve.inc(v.tensor_copy(bc, c.PS[0:64, 512 * 6:512 * 6 + 1024]))
                if gb + w > 0:
                    v.wait_ge(c.s_ge.h, gb + w)
                c.s_ve.inc(v.tensor_mul(c.AMN[0:64, 0:1024], c.PS[0:64, 512 * 4:512 * 4 + 1024], bc))

        @block.gpsimd
        def _(g):
            for w in range(4):
                g.wait_ge(c.s_ve.h, vb + 3 * w + 3)
                dstA = attn_dst[0:64, 2048 * cglob + 512 * w: 2048 * cglob + 512 * (w + 1)]
                dstB = attn_dst[64:128, 2048 * cglob + 512 * w: 2048 * cglob + 512 * (w + 1)]
                dma_ser(g, c.s_st, dstA, c.AMN[0:64, 0:512])
                dma_ser(g, c.s_st, dstB, c.AMN[0:64, 512:1024])
                g.wait_ge(c.s_st.h, c.s_st.n)
                c.s_ge.inc(g.nop())


def emit_oproj_sandwich(c, block, attn_src, wt_ready_o, bias_row_off, sums_dst, gate=None):
    nc = c.nc
    tb = c.s_te.n
    vb = c.s_ve.n
    sb = c.s_se.n
    asrc = attn_src.rearrange("p (k t) -> p k t", t=T)
    negm2 = c.STATS[:, 96:97]
    r2 = c.STATS[:, 97:98]
    v1t = c.STATS[:, 98:99]
    v2t = c.STATS[:, 99:100]
    ssq = c.STATS[:, 100:101]
    ssm = c.STATS[:, 101:102]
    junk = c.EPI[:, 0:1024]
    zn = c.EPI[:, 1024:2048]

    @block.tensor
    def _(t):
        if gate is not None:
            t.wait_ge(gate[0].h, gate[1])
        if vb:
            t.wait_ge(c.s_ve.h, vb)
        for j in range(NT):
            b0 = 0 if j % 2 == 0 else 2
            if j >= 2:
                t.wait_ge(c.s_ve.h, vb + 4 * (j - 1))
            for k in range(NKC):
                if wt_ready_o is not None and j == 0:
                    t.wait_ge(wt_ready_o[k][0].h, wt_ready_o[k][1])
                lhsT = asrc[:, k, 128 * j:128 * (j + 1)]
                t.matmul(bank(c, b0)[:, 0:512], lhsT, c.A3[:, 1024 * k:1024 * k + 512],
                         start=(k == 0), stop=False)
                t.matmul(bank(c, b0 + 1)[:, 0:512], lhsT, c.A3[:, 1024 * k + 512:1024 * (k + 1)],
                         start=(k == 0), stop=False)
            t.matmul(bank(c, b0)[:, 0:512], c.ONESb[:, 0:128],
                     c.RBb[0:1, bias_row_off:bias_row_off + 512], start=False, stop=True)
            ins = t.matmul(bank(c, b0 + 1)[:, 0:512], c.ONESb[:, 0:128],
                           c.RBb[0:1, bias_row_off + 512:bias_row_off + 1024],
                           start=False, stop=True)
            c.s_te.inc(ins)

    @block.vector
    def _(v):
        for j in range(NT):
            b0 = 0 if j % 2 == 0 else 2
            pz = c.PS[:, 512 * b0:512 * b0 + 1024]
            v.wait_ge(c.s_te.h, tb + j + 1)
            c.s_ve.inc(v.reduce_sum(ssm, pz, axis=AX))
            v.wait_ge(c.s_se.h, sb + 3 * j + 1)
            v.tensor_scalar(negm2, ssm, -1.0 / D, None, ALU.mult)
            v.tensor_mul(v2t, negm2, negm2)
            v.tensor_scalar(v1t, ssq, 1.0 / D, None, ALU.mult)
            c.s_ve.inc(v.tensor_sub(v1t, v1t, v2t))
            v.wait_ge(c.s_se.h, sb + 3 * j + 3)
            v.tensor_scalar(zn, pz, negm2, r2, ALU.add, ALU.mult)
            ins = v.scalar_tensor_tensor(
                xtile(c, j), zn, 1.0, xtile(c, j), ALU.mult, ALU.add,
                accum_out=(sums_dst[:, j:j + 1] if sums_dst is not None else None))
            c.s_ve.inc(ins)
            v.wait_ge(c.s_ve.h, c.s_ve.n)
            c.s_ve.inc(v.nop())

    @block.scalar
    def _(s):
        for j in range(NT):
            b0 = 0 if j % 2 == 0 else 2
            pz = c.PS[:, 512 * b0:512 * b0 + 1024]
            s.wait_ge(c.s_te.h, tb + j + 1)
            s.wait_ge(c.s_ve.h, vb + 4 * j + 1)
            c.s_se.inc(s.activation(junk, pz, AF.Square, accum_out=ssq))
            s.wait_ge(c.s_ve.h, vb + 4 * j + 2)
            c.s_se.inc(s.activation(v1t, v1t, AF.Ln, bias=c.EPSC[:]))
            c.s_se.inc(s.activation(r2, v1t, AF.Exp, scale=-0.5))



def stream_w_rot(c, block, dram_ap, n_chunks, ws_off, wc, gate_sem, gate_counts):
    nc = c.nc
    wsem = [c.s_wt, c.s_wu]
    dvals = []

    @block.sync
    def _(sy):
        for k in range(n_chunks):
            if k >= 2:
                sy.wait_ge(gate_sem.h, gate_counts(k - 2))
            else:
                gate_wf(sy, c)
            dma_ser(sy, wsem[k % 2], c.WF[:, 1024 * (k % 2):1024 * (k % 2) + wc],
                    dram_ap[128 * k:128 * (k + 1), :])
            dvals.append((wsem[k % 2], wsem[k % 2].n))

    ready = []

    @block.vector
    def _(v):
        for k in range(n_chunks):
            v.wait_ge(dvals[k][0].h, dvals[k][1])
            if k >= 2:
                v.wait_ge(gate_sem.h, gate_counts(k - 2))
            ins = v.tensor_copy(c.WS[:, ws_off + wc * (k % 2): ws_off + wc * (k % 2) + wc],
                                c.WF[:, 1024 * (k % 2):1024 * (k % 2) + wc])
            c.s_ve.inc(ins)
            ready.append((c.s_ve, c.s_ve.n))
    return ready


def stream_masked_w(c, block, dram_w, dram_m, n_chunks, ws_off, wc, gate0=None, dst=None):
    nc = c.nc
    gb = c.s_ge.n
    dvals = []
    assert wc <= 1024

    @block.sync
    def _(sy):
        for k in range(n_chunks):
            if k >= 1:
                sy.wait_ge(c.s_ge.h, gb + k)
            else:
                gate_wf(sy, c)
            dma_ser(sy, c.s_wt, c.WF[:, 0:wc], dram_w[128 * k:128 * (k + 1), :])
            dma_ser(sy, c.s_wt, c.WF[:, 1024:1024 + wc], dram_m[128 * k:128 * (k + 1), :])
            dvals.append((c.s_wt, c.s_wt.n))

    ready = []

    @block.gpsimd
    def _(g):
        for k in range(n_chunks):
            g.wait_ge(dvals[k][0].h, dvals[k][1])
            if k == 0 and gate0 is not None:
                g.wait_ge(gate0[0].h, gate0[1])
            dd = dst if dst is not None else c.WS
            g.tensor_scalar(c.WF[:, 1024:1024 + wc], c.WF[:, 1024:1024 + wc],
                            0.0, None, ALU.is_gt)
            ins = g.tensor_tensor(dd[:, ws_off + wc * k: ws_off + wc * (k + 1)],
                                  c.WF[:, 1024:1024 + wc], c.WF[:, 0:wc], ALU.mult)
            c.s_ge.inc(ins)
            ready.append((c.s_ge, c.s_ge.n))
    return ready



def emit_l0ffn_slots(c, block):
    nc = c.nc
    SLo = c.SL[0:16, 0:1024]
    S1 = c.SL[0:16, 1024:2048]
    S2 = c.SL[0:16, 2048:3072]

    vbq = c.s_ve.n
    teq = c.s_te.n

    @block.gpsimd
    def _(g):
        dma_ser(g, c.s_ld, S2, c.dr["sq"])
        dma_ser(g, c.s_ld, S1, c.dr["d_bo"].rearrange("(a d) -> a d", a=1).broadcast_to([16, D]))
        if vbq:
            g.wait_ge(c.s_ve.h, vbq)
        if teq:
            g.wait_ge(c.s_te.h, teq)
        dma_ser(g, c.s_ld, c.RB[0:1, 0:1024], c.dr["d_bq"].rearrange("(a d) -> a d", a=1))

    ldb = c.s_ld.n
    vb0 = c.s_ve.n

    @block.vector
    def _(v):
        v.wait_ge(c.s_ld.h, ldb)
        c.s_ve.inc(v.tensor_add(S1, S1, S2))

    tb0 = c.s_te.n

    @block.tensor
    def _(t):
        t.wait_ge(c.s_ld.h, ldb)
        for k in range(NKC):
            ins = t.transpose(bank(c, 6)[:, 16 * k:16 * (k + 1)],
                              S2[:, 128 * k:128 * (k + 1)], c.ID32[0:16, 0:16])
        c.s_te.inc(ins)

    @block.vector
    def _(v):
        v.wait_ge(c.s_te.h, tb0 + 1)
        c.s_ve.inc(v.tensor_copy(c.HT[:, 0:128], bank(c, 6)[:, 0:128]))

    vb1 = c.s_ve.n
    tb1 = c.s_te.n
    wsem = [c.s_wt, c.s_wu]
    dv1 = []

    @block.sync
    def _(sy):
        for k in range(NKC):
            if k >= 2:
                sy.wait_ge(c.s_te.h, tb1 + (k - 1))
            else:
                gate_wf(sy, c)
            dma_ser(sy, wsem[k % 2], c.WF[:, 1024 * (k % 2):1024 * (k % 2 + 1)],
                    c.dr["d_Wq"][128 * k:128 * (k + 1), :])
            dv1.append((wsem[k % 2], wsem[k % 2].n))

    @block.tensor
    def _(t):
        t.wait_ge(c.s_ve.h, vb1)
        for k in range(NKC):
            t.wait_ge(dv1[k][0].h, dv1[k][1])
            lhsT = c.HT[:, 16 * k:16 * (k + 1)]
            t.matmul(bank(c, 4)[0:16, 0:512], lhsT, c.WF[:, 1024 * (k % 2):1024 * (k % 2) + 512],
                     start=(k == 0), stop=False)
            ins = t.matmul(bank(c, 5)[0:16, 0:512], lhsT,
                           c.WF[:, 1024 * (k % 2) + 512:1024 * (k % 2 + 1)],
                           start=(k == 0), stop=False)
            c.s_te.inc(ins)
        t.wait_ge(c.s_ld.h, ldb)
        t.matmul(bank(c, 4)[0:16, 0:512], c.ONES32[:, 0:16], c.RB[0:1, 0:512],
                 start=False, stop=True)
        ins = t.matmul(bank(c, 5)[0:16, 0:512], c.ONES32[:, 0:16], c.RB[0:1, 512:1024],
                       start=False, stop=True)
        c.s_te.inc(ins)

    tb2 = c.s_te.n

    @block.vector
    def _(v):
        v.wait_ge(c.s_te.h, tb2)
        c.s_ve.inc(v.tensor_copy(S2, c.PS[0:16, 512 * 4:512 * 4 + 1024]))

    vb2 = c.s_ve.n
    tb3 = c.s_te.n

    @block.tensor
    def _(t):
        t.wait_ge(c.s_ve.h, vb2)
        for k in range(NKC):
            ins = t.transpose(bank(c, 6)[:, 16 * k:16 * (k + 1)],
                              S2[:, 128 * k:128 * (k + 1)], c.ID32[0:16, 0:16])
        c.s_te.inc(ins)

    @block.vector
    def _(v):
        v.wait_ge(c.s_te.h, tb3 + 1)
        c.s_ve.inc(v.tensor_copy(c.QSL[:, 0:128], bank(c, 6)[:, 0:128]))

    load_bias_row(c, block, c.dr["d_bv"], gates=((c.s_te, c.s_te.n),))
    load_bias_fm(c, block, c.dr["d_bk"], 2)

    for q in range(4):
        wr_k = stream_w_bf16(c, block, c.dr["d_Wk"][:, 256 * q:256 * (q + 1)], NKC, 0, wc=256)
        wr_v = stream_w_bf16(c, block, c.dr["d_Wv"][:, 256 * q:256 * (q + 1)], NKC, 2048, wc=256)
        emit_proj_fm(c, block, c.A3[:, 4096:8192], 0, 0, wr_k,
                     lambda k, lo, hi: c.XN[:, 2048 * k + lo:2048 * k + hi],
                     [2 * q, 2 * q + 1], bias_col=2)
        emit_vprime_proj(c, block, c.A3[:, 8192:12352], 4, 2048, wr_v,
                         lambda k, lo, hi: c.XN[:, 2048 * k + lo:2048 * k + hi],
                         256, None, 256 * q, zb=0)
        for jj in range(2):
            tb = c.s_te.n
            sb = c.s_se.n
            vb = c.s_ve.n
            ch = 2 * q + jj

            @block.tensor
            def _(t, jj=jj, ch=ch, tb=tb, sb=sb, vb=vb):
                t.wait_ge(c.s_ve.h, vb)
                for kt in range(NT):
                    lA = c.A3[0:64, 4096 + 2048 * jj + 128 * kt: 4096 + 2048 * jj + 128 * (kt + 1)]
                    lB = c.A3[64:128, 4096 + 2048 * jj + 128 * kt: 4096 + 2048 * jj + 128 * (kt + 1)]
                    rA = c.QSL[0:64, 16 * ch:16 * (ch + 1)]
                    rB = c.QSL[64:128, 16 * ch:16 * (ch + 1)]
                    t.matmul(bank(c, 0)[:, 16 * kt:16 * (kt + 1)], lA, rA, start=True, stop=True)
                    ins = t.matmul(bank(c, 1)[:, 16 * kt:16 * (kt + 1)], lB, rB, start=True, stop=True)
                c.s_te.inc(ins)
                t.wait_ge(c.s_se.h, sb + 1)
                for kt in range(NT):
                    for hh in range(2):
                        vsl = c.A3[:, 8192 + 260 * kt + 65 * (2 * jj + hh):
                                   8192 + 260 * kt + 65 * (2 * jj + hh) + 65]
                        psl = c.PT[:, 256 * hh + 16 * kt: 256 * hh + 16 * (kt + 1)]
                        t.matmul(bank(c, 2 + hh)[0:16, 0:65], psl, vsl,
                                 start=(kt == 0), stop=False)
                for hh in range(2):
                    ins = t.matmul(bank(c, 2 + hh)[0:16, 0:65],
                                   c.ZSB[0:1, 0:16], c.RBb[0:1, 0:65],
                                   start=False, stop=True)
                c.s_te.inc(ins)

            @block.scalar
            def _(s, tb=tb):
                s.wait_ge(c.s_te.h, tb + 1)
                s.activation(c.PT[:, 0:256], bank(c, 0)[:, 0:256], AF.Exp, scale=0.125)
                c.s_se.inc(s.activation(c.PT[:, 256:512], bank(c, 1)[:, 0:256], AF.Exp, scale=0.125))

            @block.vector
            def _(v, jj=jj, ch=ch, tb=tb):
                v.wait_ge(c.s_te.h, tb + 2)
                rc = c.FSM[0:16, 86:88]
                v.reciprocal(rc, c.PS[0:16, 1088:1601:512])
                for hh in range(2):
                    h = 2 * ch + hh
                    ins = v.tensor_scalar(
                        c.SL[0:16, 2048 + 64 * h:2048 + 64 * (h + 1)],
                        bank(c, 2 + hh)[0:16, 0:64],
                        rc[:, hh:hh + 1], None, ALU.mult)
                c.s_ve.inc(ins)

    vb3 = c.s_ve.n
    tb4 = c.s_te.n

    @block.tensor
    def _(t):
        t.wait_ge(c.s_ve.h, vb3)
        for k in range(NKC):
            ins = t.transpose(bank(c, 6)[:, 16 * k:16 * (k + 1)],
                              c.SL[0:16, 2048 + 128 * k:2048 + 128 * (k + 1)], c.ID32[0:16, 0:16])
        c.s_te.inc(ins)

    @block.vector
    def _(v):
        v.wait_ge(c.s_te.h, tb4 + 1)
        c.s_ve.inc(v.tensor_copy(c.HT[:, 0:128], bank(c, 6)[:, 0:128]))

    vb4 = c.s_ve.n
    tb5 = c.s_te.n
    dv2 = []

    @block.sync
    def _(sy):
        for k in range(NKC):
            if k >= 2:
                sy.wait_ge(c.s_te.h, tb5 + (k - 1))
            else:
                gate_wf(sy, c)
            dma_ser(sy, wsem[k % 2], c.WF[:, 1024 * (k % 2):1024 * (k % 2 + 1)],
                    c.dr["d_Wo"][128 * k:128 * (k + 1), :])
            dv2.append((wsem[k % 2], wsem[k % 2].n))

    @block.tensor
    def _(t):
        t.wait_ge(c.s_ve.h, vb4)
        for k in range(NKC):
            t.wait_ge(dv2[k][0].h, dv2[k][1])
            lhsT = c.HT[:, 16 * k:16 * (k + 1)]
            t.matmul(bank(c, 4)[0:16, 0:512], lhsT, c.WF[:, 1024 * (k % 2):1024 * (k % 2) + 512],
                     start=(k == 0), stop=False)
            ins = t.matmul(bank(c, 5)[0:16, 0:512], lhsT,
                           c.WF[:, 1024 * (k % 2) + 512:1024 * (k % 2 + 1)],
                           start=(k == 0), stop=False)
            if k < NKC - 1:
                c.s_te.inc(ins)
        t.matmul(bank(c, 4)[0:16, 0:512], c.ZS32[0:1, 0:16], c.RB[0:1, 0:512],
                 start=False, stop=True)
        ins = t.matmul(bank(c, 5)[0:16, 0:512], c.ZS32[0:1, 0:16], c.RB[0:1, 0:512],
                       start=False, stop=True)
        c.s_te.inc(ins)

    tb6 = c.s_te.n

    @block.vector
    def _(v):
        v.wait_ge(c.s_te.h, tb6)
        c.s_ve.inc(v.tensor_add(c.SL[0:16, 0:1024], c.SL[0:16, 1024:2048],
                                c.PS[0:16, 512 * 4:512 * 4 + 1024]))


def emit_l0ffn_experts(c, block):
    nc = c.nc
    UG = c.A3[:].bitcast(F32)[:, 0:1536]
    SLo = c.SL[0:16, 0:1024]
    S1 = c.SL[0:16, 1024:2048]
    S2 = c.SL[0:16, 2048:3072]
    S3 = c.EPI[0:16, 1024:2048]
    vb0 = c.s_ve.n
    gb0 = c.s_ge.n

    @block.gpsimd
    def _(g):
        g.wait_ge(c.s_ve.h, vb0)
        dma_ser(g, c.s_ld, c.ag1_in.ap(), SLo)
        g.wait_ge(c.s_ld.h, c.s_ld.n)
        from concourse.aot_env import allow_ici_all_to_all
        with allow_ici_all_to_all():
            c.s_cc.inc(g.collective_compute(
                "AllToAll", ALU.bypass, replica_groups=[list(range(NCORES))],
                ins=[c.ag1_in.ap().opt()], outs=[c.ag1_out.ap().opt()]))
        g.wait_ge(c.s_cc.h, c.s_cc.n)
        dma_ser(g, c.s_ld, S2, c.ag1_out.ap()[0:16, :])
        dma_ser(g, c.s_ld, S1, c.dr["e_ln_g"].rearrange("(a e) d -> a e d", a=1).broadcast_to([8, 2, D]))
        dma_ser(g, c.s_ld, S3, c.dr["e_ln_b"].rearrange("(a e) d -> a e d", a=1).broadcast_to([8, 2, D]))
        g.wait_ge(c.s_ld.h, c.s_ld.n)
        c.s_ge.inc(g.nop())

    ge1 = c.s_ge.n
    sb0 = c.s_se.n
    essq = c.FSM[0:16, 80:81]
    esum = c.FSM[0:16, 81:82]
    enegm = c.FSM[0:16, 82:83]
    evar = c.FSM[0:16, 83:84]
    er = c.FSM[0:16, 84:85]
    et = c.FSM[0:16, 85:86]

    @block.scalar
    def _(s):
        s.wait_ge(c.s_ge.h, ge1)
        c.s_se.inc(s.activation(c.EPI[0:16, 0:1024], S2, AF.Square, accum_out=essq))

    @block.vector
    def _(v):
        v.wait_ge(c.s_ge.h, ge1)
        v.reduce_sum(esum, S2, axis=AX)
        v.wait_ge(c.s_se.h, sb0 + 1)
        v.tensor_scalar(enegm, esum, -1.0 / D, None, ALU.mult)
        v.tensor_mul(et, enegm, enegm)
        v.tensor_scalar(evar, essq, 1.0 / D, None, ALU.mult)
        c.s_ve.inc(v.tensor_sub(evar, evar, et))

    sb1 = c.s_se.n
    vb1 = c.s_ve.n

    @block.scalar
    def _(s):
        s.wait_ge(c.s_ve.h, vb1)
        s.activation(evar, evar, AF.Ln, bias=c.EPSC[0:16, :])
        c.s_se.inc(s.activation(er, evar, AF.Exp, scale=-0.5))

    @block.vector
    def _(v):
        v.wait_ge(c.s_se.h, sb1 + 1)
        v.tensor_scalar(S2, S2, enegm, er, ALU.add, ALU.mult)
        v.tensor_mul(S2, S2, S1)
        c.s_ve.inc(v.tensor_add(S2, S2, S3))

    vb2 = c.s_ve.n
    tb0 = c.s_te.n

    @block.tensor
    def _(t):
        t.wait_ge(c.s_ve.h, vb2)
        for k in range(NKC):
            ins = t.transpose(bank(c, 6)[:, 16 * k:16 * (k + 1)],
                              S2[:, 128 * k:128 * (k + 1)], c.ID32[0:16, 0:16])
        c.s_te.inc(ins)

    @block.vector
    def _(v):
        v.wait_ge(c.s_te.h, tb0 + 1)
        c.s_ve.inc(v.tensor_copy(c.HT[:, 0:128], bank(c, 6)[:, 0:128]))

    vbH = c.s_ve.n

    mats = [("eW1", "eb1", 0, 0), ("eWg", "ebg", 0, 512), ("eW1", "eb1", 1, 1024), ("eWg", "ebg", 1, 1536)]
    tbU = c.s_te.n
    wsem = [c.s_wt, c.s_wu]
    dvU = []
    ndma = [0]

    @block.sync
    def _(sy):
        for (wname, bname, e, off) in mats:
            for k in range(NKC):
                for g4 in range(4):
                    if ndma[0] >= 2:
                        sy.wait_ge(c.s_te.h, tbU + (ndma[0] - 1))
                    else:
                        gate_wf(sy, c)
                    dma_ser(sy, wsem[ndma[0] % 2],
                            c.WF[:, 1024 * (ndma[0] % 2):1024 * (ndma[0] % 2 + 1)],
                            c.dr[wname][e, 128 * k:128 * (k + 1), 1024 * g4:1024 * (g4 + 1)])
                    dvU.append((wsem[ndma[0] % 2], wsem[ndma[0] % 2].n))
                    ndma[0] += 1

    nmm = [0]

    @block.tensor
    def _(t):
        t.wait_ge(c.s_ve.h, vbH)
        pall = c.PS[:, 0:2048]
        for (wname, bname, e, off) in mats:
            for k in range(NKC):
                for g4 in range(4):
                    t.wait_ge(dvU[nmm[0]][0].h, dvU[nmm[0]][1])
                    rhs = c.HT[:, 16 * k + e:16 * k + 16:2]
                    for ft in range(8):
                        fcol = off + 8 * (8 * g4 + ft)
                        ins = t.matmul(pall[:, fcol:fcol + 8],
                                       c.WF[:, 1024 * (nmm[0] % 2) + 128 * ft:
                                            1024 * (nmm[0] % 2) + 128 * (ft + 1)],
                                       rhs, start=(k == 0 and g4 == 0 and ft == 0),
                                       stop=False)
                    nmm[0] += 1
                    c.s_te.inc(ins)

    tbB = c.s_te.n
    ldB = c.s_ld.n

    @block.gpsimd
    def _(g):
        nq = 0
        for i, (wname, bname, e, off) in enumerate(mats):
            for qtr in range(4):
                if nq:
                    g.wait_ge(c.s_te.h, tbB + nq)
                dma_ser(g, c.s_ld, c.RB[0:1, 0:1024],
                        c.dr[bname][e, 1024 * qtr:1024 * (qtr + 1)].rearrange("(a d) -> a d", a=1))
                nq += 1

    @block.tensor
    def _(t):
        nb = 0
        pall = c.PS[:, 0:2048]
        for i, (wname, bname, e, off) in enumerate(mats):
            for qtr in range(4):
                t.wait_ge(c.s_ld.h, ldB + 16 * (nb + 1))
                for fi in range(8):
                    ft = 8 * qtr + fi
                    fcol = off + 8 * ft
                    ins = t.matmul(pall[:, fcol:fcol + 8],
                                   c.RB[0:1, 128 * fi:128 * (fi + 1)],
                                   c.ONES32[0:1, 0:8], start=False,
                                   stop=(qtr == 3 and fi == 7))
                nb += 1
                c.s_te.inc(ins)

    tb1 = c.s_te.n
    sb2 = c.s_se.n

    @block.scalar
    def _(s):
        s.wait_ge(c.s_te.h, tb1)
        for b4 in range(4):
            ins = s.activation(c.EPI[:, 256 * b4:256 * (b4 + 1)],
                               c.PS[:, 512 * b4:512 * b4 + 256], AF.Exp, scale=-1.0)
        c.s_se.inc(ins)

    vb3 = c.s_ve.n

    @block.vector
    def _(v):
        v.wait_ge(c.s_se.h, sb2 + 1)
        v.tensor_scalar(c.EPI[:, 1024:2048], c.EPI[:, 0:1024], 1.0, None, ALU.add)
        v.reciprocal(c.EPI[:, 0:1024], c.EPI[:, 1024:2048])
        pst = c.PS[:, 0:2048].rearrange("p (b x) -> p b x", x=512)[:, :, 0:256]
        v.tensor_mul(UG[:, 0:1024].rearrange("p (b x) -> p b x", x=256), pst,
                     c.EPI[:, 0:1024].rearrange("p (b x) -> p b x", x=256))
        v.tensor_mul(UG[:, 1024:1280], UG[:, 0:256], UG[:, 256:512])
        c.s_ve.inc(v.tensor_mul(UG[:, 1280:1536], UG[:, 512:768], UG[:, 768:1024]))

    vb4 = c.s_ve.n
    tb2 = c.s_te.n
    dv3 = []
    nd2 = [0]

    @block.sync
    def _(sy):
        for e in range(2):
            for ft in range(32):
                if nd2[0] >= 2:
                    sy.wait_ge(c.s_te.h, tb2 + (nd2[0] - 1))
                else:
                    gate_wf(sy, c)
                dma_ser(sy, wsem[nd2[0] % 2],
                        c.WF[:, 1024 * (nd2[0] % 2):1024 * (nd2[0] % 2 + 1)],
                        c.dr["eW2"][e, 128 * ft:128 * (ft + 1), :])
                dv3.append((wsem[nd2[0] % 2], wsem[nd2[0] % 2].n))
                nd2[0] += 1

    @block.tensor
    def _(t):
        t.wait_ge(c.s_ve.h, vb4)
        nm = 0
        for e in range(2):
            zb = 4 + 2 * e
            for ft in range(32):
                t.wait_ge(dv3[nm][0].h, dv3[nm][1])
                lhsT = UG[:, 1024 + 256 * e + 8 * ft: 1024 + 256 * e + 8 * (ft + 1)]
                t.matmul(bank(c, zb)[0:8, 0:512], lhsT,
                         c.WF[:, 1024 * (nm % 2):1024 * (nm % 2) + 512],
                         start=(ft == 0), stop=False)
                ins = t.matmul(bank(c, zb + 1)[0:8, 0:512], lhsT,
                               c.WF[:, 1024 * (nm % 2) + 512:1024 * (nm % 2 + 1)],
                               start=(ft == 0), stop=False)
                nm += 1
                if ft < 31:
                    c.s_te.inc(ins)
            t.matmul(bank(c, zb)[0:8, 0:512], c.ZS32[0:1, 0:8], c.RB[0:1, 0:512],
                     start=False, stop=True)
            ins = t.matmul(bank(c, zb + 1)[0:8, 0:512], c.ZS32[0:1, 0:8], c.RB[0:1, 0:512],
                           start=False, stop=True)
            c.s_te.inc(ins)

    tb3 = c.s_te.n

    @block.vector
    def _(v):
        v.wait_ge(c.s_te.h, tb3)
        v.tensor_copy(S1[0:8, :], c.PS[0:8, 512 * 4:512 * 4 + 1024])
        c.s_ve.inc(v.tensor_copy(S2[0:8, :], c.PS[0:8, 512 * 6:512 * 6 + 1024]))

    vb5 = c.s_ve.n
    gb1 = c.s_ge.n

    @block.gpsimd
    def _(g):
        g.wait_ge(c.s_ve.h, vb5)
        zi = c.agz_in.ap().rearrange("(b two) d -> b two d", two=2)
        dma_ser(g, c.s_ld, zi[:, 0, :], S1[0:8, :])
        dma_ser(g, c.s_ld, zi[:, 1, :], S2[0:8, :])
        g.wait_ge(c.s_ld.h, c.s_ld.n)
        from concourse.aot_env import allow_ici_all_to_all
        with allow_ici_all_to_all():
            c.s_cc.inc(g.collective_compute(
                "AllToAll", ALU.bypass, replica_groups=[list(range(NCORES))],
                ins=[c.agz_in.ap().opt()], outs=[c.agz_out.ap().opt()]))
        g.wait_ge(c.s_cc.h, c.s_cc.n)
        dma_ser(g, c.s_ld, S2, c.agz_out.ap()[0:16, :])
        dma_ser(g, c.s_ld, S1, c.dr["eb2"])
        g.wait_ge(c.s_ld.h, c.s_ld.n)
        c.s_ge.inc(g.nop())

    ge2 = c.s_ge.n

    @block.vector
    def _(v):
        v.wait_ge(c.s_ge.h, ge2)
        v.tensor_add(S2, S2, S1)
        c.s_ve.inc(v.tensor_add(SLo, SLo, S2))



def emit_l0ffn_cmha(c, block):
    nc = c.nc
    SLo = c.SL[0:16, 0:1024]
    S2 = c.SL[0:16, 2048:3072]
    vb0 = c.s_ve.n
    tb0 = c.s_te.n

    @block.tensor
    def _(t):
        t.wait_ge(c.s_ve.h, vb0)
        for k in range(NKC):
            ins = t.transpose(bank(c, 6)[:, 16 * k:16 * (k + 1)],
                              SLo[:, 128 * k:128 * (k + 1)], c.ID32[0:16, 0:16])
        c.s_te.inc(ins)

    @block.vector
    def _(v):
        v.wait_ge(c.s_te.h, tb0 + 1)
        c.s_ve.inc(v.tensor_copy(c.QSL[:, 128:256], bank(c, 6)[:, 0:128]))

    load_bias_row(c, block, c.dr["c_bk"], gates=((c.s_te, c.s_te.n),))
    vbT = c.s_ve.n

    def tm_out_slot_proj(wname, bias_rbb_off, zbank):
        tbp = c.s_te.n
        vcur = c.s_ve.n
        wr = stream_w_rot(c, block, c.dr[wname], NKC, 0, 1024,
                          c.s_te, lambda k: tbp + k + 1)

        @block.tensor
        def _(t):
            t.wait_ge(c.s_ve.h, vcur)
            for k in range(NKC):
                t.wait_ge(wr[k][0].h, wr[k][1])
                lhsT = c.QSL[:, 128 + 16 * k:128 + 16 * (k + 1)]
                t.matmul(bank(c, zbank)[0:16, 0:512], lhsT, c.WS[:, 1024 * (k % 2):1024 * (k % 2) + 512],
                         start=(k == 0), stop=False)
                ins = t.matmul(bank(c, zbank + 1)[0:16, 0:512], lhsT,
                               c.WS[:, 1024 * (k % 2) + 512:1024 * (k % 2 + 1)],
                               start=(k == 0), stop=False)
                c.s_te.inc(ins)
            t.matmul(bank(c, zbank)[0:16, 0:512], c.ONESb[0:1, 0:16],
                     c.RBb[0:1, bias_rbb_off:bias_rbb_off + 512], start=False, stop=True)
            ins = t.matmul(bank(c, zbank + 1)[0:16, 0:512], c.ONESb[0:1, 0:16],
                           c.RBb[0:1, bias_rbb_off + 512:bias_rbb_off + 1024],
                           start=False, stop=True)
            c.s_te.inc(ins)
        return c.s_te.n

    tK = tm_out_slot_proj("c_Wk", 0, 4)

    @block.vector
    def _(v):
        v.wait_ge(c.s_te.h, tK)
        c.s_ve.inc(v.tensor_copy(S2, c.PS[0:16, 512 * 4:512 * 4 + 1024]))

    vb1 = c.s_ve.n
    tb1 = c.s_te.n

    @block.tensor
    def _(t):
        t.wait_ge(c.s_ve.h, vb1)
        for k in range(NKC):
            ins = t.transpose(bank(c, 6)[:, 16 * k:16 * (k + 1)],
                              S2[:, 128 * k:128 * (k + 1)], c.ID32[0:16, 0:16])
        c.s_te.inc(ins)

    @block.vector
    def _(v):
        v.wait_ge(c.s_te.h, tb1 + 1)
        c.s_ve.inc(v.tensor_copy(c.QSL[:, 0:128], bank(c, 6)[:, 0:128]))

    load_bias_row(c, block, c.dr["c_bv"], gates=((c.s_te, c.s_te.n),))
    vbT2 = c.s_ve.n
    tV = tm_out_slot_proj("c_Wv", 0, 4)

    @block.vector
    def _(v):
        v.wait_ge(c.s_te.h, tV)
        dv = c.SL[:].bitcast(BF16)[0:16, 2048:3088].rearrange("p (h e) -> p h e", e=65)
        v.tensor_copy(dv[:, :, 0:64],
                      c.PS[0:16, 512 * 4:512 * 4 + 1024].rearrange("p (h e) -> p h e", e=64))
        c.s_ve.inc(v.memset(dv[:, :, 64:65], 1.0))

    load_bias_fm(c, block, c.dr["c_bq"], 3)
    for q in range(4):
        wr_q = stream_w_bf16(c, block, c.dr["c_Wq"][:, 256 * q:256 * (q + 1)], NKC, 0, wc=256,
                             gate0=(c.s_te, c.s_te.n))
        emit_proj_fm(c, block, c.A3[:, 0:4096], 0, 0, wr_q,
                     lambda k, lo, hi: c.XN[:, 2048 * k + lo:2048 * k + hi],
                     [2 * q, 2 * q + 1], bias_col=3, wslice=256)
        for jj in range(2):
            ch = 2 * q + jj
            tb = c.s_te.n
            sb = c.s_se.n
            vb = c.s_ve.n
            gb = c.s_ge.n

            @block.tensor
            def _(t, jj=jj, ch=ch, tb=tb, sb=sb, vb=vb):
                t.wait_ge(c.s_ve.h, vb)
                for w in range(4):
                    if w > 0:
                        t.wait_ge(c.s_ve.h, vb + 3 * w)
                    kA = c.QSL[0:64, 16 * ch:16 * (ch + 1)]
                    kB = c.QSL[64:128, 16 * ch:16 * (ch + 1)]
                    rA = c.A3[0:64, 2048 * jj + 512 * w:2048 * jj + 512 * (w + 1)]
                    rB = c.A3[64:128, 2048 * jj + 512 * w:2048 * jj + 512 * (w + 1)]
                    if w >= 1:
                        t.wait_ge(c.s_se.h, sb + w)
                    t.matmul(bank(c, 0)[0:16, 0:512], kA, rA, start=True, stop=True)
                    ins = t.matmul(bank(c, 1)[0:16, 0:512], kB, rB, start=True, stop=True)
                    c.s_te.inc(ins)
                    t.wait_ge(c.s_se.h, sb + w + 1)
                    for hh in range(2):
                        vsl = c.SL[:].bitcast(BF16)[0:16, 2048 + 65 * (2 * ch + hh):
                                                    2048 + 65 * (2 * ch + hh) + 65]
                        psl = c.PT[0:16, 512 * hh:512 * hh + 512]
                        ins = t.matmul(bank(c, 4 + hh)[0:65, 0:512], vsl, psl,
                                       start=True, stop=True)
                    c.s_te.inc(ins)
                    t.wait_ge(c.s_ve.h, vb + 3 * w + 1)
                    rr = c.EPI[0:1, 1024:2048]
                    t.matmul(bank(c, 6)[0:64, 0:512], c.ONES32[:, 0:64], rr[:, 0:512],
                             start=True, stop=True)
                    ins = t.matmul(bank(c, 7)[0:64, 0:512], c.ONES32[:, 0:64], rr[:, 512:1024],
                                   start=True, stop=True)
                    c.s_te.inc(ins)

            @block.scalar
            def _(s, tb=tb, sb=sb):
                for w in range(4):
                    s.wait_ge(c.s_te.h, tb + 3 * w + 1)
                    s.activation(c.PT[0:16, 0:512], bank(c, 0)[0:16, 0:512], AF.Exp, scale=0.125)
                    c.s_se.inc(s.activation(c.PT[0:16, 512:1024], bank(c, 1)[0:16, 0:512],
                                            AF.Exp, scale=0.125))

            @block.vector
            def _(v, ch=ch, tb=tb, vb=vb, gb=gb):
                for w in range(4):
                    v.wait_ge(c.s_te.h, tb + 3 * w + 2)
                    rr = c.EPI[0:1, 1024:2048]
                    c.s_ve.inc(v.reciprocal(rr, c.PS[64:65, 512 * 4:512 * 4 + 1024]))
                    v.wait_ge(c.s_te.h, tb + 3 * w + 3)
                    bc = c.EPI[0:64, 1024:2048]
                    c.s_ve.inc(v.tensor_copy(bc, c.PS[0:64, 512 * 6:512 * 6 + 1024]))
                    if gb + w > 0:
                        v.wait_ge(c.s_ge.h, gb + w)
                    c.s_ve.inc(v.tensor_mul(c.AMN[0:64, 0:1024],
                                            c.PS[0:64, 512 * 4:512 * 4 + 1024], bc))

            @block.gpsimd
            def _(g, ch=ch, vb=vb):
                for w in range(4):
                    g.wait_ge(c.s_ve.h, vb + 3 * w + 3)
                    dstA = c.A2[0:64, 2048 * ch + 512 * w:2048 * ch + 512 * (w + 1)]
                    dstB = c.A2[64:128, 2048 * ch + 512 * w:2048 * ch + 512 * (w + 1)]
                    dma_ser(g, c.s_st, dstA, c.AMN[0:64, 0:512])
                    dma_ser(g, c.s_st, dstB, c.AMN[0:64, 512:1024])
                    g.wait_ge(c.s_st.h, c.s_st.n)
                    c.s_ge.inc(g.nop())



def emit_l1ffn(c, block):
    nc = c.nc
    ldb = c.s_ld.n

    @block.gpsimd
    def _(g):
        dma_ser(g, c.s_ld, c.FSM[:, 0:16], c.dr["f_W1"])
        dma_ser(g, c.s_ld, c.FSM[:, 16:32], c.dr["f_b1"])
        dma_ser(g, c.s_ld, c.FSM[:, 32:48], c.dr["f_W2"])
        dma_ser(g, c.s_ld, c.FSM[:, 64:65], c.dr["f_bin"].rearrange("(d a) -> d a", a=1))
        dma_ser(g, c.s_ld, c.FSM[:, 65:66], c.dr["f_b2"].rearrange("(d a) -> d a", a=1))
        gate_wf(g, c)
        dma_ser(g, c.s_ld, c.WF[0:64, 0:1024], c.dr["f_Wout"])

    load_bias_row(c, block, c.dr["f_bout"], gates=((c.s_te, c.s_te.n),))

    @block.vector
    def _(v):
        v.wait_ge(c.s_ld.h, ldb + 96)
        v.scalar_tensor_tensor(c.FSM[:, 48:64], c.FSM[:, 0:16], c.FSM[:, 64:65],
                               c.FSM[:, 16:32], ALU.mult, ALU.add)
        c.s_ve.inc(v.tensor_copy(c.WS[0:64, 0:1024], c.WF[0:64, 0:1024]))

    vbW = c.s_ve.n

    @block.gpsimd
    def _(g):
        g.wait_ge(c.s_ve.h, vbW)
        dma_ser(g, c.s_ld, c.WS[64:65, 0:1024], c.RBb[0:1, 0:1024])

    ldW = c.s_ld.n

    wr_in = stream_w_bf16(c, block, c.dr["f_Win"], NKC, 1024, wc=64)

    tb0 = c.s_te.n
    vb0 = c.s_ve.n

    @block.tensor
    def _(t):
        t.wait_ge(c.s_ve.h, vb0)
        for k in range(NKC):
            t.wait_ge(wr_in[k][0].h, wr_in[k][1])
            lhsT = c.WS[:, 1024 + 64 * k:1024 + 64 * (k + 1)]
            for w in range(4):
                t.matmul(c.PS[0:64, 512 * w:512 * (w + 1)], lhsT,
                         c.XN[:, 2048 * k + 512 * w:2048 * k + 512 * (w + 1)],
                         start=(k == 0), stop=False)
        for w in range(4):
            ins = t.matmul(c.PS[0:64, 512 * w:512 * (w + 1)], c.ZSB[0:1, 0:64],
                           c.RBb[0:1, 0:512], start=False, stop=True)
        c.s_te.inc(ins)

    sb0 = c.s_se.n
    vb1 = c.s_ve.n

    @block.scalar
    def _(s):
        s.wait_ge(c.s_te.h, tb0 + 1)
        for i in range(16):
            if i >= 1:
                s.wait_ge(c.s_ve.h, vb1 + i)
            ins = s.activation(c.EPI[0:64, 0:2048], c.PS[0:64, 0:2048], AF.Relu,
                               scale=c.FSM[:, i:i + 1], bias=c.FSM[:, 48 + i:48 + i + 1])
            c.s_se.inc(ins)

    gA3 = c.A3[:].bitcast(F32)[0:64, 0:2048]
    GBB = c.A3[0:65, 4096:6144]

    @block.vector
    def _(v):
        for i in range(16):
            v.wait_ge(c.s_se.h, sb0 + i + 1)
            if i == 0:
                ins = v.tensor_scalar(gA3, c.EPI[0:64, 0:2048],
                                      c.FSM[:, 32:33], None, ALU.mult)
            else:
                ins = v.scalar_tensor_tensor(gA3, c.EPI[0:64, 0:2048],
                                             c.FSM[:, 32 + i:32 + i + 1],
                                             gA3, ALU.mult, ALU.add)
            c.s_ve.inc(ins)
        v.tensor_scalar(GBB[0:64, :], gA3, c.FSM[:, 65:66], None, ALU.add)
        c.s_ve.inc(v.memset(GBB[64:65, :], 1.0))

    vbG = c.s_ve.n

    GBB2 = c.A3[0:65, 4096:6144]

    def tile_mms(t, j, b0):
        if j == 0:
            t.wait_ge(c.s_ld.h, ldW)
        lhsT = GBB2[:, 128 * j:128 * (j + 1)]
        t.matmul(bank(c, b0)[:, 0:512], lhsT, c.WS[0:65, 0:512], start=True, stop=True)
        return t.matmul(bank(c, b0 + 1)[:, 0:512], lhsT, c.WS[0:65, 512:1024],
                        start=True, stop=True)

    emit_tmout_sandwich(c, block, tile_mms, (c.s_ve, vbG), c.STATS[:, 112:128])


def emit_tmout_sandwich(c, block, tile_mms, gate, sums_dst):
    nc = c.nc
    tb = c.s_te.n
    vb = c.s_ve.n
    sb = c.s_se.n
    negm2 = c.STATS[:, 96:97]
    r2 = c.STATS[:, 97:98]
    v1t = c.STATS[:, 98:99]
    v2t = c.STATS[:, 99:100]
    ssq = c.STATS[:, 100:101]
    ssm = c.STATS[:, 101:102]
    junk = c.EPI[:, 0:1024]
    zn = c.EPI[:, 1024:2048]

    @block.tensor
    def _(t):
        if gate is not None:
            t.wait_ge(gate[0].h, gate[1])
        if vb:
            t.wait_ge(c.s_ve.h, vb)
        for j in range(NT):
            b0 = 0 if j % 2 == 0 else 2
            if j >= 2:
                t.wait_ge(c.s_ve.h, vb + 4 * (j - 1))
            ins = tile_mms(t, j, b0)
            c.s_te.inc(ins)

    lvl = OP_LVL

    @block.vector
    def _(v):
        for j in range(NT):
            b0 = 0 if j % 2 == 0 else 2
            pz = c.PS[:, 512 * b0:512 * b0 + 1024]
            v.wait_ge(c.s_te.h, tb + j + 1)
            if lvl >= 2:
                c.s_ve.inc(v.reduce_sum(ssm, pz, axis=AX))
            else:
                c.s_ve.inc(v.tensor_copy(zn, pz[:, 0:1024]))
            if lvl >= 2:
                v.wait_ge(c.s_se.h, sb + 3 * j + 1)
            v.tensor_scalar(negm2, ssm, -1.0 / D, None, ALU.mult)
            v.tensor_mul(v2t, negm2, negm2)
            v.tensor_scalar(v1t, ssq, 1.0 / D, None, ALU.mult)
            c.s_ve.inc(v.tensor_sub(v1t, v1t, v2t))
            if lvl >= 3:
                v.wait_ge(c.s_se.h, sb + 3 * j + 3)
                v.tensor_scalar(zn, pz, negm2, r2, ALU.add, ALU.mult)
            if lvl >= 4:
                ins = v.scalar_tensor_tensor(
                    xtile(c, j), zn, 1.0, xtile(c, j), ALU.mult, ALU.add,
                    accum_out=(sums_dst[:, j:j + 1] if sums_dst is not None else None))
                c.s_ve.inc(ins)
            else:
                c.s_ve.inc(v.nop())
            v.wait_ge(c.s_ve.h, c.s_ve.n)
            c.s_ve.inc(v.nop())

    @block.scalar
    def _(s):
        for j in range(NT):
            b0 = 0 if j % 2 == 0 else 2
            pz = c.PS[:, 512 * b0:512 * b0 + 1024]
            s.wait_ge(c.s_te.h, tb + j + 1)
            s.wait_ge(c.s_ve.h, vb + 4 * j + 1)
            if lvl >= 2:
                c.s_se.inc(s.activation(junk, pz, AF.Square, accum_out=ssq))
            else:
                c.s_se.inc(s.nop())
            if lvl >= 3:
                s.wait_ge(c.s_ve.h, vb + 4 * j + 2)
                c.s_se.inc(s.activation(v1t, v1t, AF.Ln, bias=c.EPSC[:]))
                c.s_se.inc(s.activation(r2, v1t, AF.Exp, scale=-0.5))
            else:
                c.s_se.inc(s.nop())
                c.s_se.inc(s.nop())



def emit_l0_attn(c, block, upto=4):
    load_bias_fm(c, block, c.dr["l0_bq"], 0)
    load_bias_fm(c, block, c.dr["l0_bk"], 1)
    load_bias_row(c, block, c.dr["l0_bv"])
    xn = lambda k, lo, hi: c.XN[:, 2048 * k + lo:2048 * k + hi]
    for q in range(4):
        wrq = stream_w_bf16(c, block, c.dr["l0_WqT"][:, 256 * q:256 * (q + 1)], NKC, 0,
                            wc=256, gate0=(c.s_te, c.s_te.n))
        wrk = stream_w_bf16(c, block, c.dr["l0_WkT"][:, 256 * q:256 * (q + 1)], NKC, 2048, wc=256)
        wrv = stream_w_bf16(c, block, c.dr["l0_WvT"][:, 256 * q:256 * (q + 1)], NKC, 4096, wc=256)
        emit_proj_fm(c, block, c.A3[:, 0:4096], 0, 0, wrq, xn, [2 * q, 2 * q + 1], bias_col=0)
        emit_proj_fm(c, block, c.A3[:, 4096:8192], 0, 2048, wrk, xn, [2 * q, 2 * q + 1], bias_col=1)
        emit_vprime_proj(c, block, c.A3[:, 8192:12352], 4, 4096, wrv, xn, 256,
                         None, 256 * q, zb=0)
        if upto >= 3:
            emit_attention_quarter(c, block, q, 0.125, c.A2)
    if upto >= 4:
        load_bias_row(c, block, c.dr["l0_bo"], gates=((c.s_te, c.s_te.n),))
        wro = stream_w_bf16(c, block, c.dr["l0_WoT"], NKC, 0, wc=1024,
                            gate0=(c.s_te, c.s_te.n), dst=c.A3)
        if OP_LVL >= 1:
            emit_oproj_sandwich(c, block, c.A2, wro, 0, c.STATS[:, 112:128],
                                gate=(c.s_ge, c.s_ge.n))


def emit_l1_attn(c, block):
    @block.vector
    def _(v):
        c.s_ve.inc(v.memset(c.RBb[0:1, 0:1024], 0.0))

    xn = lambda k, lo, hi: c.XN[:, 2048 * k + lo:2048 * k + hi]
    for q in range(4):
        wrq = stream_masked_w(c, block, c.dr["l1_qkvT"][:, 256 * q:256 * (q + 1)],
                              c.dr["l1_qkvmT"][:, 256 * q:256 * (q + 1)], NKC, 0, 256,
                              gate0=(c.s_te, c.s_te.n))
        wrk = stream_masked_w(c, block, c.dr["l1_qkvT"][:, 1024 + 256 * q:1024 + 256 * (q + 1)],
                              c.dr["l1_qkvmT"][:, 1024 + 256 * q:1024 + 256 * (q + 1)],
                              NKC, 2048, 256)
        wrv = stream_masked_w(c, block, c.dr["l1_qkvT"][:, 2048 + 256 * q:2048 + 256 * (q + 1)],
                              c.dr["l1_qkvmT"][:, 2048 + 256 * q:2048 + 256 * (q + 1)],
                              NKC, 4096, 256)
        emit_proj_fm(c, block, c.A3[:, 0:4096], 0, 0, wrq, xn, [2 * q, 2 * q + 1], bias_col=None)
        emit_proj_fm(c, block, c.A3[:, 4096:8192], 0, 2048, wrk, xn, [2 * q, 2 * q + 1], bias_col=None)
        emit_vprime_proj(c, block, c.A3[:, 8192:12352], 4, 4096, wrv, xn, 256,
                         None, 256 * q, zb=0)
        emit_attention_quarter(c, block, q, 1.0 / 32, c.A2)
    wro = stream_masked_w(c, block, c.dr["l1_outT"], c.dr["l1_outmT"], NKC, 0, 1024,
                          gate0=(c.s_te, c.s_te.n), dst=c.A3)
    teb = c.s_te.n
    ldb = c.s_ld.n

    @block.gpsimd
    def _(g):
        g.wait_ge(c.s_te.h, teb)
        dma_ser(g, c.s_ld, c.RB[0:1, 0:1024], c.dr["l1_ob"].rearrange("(a d) -> a d", a=1))
        dma_ser(g, c.s_ld, c.EPI[0:1, 0:1024], c.dr["l1_obm"].rearrange("(a d) -> a d", a=1))
        g.wait_ge(c.s_ld.h, c.s_ld.n)
        g.tensor_scalar(c.EPI[0:1, 0:1024], c.EPI[0:1, 0:1024], 0.0, None, ALU.is_gt)
        c.s_ge.inc(g.tensor_tensor(c.RBb[0:1, 0:1024], c.EPI[0:1, 0:1024],
                                   c.RB[0:1, 0:1024], ALU.mult))

    emit_oproj_sandwich(c, block, c.A2, wro, 0, c.STATS[:, 112:128],
                        gate=(c.s_ge, c.s_ge.n))



def emit_all(c, block):
    emit_init(c, block)

    ph = 99 if DEV_PHASE is None else DEV_PHASE
    if ph >= 1:
        negm, r = emit_ln_stats(c, block, lambda j: xtile(c, j), wait_sems=(c.s_ld,))
        emit_ln_apply_transpose(c, block, lambda j: xtile(c, j), negm, r, c.XN, c.A2)
    if ph >= 2:
        emit_l0_attn(c, block, upto=min(ph, 4))
    if N_SUB >= 2:
        negm, r = emit_ln_stats(c, block, lambda j: xtile(c, j),
                                have_sums=c.STATS[:, 112:128], wait_sems=(c.s_ve,))
        emit_ln_apply_transpose(c, block, lambda j: xtile(c, j), negm, r, c.XN, c.A2)
        emit_l0ffn_slots(c, block)
        if DEBUG_DUMPS:
            vdump = c.s_ve.n

            @block.gpsimd
            def _(g):
                g.wait_ge(c.s_ve.h, vdump)
                dma_ser(g, c.s_st, c.dbg_o, c.SL[0:16, 0:1024])
                g.wait_ge(c.s_st.h, c.s_st.n)
                c.s_ge.inc(g.nop())
        emit_l0ffn_experts(c, block)
        if DEBUG_DUMPS:
            vdump2 = c.s_ve.n

            @block.gpsimd
            def _(g):
                g.wait_ge(c.s_ve.h, vdump2)
                dma_ser(g, c.s_st, c.dbg_of, c.SL[0:16, 0:1024])
                dma_ser(g, c.s_st, c.dbg_h, c.SL[0:16, 2048:3072])
                g.wait_ge(c.s_st.h, c.s_st.n)
                c.s_ge.inc(g.nop())
        emit_l0ffn_cmha(c, block)
        if DEBUG_DUMPS:
            vdump3 = c.s_ge.n

            @block.gpsimd
            def _(g):
                g.wait_ge(c.s_ge.h, vdump3)
                dma_ser(g, c.s_st, c.dbg_a2, c.A2[:])
                dma_ser(g, c.s_st, c.dbg_ks, c.QSL[:])
                g.wait_ge(c.s_st.h, c.s_st.n)
                c.s_ge.inc(g.nop())
        load_bias_row(c, block, c.dr["c_bo"], gates=((c.s_te, c.s_te.n),))
        wr_co = stream_w_bf16(c, block, c.dr["c_Wo"], NKC, 0, wc=1024,
                              gate0=(c.s_te, c.s_te.n), dst=c.A3)
        emit_oproj_sandwich(c, block, c.A2, wr_co, 0, c.STATS[:, 112:128],
                            gate=(c.s_ge, c.s_ge.n))
    if N_SUB >= 3:
        negm, r = emit_ln_stats(c, block, lambda j: xtile(c, j),
                                have_sums=c.STATS[:, 112:128], wait_sems=(c.s_ve,))
        emit_ln_apply_transpose(c, block, lambda j: xtile(c, j), negm, r, c.XN, c.A2)
        emit_l1_attn(c, block)
    if N_SUB >= 4:
        negm, r = emit_ln_stats(c, block, lambda j: xtile(c, j),
                                have_sums=c.STATS[:, 112:128], wait_sems=(c.s_ve,))
        emit_ln_apply_transpose(c, block, lambda j: xtile(c, j), negm, r, c.XN, c.A2)
        emit_l1ffn(c, block)

    vfin = c.s_ve.n

    @block.gpsimd
    def _(g):
        g.wait_ge(c.s_ve.h, vfin)
        for j in range(NT):
            dma_ser(g, c.s_st, c.out[128 * j:128 * (j + 1), :], xtile(c, j))
        g.wait_ge(c.s_st.h, c.s_st.n)


def build_program():
    nc, c = build_nc()
    with nc.Block() as block:
        emit_all(c, wrap_block(block, c))
    return nc, c



def _prep_inputs(inputs):
    f = np.ascontiguousarray
    g = {}
    g["l0_WqT"] = f(inputs["L0_attn_Wq"].T)
    g["l0_WkT"] = f(inputs["L0_attn_Wk"].T)
    g["l0_WvT"] = f(inputs["L0_attn_Wv"].T)
    g["l0_WoT"] = f(inputs["L0_attn_Wo"].T)
    g["l0_bq"] = inputs["L0_attn_bq"]
    g["l0_bk"] = inputs["L0_attn_bk"]
    g["l0_bv"] = inputs["L0_attn_bv"]
    g["l0_bo"] = inputs["L0_attn_bo"]
    g["sq"] = f(inputs["L0_ffn_slot_query"][0])
    for p in ("d", "c"):
        for m in ("Wq", "Wk", "Wv", "Wo"):
            g[f"{p}_{m}"] = f(inputs[f"L0_ffn_{p}_{m}"])
        for m in ("bq", "bk", "bv", "bo"):
            g[f"{p}_{m}"] = inputs[f"L0_ffn_{p}_{m}"]
    g["eb2"] = inputs["L0_ffn_e_b2"]
    g["l1_qkvT"] = f(inputs["L1_attn_qkv_weight"].T)
    g["l1_qkvmT"] = f(inputs["L1_attn_qkv_weight_mask"].T)
    g["l1_outT"] = f(inputs["L1_attn_out_weight"].T)
    g["l1_outmT"] = f(inputs["L1_attn_out_weight_mask"].T)
    g["l1_ob"] = inputs["L1_attn_out_bias"]
    g["l1_obm"] = inputs["L1_attn_out_bias_mask"]
    g["f_Win"] = inputs["L1_ffn_Win"]
    g["f_bin"] = inputs["L1_ffn_bin_"]
    g["f_W1"] = inputs["L1_ffn_W1"]
    g["f_b1"] = inputs["L1_ffn_b1"]
    g["f_W2"] = inputs["L1_ffn_W2"]
    g["f_b2"] = inputs["L1_ffn_b2"]
    g["f_Wout"] = inputs["L1_ffn_Wout"]
    g["f_bout"] = inputs["L1_ffn_bout"]
    y = inputs["L0_attn_y"]
    in_maps = []
    for cid in range(NCORES):
        m = dict(g)
        m["y"] = f(y[cid])
        m["e_ln_g"] = f(inputs["L0_ffn_e_ln_g"][2 * cid:2 * cid + 2])
        m["e_ln_b"] = f(inputs["L0_ffn_e_ln_b"][2 * cid:2 * cid + 2])
        m["eW1"] = f(inputs["L0_ffn_e_W1"][2 * cid:2 * cid + 2])
        m["eWg"] = f(inputs["L0_ffn_e_Wg"][2 * cid:2 * cid + 2])
        m["eW2"] = f(inputs["L0_ffn_e_W2"][2 * cid:2 * cid + 2])
        m["eb1"] = f(inputs["L0_ffn_e_b1"][2 * cid:2 * cid + 2])
        m["ebg"] = f(inputs["L0_ffn_e_bg"][2 * cid:2 * cid + 2])
        in_maps.append(m)
    return in_maps


def kernel(**inputs):
    in_maps = _prep_inputs(inputs)
    nc, _ = build_program()
    res = run_bass_kernel_spmd(nc, in_maps, core_ids=list(range(NCORES)))
    return np.stack([res.results[cid]["out"] for cid in range(NCORES)], axis=0)


# version 03
# speedup vs baseline: 1.0194x; 1.0194x over previous
import numpy as np
import concourse.bass as bass
import concourse.mybir as mybir
from concourse.bass_utils import run_bass_kernel_spmd

F32 = mybir.dt.float32
BF16 = mybir.dt.bfloat16
AF = mybir.ActivationFunctionType
ALU = mybir.AluOpType
AX = mybir.AxisListType.X

NCORES = 8
T = 2048
D = 1024
H = 16
DH = 64
NT = T // 128
NKC = D // 128
EPS = 1e-5

N_SUB = 4
DEBUG_DUMPS = False
SELF_EDGES = "v"
DEV_PHASE = None
OP_LVL = 4


_PENDING = {}
_SEEN = {}


def traceback_stack():
    import traceback
    return traceback.format_stack()[-6:-2]



class Sem:
    def __init__(self, nc, name):
        self.h = nc.alloc_semaphore(name)
        self.n = 0

    def inc(self, instr, k=1):
        key = id(instr)
        if key in _SEEN and _SEEN[key][0] is self and _SEEN[key][2] is instr:
            import traceback
            raise RuntimeError("double Sem.inc on same instruction: first at\n"
                               + "".join(_SEEN[key][1]))
        _SEEN[key] = (self, traceback_stack(), instr)
        instr.then_inc(self.h, k)
        self.n += k
        w = _PENDING.pop(id(instr), None)
        if w is not None:
            w._last = (self, self.n)
            w._pending = None
        return instr

    def wait(self, eng):
        if self.n:
            eng.wait_ge(self.h, self.n)


class Ctx:
    pass


def build_nc():
    nc = bass.Bass("TRN2", target_bir_lowering=False, debug=False)
    c = Ctx()
    c.nc = nc

    dr = {}

    def din(name, shape, dt=F32):
        dr[name] = nc.dram_tensor(name, list(shape), dt, kind="ExternalInput").ap()
        return dr[name]

    din("y", (T, D))
    din("l0_WqT", (D, D)); din("l0_WkT", (D, D)); din("l0_WvT", (D, D)); din("l0_WoT", (D, D))
    din("l0_bq", (D,)); din("l0_bk", (D,)); din("l0_bv", (D,)); din("l0_bo", (D,))
    din("sq", (16, D))
    for p in ("d", "c"):
        for m in ("Wq", "Wk", "Wv", "Wo"):
            din(f"{p}_{m}", (D, D))
        for m in ("bq", "bk", "bv", "bo"):
            din(f"{p}_{m}", (D,))
    din("e_ln_g", (2, D)); din("e_ln_b", (2, D))
    din("eW1", (2, D, 4096)); din("eWg", (2, D, 4096)); din("eW2", (2, 4096, D))
    din("eb1", (2, 4096)); din("ebg", (2, 4096)); din("eb2", (16, D))
    din("l1_qkvT", (D, 3 * D)); din("l1_qkvmT", (D, 3 * D))
    din("l1_outT", (D, D)); din("l1_outmT", (D, D))
    din("l1_ob", (D,)); din("l1_obm", (D,))
    din("f_Win", (D, 64)); din("f_bin", (64,))
    din("f_W1", (64, 16)); din("f_b1", (64, 16)); din("f_W2", (64, 16)); din("f_b2", (64,))
    din("f_Wout", (64, D)); din("f_bout", (D,))
    c.dr = dr

    c.out = nc.dram_tensor("out", [T, D], F32, kind="ExternalOutput").ap()
    if DEBUG_DUMPS:
        c.dbg_o = nc.dram_tensor("dbg_o", [16, D], F32, kind="ExternalOutput").ap()
        c.dbg_of = nc.dram_tensor("dbg_of", [16, D], F32, kind="ExternalOutput").ap()
        c.dbg_h = nc.dram_tensor("dbg_h", [16, D], F32, kind="ExternalOutput").ap()
        c.dbg_a2 = nc.dram_tensor("dbg_a2", [128, 16384], BF16, kind="ExternalOutput").ap()
        c.dbg_ks = nc.dram_tensor("dbg_ks", [128, 256], BF16, kind="ExternalOutput").ap()

    c.ag1_in = nc.dram_tensor("ag1_in", [16, D], F32)
    c.ag1_out = nc.dram_tensor("ag1_out", [16, D], F32)
    c.agz_in = nc.dram_tensor("agz_in", [16, D], F32)
    c.agz_out = nc.dram_tensor("agz_out", [16, D], F32)

    A = nc.alloc_sbuf_tensor
    c.X = A("X", [128, NT * D], F32)
    c.XN = A("XN", [128, NKC * T], BF16)
    c.A2 = A("A2", [128, NKC * T], BF16)
    c.A3 = A("A3", [128, 12352], BF16)
    c.WS = A("WS", [128, 6144], BF16)
    c.WF = A("WF", [128, 2048], F32)
    c.PT = A("PT", [128, 2048], BF16)
    c.EPI = A("EPI", [128, 2048], F32)
    c.STATS = A("STATS", [128, 128], F32)
    c.RB = A("RB", [1, 1024], F32)
    c.RBb = A("RBb", [1, 2048], BF16)
    c.ID32 = A("ID32", [16, 16], F32)
    c.IDb = A("IDb", [128, 128], BF16)
    c.ONESb = A("ONESb", [1, 128], BF16)
    c.ONES32 = A("ONES32", [1, 64], F32)
    c.SL = A("SL", [16, 3072], F32)
    c.HT = A("HT", [128, 128], F32)
    c.BSM = A("BSM", [128, 32], F32)
    c.QSL = A("QSL", [128, 256], BF16)
    c.FSM = A("FSM", [64, 96], F32)
    c.EPSC = A("EPSC", [128, 1], F32)
    c.ZSB = A("ZSB", [1, 128], BF16)
    c.ZS32 = A("ZS32", [1, 16], F32)
    c.AMN = c.EPI[:].bitcast(BF16)

    c.PS = nc.alloc_psum_tensor("PS", [128, 4096], F32)
    c.PSb = c.PS[:].bitcast(BF16)

    c.s_ld = Sem(nc, "s_ld")
    c.s_wt = Sem(nc, "s_wt")
    c.s_wu = Sem(nc, "s_wu")
    c.s_te = Sem(nc, "s_te")
    c.s_ve = Sem(nc, "s_ve")
    c.s_se = Sem(nc, "s_se")
    c.s_ge = Sem(nc, "s_ge")
    c.s_st = Sem(nc, "s_st")
    c.s_cc = Sem(nc, "s_cc")
    c.s_vv = Sem(nc, "s_vv")
    c.s_ss = Sem(nc, "s_ss")
    c.s_gg = Sem(nc, "s_gg")
    c.wv = EngWrap(c.s_vv, "v")
    c.ws_ = EngWrap(c.s_ss, "s")
    c.wg = EngWrap(c.s_gg, "g")
    return nc, c


_WRAP_OPS = {
    "tensor_copy", "tensor_tensor", "tensor_add", "tensor_sub", "tensor_mul",
    "tensor_max", "tensor_scalar", "tensor_single_scalar", "tensor_scalar_add",
    "tensor_scalar_sub", "tensor_scalar_mul", "tensor_scalar_max",
    "tensor_scalar_min", "tensor_relu", "tensor_reduce", "reduce_sum",
    "reduce_max", "scalar_tensor_tensor", "tensor_tensor_scan", "memset",
    "memzero", "reciprocal", "select", "bn_stats", "bn_aggr", "iota",
    "affine_select", "activation", "sign", "square", "sqrt", "mul", "add",
    "copy", "transpose",
}


class EngWrap:

    def __init__(self, sem, tag):
        self._e = None
        self._s = sem
        self._tag = tag
        self._pending = None
        self._last = None

    def _flush(self):
        p = self._pending
        if p is not None and id(p) in _PENDING:
            _PENDING.pop(id(p))
            p.then_inc(self._s.h, 1)
            self._s.n += 1
            self._last = (self._s, self._s.n)
        self._pending = None

    def __getattr__(self, name):
        fn = getattr(self._e, name)
        if name not in _WRAP_OPS or self._tag not in SELF_EDGES:
            return fn

        def call(*a, **k):
            self._flush()
            if self._last is not None:
                self._e.wait_ge(self._last[0].h, self._last[1])
            ins = fn(*a, **k)
            self._pending = ins
            _PENDING[id(ins)] = self
            return ins
        return call


def wrap_block(block, c):
    class B:
        pass
    b = B()

    def mk(section, w):
        def deco(f):
            def g(eng):
                w._e = eng
                return f(w)
            return section(g)
        return deco
    b.vector = mk(block.vector, c.wv)
    b.scalar = mk(block.scalar, c.ws_)
    b.gpsimd = mk(block.gpsimd, c.wg)
    b.tensor = block.tensor
    b.sync = block.sync
    return b


def gate_wf(eng, c):
    if c.s_ve.n:
        eng.wait_ge(c.s_ve.h, c.s_ve.n)
    if c.s_te.n:
        eng.wait_ge(c.s_te.h, c.s_te.n)
    if c.s_ge.n:
        eng.wait_ge(c.s_ge.h, c.s_ge.n)


def dma_ser(eng, sem, out, in_, **kw):
    if sem.n:
        eng.wait_ge(sem.h, sem.n)
    return sem.inc(eng.dma_start(out=out, in_=in_, **kw), 16)


def bank(c, b, n=1):
    return c.PS[:, 512 * b: 512 * (b + n)]


def xtile(c, j):
    return c.X[:, 1024 * j: 1024 * (j + 1)]


def emit_init(c, block):
    nc = c.nc

    @block.gpsimd
    def _(g):
        for j in range(NT):
            dma_ser(g, c.s_ld, xtile(c, j), c.dr["y"][128 * j:128 * (j + 1), :])
        c.s_ge.inc(g.memset(c.ID32[:], 0.0))
        c.s_ge.inc(g.memset(c.IDb[:], 0.0))
        g.wait_ge(c.s_ge.h, c.s_ge.n)
        c.s_ge.inc(g.affine_select(out=c.ID32[:], in_=c.ID32[:], compare_op=ALU.not_equal,
                                   fill=1.0, base=0, pattern=[[-1, 16]], channel_multiplier=1))
        c.s_ge.inc(g.affine_select(out=c.IDb[:], in_=c.IDb[:], compare_op=ALU.not_equal,
                                   fill=1.0, base=0, pattern=[[-1, 128]], channel_multiplier=1))
        g.wait_ge(c.s_ge.h, c.s_ge.n)
        c.s_ge.inc(g.nop())

    @block.vector
    def _(v):
        v.memset(c.ONESb[:], 1.0)
        v.memset(c.ONES32[0:1, 0:64], 1.0)
        v.memset(c.EPSC[:], EPS)
        v.memset(c.ZSB[:], 0.0)
        v.memset(c.ZS32[:], 0.0)
        v.memset(c.RBb[0:1, :], 0.0)
        v.memset(c.RB[0:1, :], 0.0)
        c.s_ve.inc(v.nop())

    @block.scalar
    def _(s):
        s.add_instruction(mybir.InstLoadActFuncSet(
            name=nc.get_next_instruction_name(), act_func_set_id=6, ins=[], outs=[]))
        c.s_se.inc(s.nop())

    @block.tensor
    def _(t):
        t.wait_ge(c.s_ge.h, c.s_ge.n)


def emit_ln_stats(c, block, src_tiles, have_sums=None, wait_sems=()):
    nc = c.nc
    ssum = c.STATS[:, 32:48]
    ssq = c.STATS[:, 48:64]
    negm = c.STATS[:, 0:16]
    r = c.STATS[:, 16:32]
    t1 = c.STATS[:, 64:80]
    t2 = c.STATS[:, 80:96]
    junk = c.EPI[:, 0:1024]

    if have_sums is None:
        @block.vector
        def _(v):
            for s in wait_sems:
                s.wait(v)
            for j in range(NT):
                ins = v.reduce_sum(ssum[:, j:j + 1], src_tiles(j), axis=AX)
            c.s_ve.inc(ins)

    @block.scalar
    def _(s):
        for sm in wait_sems:
            sm.wait(s)
        for j in range(NT):
            if j:
                s.wait_ge(c.s_se.h, c.s_se.n)
            c.s_se.inc(s.activation(junk[:], src_tiles(j), AF.Square, accum_out=ssq[:, j:j + 1]))

    sums = have_sums if have_sums is not None else ssum

    @block.vector
    def _(v):
        v.wait_ge(c.s_ve.h, c.s_ve.n)
        c.s_se.wait(v)
        v.tensor_scalar(negm[:], sums, -1.0 / D, None, ALU.mult)
        v.tensor_mul(t2[:], negm[:], negm[:])
        v.tensor_scalar(t1[:], ssq[:], 1.0 / D, None, ALU.mult)
        c.s_ve.inc(v.tensor_sub(t1[:], t1[:], t2[:]))

    @block.scalar
    def _(s):
        c.s_ve.wait(s)
        s.activation(t1[:], t1[:], AF.Ln, bias=c.EPSC[:])
        c.s_se.inc(s.activation(r[:], t1[:], AF.Exp, scale=-0.5))
    return negm, r


def emit_ln_apply_transpose(c, block, src_tiles, negm, r, dst_fm, stage):
    nc = c.nc
    vb = c.s_ve.n
    tb = c.s_te.n

    @block.vector
    def _(v):
        c.s_se.wait(v)
        for j in range(NT):
            ins = v.tensor_scalar(stage[:, 1024 * j:1024 * (j + 1)], src_tiles(j),
                                  negm[:, j:j + 1], r[:, j:j + 1], ALU.add, ALU.mult)
            c.s_ve.inc(ins)

    @block.tensor
    def _(t):
        gi = 0
        for j in range(NT):
            t.wait_ge(c.s_ve.h, vb + j + 1)
            for g in range(2):
                b = 6 + gi % 2
                if gi >= 2:
                    t.wait_ge(c.s_ve.h, vb + 16 + (gi - 1))
                for kk in range(4):
                    k = 4 * g + kk
                    ins = t.transpose(c.PSb[:, 1024 * b + 128 * kk:1024 * b + 128 * (kk + 1)],
                                      stage[:, 1024 * j + 128 * k: 1024 * j + 128 * (k + 1)],
                                      c.IDb[:])
                c.s_te.inc(ins)
                gi += 1

    @block.vector
    def _(v):
        gi = 0
        vw = dst_fm.rearrange("p (k t) -> p k t", t=T)
        for j in range(NT):
            for g in range(2):
                b = 6 + gi % 2
                v.wait_ge(c.s_te.h, tb + gi + 1)
                dst = vw[:, 4 * g:4 * g + 4, 128 * j:128 * (j + 1)]
                c.s_ve.inc(v.tensor_copy(dst, c.PSb[:, 1024 * b:1024 * b + 512]
                                         .rearrange("p (a t) -> p a t", a=4)))
                gi += 1


def stream_w_bf16(c, block, dram_ap, n_chunks, ws_off, wc=1024, gate0=None, dst=None):
    nc = c.nc
    vb = c.s_ve.n
    wsem = [c.s_wt, c.s_wu]
    dvals = []

    @block.sync
    def _(sy):
        for k in range(n_chunks):
            if k >= 2:
                sy.wait_ge(c.s_ve.h, vb + (k - 1))
            else:
                gate_wf(sy, c)
            dma_ser(sy, wsem[k % 2], c.WF[:, 1024 * (k % 2):1024 * (k % 2) + wc],
                    dram_ap[128 * k:128 * (k + 1), :])
            dvals.append((wsem[k % 2], wsem[k % 2].n))

    ready = []

    @block.vector
    def _(v):
        for k in range(n_chunks):
            v.wait_ge(dvals[k][0].h, dvals[k][1])
            if k == 0 and gate0 is not None:
                v.wait_ge(gate0[0].h, gate0[1])
            dd = dst if dst is not None else c.WS
            ins = v.tensor_copy(dd[:, ws_off + wc * k: ws_off + wc * (k + 1)],
                                c.WF[:, 1024 * (k % 2):1024 * (k % 2) + wc])
            c.s_ve.inc(ins)
            ready.append((c.s_ve, c.s_ve.n))
    return ready


def load_bias_fm(c, block, dram_ap, col):
    @block.gpsimd
    def _(g):
        vw = dram_ap.rearrange("(k p) -> p k", p=128)
        dma_ser(g, c.s_ld, c.BSM[:, 8 * col:8 * col + 8], vw,
                allow_slow_non_contiguous=True)


def load_bias_row(c, block, dram_ap, slot=0, length=D, make_bf=True, gates=()):
    vb = c.s_ve.n

    tb = c.s_te.n

    @block.gpsimd
    def _(g):
        if vb:
            g.wait_ge(c.s_ve.h, vb)
        if tb:
            g.wait_ge(c.s_te.h, tb)
        for sm, val in gates:
            if val:
                g.wait_ge(sm.h, val)
        src = dram_ap.rearrange("(a d) -> a d", a=1) if len(dram_ap.shape) == 1 else dram_ap
        dma_ser(g, c.s_ld, c.RB[0:1, 0:length], src)

    if make_bf:
        @block.vector
        def _(v):
            c.s_ld.wait(v)
            for sm, val in gates:
                if val:
                    v.wait_ge(sm.h, val)
            c.s_ve.inc(v.tensor_copy(c.RBb[0:1, 0:length], c.RB[0:1, 0:length]))



def emit_proj_fm(c, block, dst_fm, dst_chunk0, ws_off, wt_ready, src_chunk, e_chunks,
                 bias_col, wslice=128):
    nc = c.nc
    tb = c.s_te.n
    vb = c.s_ve.n
    wtot = 128 * len(e_chunks)

    @block.tensor
    def _(t):
        if vb:
            t.wait_ge(c.s_ve.h, vb)
        for jj, j in enumerate(e_chunks):
            b0 = 0 if jj % 2 == 0 else 4
            if jj >= 2:
                t.wait_ge(c.s_ve.h, vb + jj - 1)
            for k in range(NKC):
                if wt_ready is not None and jj == 0:
                    t.wait_ge(wt_ready[k][0].h, wt_ready[k][1])
                lhsT = c.WS[:, ws_off + wtot * k + 128 * jj: ws_off + wtot * k + 128 * (jj + 1)]
                for w in range(4):
                    t.matmul(bank(c, b0 + w)[:, 0:512], lhsT,
                             src_chunk(k, 512 * w, 512 * (w + 1)),
                             start=(k == 0), stop=False)
            for w in range(4):
                ins = t.matmul(bank(c, b0 + w)[:, 0:512], c.ZSB[0:1, 0:128],
                               c.RBb[0:1, 0:512], start=False, stop=True)
            c.s_te.inc(ins)

    ldn = c.s_ld.n

    @block.vector
    def _(v):
        if bias_col is not None and ldn:
            v.wait_ge(c.s_ld.h, ldn)
        vw = dst_fm.rearrange("p (k t) -> p k t", t=T)
        for jj, j in enumerate(e_chunks):
            b0 = 0 if jj % 2 == 0 else 4
            v.wait_ge(c.s_te.h, tb + jj + 1)
            pv = c.PS[:, 512 * b0:512 * b0 + 2048]
            dst = vw[:, dst_chunk0 + jj, :]
            if bias_col is not None:
                ins = v.tensor_scalar(dst, pv, c.BSM[:, 8 * bias_col + j:8 * bias_col + j + 1],
                                      None, ALU.add)
            else:
                ins = v.tensor_copy(dst, pv)
            c.s_ve.inc(ins)


def emit_vprime_proj(c, block, dst, nheads, ws_off, wt_ready, src_chunk, wc,
                     bias_slot, bias_off, zb=0):
    nc = c.nc
    tb = c.s_te.n
    vb = c.s_ve.n
    CW = 65 * nheads

    @block.tensor
    def _(t):
        if vb:
            t.wait_ge(c.s_ve.h, vb)
        for j in range(NT):
            b = zb + (j % 2)
            if j >= 2:
                t.wait_ge(c.s_ve.h, vb + j - 1)
            for k in range(NKC):
                if wt_ready is not None and j == 0:
                    t.wait_ge(wt_ready[k][0].h, wt_ready[k][1])
                t.matmul(bank(c, b)[:, 0:wc], src_chunk(k, 128 * j, 128 * (j + 1)),
                         c.WS[:, ws_off + wc * k: ws_off + wc * (k + 1)],
                         start=(k == 0), stop=False)
            ins = t.matmul(bank(c, b)[:, 0:wc], c.ONESb[:, 0:128],
                           c.RBb[0:1, bias_off:bias_off + wc], start=False, stop=True)
            c.s_te.inc(ins)

    @block.vector
    def _(v):
        dv = dst.rearrange("p (j h e) -> p j h e", h=nheads, e=65)
        for j in range(NT):
            b = zb + (j % 2)
            v.wait_ge(c.s_te.h, tb + j + 1)
            v.tensor_copy(dv[:, j, :, 0:64],
                          bank(c, b)[:, 0:wc].rearrange("p (h e) -> p h e", e=64))
            c.s_ve.inc(v.memset(dv[:, j, :, 64:65], 1.0))



def emit_attention_quarter(c, block, q, scale, attn_dst):
    nc = c.nc
    NKT = 16

    for jj in range(2):
        cglob = 2 * q + jj
        tb = c.s_te.n
        vb = c.s_ve.n
        sb = c.s_se.n

        @block.tensor
        def _(t):
            if vb:
                t.wait_ge(c.s_ve.h, vb)
            for w in range(4):
                for kt in range(NKT):
                    sc = 2 * (kt % 2)
                    if not (w == 0 and kt < 2):
                        t.wait_ge(c.s_se.h, sb + 16 * w + kt - 1)
                    if w > 0 and kt == 0:
                        t.wait_ge(c.s_ve.h, vb + 3 * w)
                    lA = c.A3[0 + 64 * 0:64, 4096 + 2048 * jj + 128 * kt: 4096 + 2048 * jj + 128 * (kt + 1)]
                    lB = c.A3[64:128, 4096 + 2048 * jj + 128 * kt: 4096 + 2048 * jj + 128 * (kt + 1)]
                    rA = c.A3[0:64, 2048 * jj + 512 * w: 2048 * jj + 512 * (w + 1)]
                    rB = c.A3[64:128, 2048 * jj + 512 * w: 2048 * jj + 512 * (w + 1)]
                    t.matmul(bank(c, sc)[:, 0:512], lA, rA, start=True, stop=True)
                    ins = t.matmul(bank(c, sc + 1)[:, 0:512], lB, rB, start=True, stop=True)
                    c.s_te.inc(ins)
                    t.wait_ge(c.s_se.h, sb + 16 * w + kt + 1)
                    vA = c.A3[:, 8192 + 260 * kt + 65 * (2 * jj): 8192 + 260 * kt + 65 * (2 * jj) + 65]
                    vB = c.A3[:, 8192 + 260 * kt + 65 * (2 * jj + 1): 8192 + 260 * kt + 65 * (2 * jj + 1) + 65]
                    pA = c.PT[:, 1024 * (kt % 2): 1024 * (kt % 2) + 512]
                    pB = c.PT[:, 1024 * (kt % 2) + 512: 1024 * (kt % 2) + 1024]
                    t.matmul(bank(c, 4)[0:65, 0:512], vA, pA, start=(kt == 0), stop=False)
                    ins = t.matmul(bank(c, 5)[0:65, 0:512], vB, pB, start=(kt == 0), stop=False)
                    if kt < NKT - 1:
                        c.s_te.inc(ins)
                t.matmul(bank(c, 4)[0:65, 0:512], c.ZSB[0:1, 0:65],
                         c.RBb[0:1, 0:512], start=False, stop=True)
                ins = t.matmul(bank(c, 5)[0:65, 0:512], c.ZSB[0:1, 0:65],
                               c.RBb[0:1, 0:512], start=False, stop=True)
                c.s_te.inc(ins)
                t.wait_ge(c.s_ve.h, vb + 3 * w + 1)
                rr = c.EPI[0:1, 1024:2048]
                t.matmul(bank(c, 6)[0:64, 0:512], c.ONES32[:, 0:64], rr[:, 0:512],
                         start=True, stop=True)
                ins = t.matmul(bank(c, 7)[0:64, 0:512], c.ONES32[:, 0:64], rr[:, 512:1024],
                               start=True, stop=True)
                c.s_te.inc(ins)

        @block.scalar
        def _(s):
            for w in range(4):
                for kt in range(NKT):
                    s.wait_ge(c.s_te.h, tb + 33 * w + 2 * kt + 1)
                    ins = s.activation(c.PT[:, 1024 * (kt % 2):1024 * (kt % 2 + 1)],
                                       c.PS[:, 512 * (2 * (kt % 2)): 512 * (2 * (kt % 2)) + 1024],
                                       AF.Exp, scale=scale)
                    c.s_se.inc(ins)

        gb = c.s_ge.n

        @block.vector
        def _(v):
            for w in range(4):
                v.wait_ge(c.s_te.h, tb + 33 * w + 32)
                rr = c.EPI[0:1, 1024:2048]
                c.s_ve.inc(v.reciprocal(rr, c.PS[64:65, 512 * 4:512 * 4 + 1024]))
                v.wait_ge(c.s_te.h, tb + 33 * w + 33)
                bc = c.EPI[0:64, 1024:2048]
                c.s_ve.inc(v.tensor_copy(bc, c.PS[0:64, 512 * 6:512 * 6 + 1024]))
                if gb + w > 0:
                    v.wait_ge(c.s_ge.h, gb + w)
                c.s_ve.inc(v.tensor_mul(c.AMN[0:64, 0:1024], c.PS[0:64, 512 * 4:512 * 4 + 1024], bc))

        @block.gpsimd
        def _(g):
            for w in range(4):
                g.wait_ge(c.s_ve.h, vb + 3 * w + 3)
                dstA = attn_dst[0:64, 2048 * cglob + 512 * w: 2048 * cglob + 512 * (w + 1)]
                dstB = attn_dst[64:128, 2048 * cglob + 512 * w: 2048 * cglob + 512 * (w + 1)]
                dma_ser(g, c.s_st, dstA, c.AMN[0:64, 0:512])
                dma_ser(g, c.s_st, dstB, c.AMN[0:64, 512:1024])
                g.wait_ge(c.s_st.h, c.s_st.n)
                c.s_ge.inc(g.nop())


def emit_oproj_sandwich(c, block, attn_src, wt_ready_o, bias_row_off, sums_dst, gate=None):
    nc = c.nc
    tb = c.s_te.n
    vb = c.s_ve.n
    sb = c.s_se.n
    asrc = attn_src.rearrange("p (k t) -> p k t", t=T)
    negm2 = c.STATS[:, 96:97]
    r2 = c.STATS[:, 97:98]
    v1t = c.STATS[:, 98:99]
    v2t = c.STATS[:, 99:100]
    ssq = c.STATS[:, 100:101]
    ssm = c.STATS[:, 101:102]
    junk = c.EPI[:, 0:1024]
    zn = c.EPI[:, 1024:2048]

    @block.tensor
    def _(t):
        if gate is not None:
            t.wait_ge(gate[0].h, gate[1])
        if vb:
            t.wait_ge(c.s_ve.h, vb)
        for j in range(NT):
            b0 = 0 if j % 2 == 0 else 2
            if j >= 2:
                t.wait_ge(c.s_ve.h, vb + 4 * (j - 1))
            for k in range(NKC):
                if wt_ready_o is not None and j == 0:
                    t.wait_ge(wt_ready_o[k][0].h, wt_ready_o[k][1])
                lhsT = asrc[:, k, 128 * j:128 * (j + 1)]
                t.matmul(bank(c, b0)[:, 0:512], lhsT, c.A3[:, 1024 * k:1024 * k + 512],
                         start=(k == 0), stop=False)
                t.matmul(bank(c, b0 + 1)[:, 0:512], lhsT, c.A3[:, 1024 * k + 512:1024 * (k + 1)],
                         start=(k == 0), stop=False)
            t.matmul(bank(c, b0)[:, 0:512], c.ONESb[:, 0:128],
                     c.RBb[0:1, bias_row_off:bias_row_off + 512], start=False, stop=True)
            ins = t.matmul(bank(c, b0 + 1)[:, 0:512], c.ONESb[:, 0:128],
                           c.RBb[0:1, bias_row_off + 512:bias_row_off + 1024],
                           start=False, stop=True)
            c.s_te.inc(ins)

    @block.vector
    def _(v):
        for j in range(NT):
            b0 = 0 if j % 2 == 0 else 2
            pz = c.PS[:, 512 * b0:512 * b0 + 1024]
            v.wait_ge(c.s_te.h, tb + j + 1)
            c.s_ve.inc(v.reduce_sum(ssm, pz, axis=AX))
            v.wait_ge(c.s_se.h, sb + 3 * j + 1)
            v.tensor_scalar(negm2, ssm, -1.0 / D, None, ALU.mult)
            v.tensor_mul(v2t, negm2, negm2)
            v.tensor_scalar(v1t, ssq, 1.0 / D, None, ALU.mult)
            c.s_ve.inc(v.tensor_sub(v1t, v1t, v2t))
            v.wait_ge(c.s_se.h, sb + 3 * j + 3)
            v.tensor_scalar(zn, pz, negm2, r2, ALU.add, ALU.mult)
            ins = v.scalar_tensor_tensor(
                xtile(c, j), zn, 1.0, xtile(c, j), ALU.mult, ALU.add,
                accum_out=(sums_dst[:, j:j + 1] if sums_dst is not None else None))
            c.s_ve.inc(ins)
            v.wait_ge(c.s_ve.h, c.s_ve.n)
            c.s_ve.inc(v.nop())

    @block.scalar
    def _(s):
        for j in range(NT):
            b0 = 0 if j % 2 == 0 else 2
            pz = c.PS[:, 512 * b0:512 * b0 + 1024]
            s.wait_ge(c.s_te.h, tb + j + 1)
            s.wait_ge(c.s_ve.h, vb + 4 * j + 1)
            c.s_se.inc(s.activation(junk, pz, AF.Square, accum_out=ssq))
            s.wait_ge(c.s_ve.h, vb + 4 * j + 2)
            c.s_se.inc(s.activation(v1t, v1t, AF.Ln, bias=c.EPSC[:]))
            c.s_se.inc(s.activation(r2, v1t, AF.Exp, scale=-0.5))



def stream_w_rot(c, block, dram_ap, n_chunks, ws_off, wc, gate_sem, gate_counts):
    nc = c.nc
    wsem = [c.s_wt, c.s_wu]
    dvals = []

    @block.sync
    def _(sy):
        for k in range(n_chunks):
            if k >= 2:
                sy.wait_ge(gate_sem.h, gate_counts(k - 2))
            else:
                gate_wf(sy, c)
            dma_ser(sy, wsem[k % 2], c.WF[:, 1024 * (k % 2):1024 * (k % 2) + wc],
                    dram_ap[128 * k:128 * (k + 1), :])
            dvals.append((wsem[k % 2], wsem[k % 2].n))

    ready = []

    @block.vector
    def _(v):
        for k in range(n_chunks):
            v.wait_ge(dvals[k][0].h, dvals[k][1])
            if k >= 2:
                v.wait_ge(gate_sem.h, gate_counts(k - 2))
            ins = v.tensor_copy(c.WS[:, ws_off + wc * (k % 2): ws_off + wc * (k % 2) + wc],
                                c.WF[:, 1024 * (k % 2):1024 * (k % 2) + wc])
            c.s_ve.inc(ins)
            ready.append((c.s_ve, c.s_ve.n))
    return ready


def stream_masked_w(c, block, dram_w, dram_m, n_chunks, ws_off, wc, gate0=None, dst=None):
    nc = c.nc
    gb = c.s_ge.n
    dvals = []
    assert wc <= 1024

    @block.sync
    def _(sy):
        for k in range(n_chunks):
            if k >= 1:
                sy.wait_ge(c.s_ge.h, gb + k)
            else:
                gate_wf(sy, c)
            dma_ser(sy, c.s_wt, c.WF[:, 0:wc], dram_w[128 * k:128 * (k + 1), :])
            dma_ser(sy, c.s_wt, c.WF[:, 1024:1024 + wc], dram_m[128 * k:128 * (k + 1), :])
            dvals.append((c.s_wt, c.s_wt.n))

    ready = []

    @block.gpsimd
    def _(g):
        for k in range(n_chunks):
            g.wait_ge(dvals[k][0].h, dvals[k][1])
            if k == 0 and gate0 is not None:
                g.wait_ge(gate0[0].h, gate0[1])
            dd = dst if dst is not None else c.WS
            g.tensor_scalar(c.WF[:, 1024:1024 + wc], c.WF[:, 1024:1024 + wc],
                            0.0, None, ALU.is_gt)
            ins = g.tensor_tensor(dd[:, ws_off + wc * k: ws_off + wc * (k + 1)],
                                  c.WF[:, 1024:1024 + wc], c.WF[:, 0:wc], ALU.mult)
            c.s_ge.inc(ins)
            ready.append((c.s_ge, c.s_ge.n))
    return ready



def emit_l0ffn_slots(c, block):
    nc = c.nc
    SLo = c.SL[0:16, 0:1024]
    S1 = c.SL[0:16, 1024:2048]
    S2 = c.SL[0:16, 2048:3072]

    vbq = c.s_ve.n
    teq = c.s_te.n

    @block.gpsimd
    def _(g):
        dma_ser(g, c.s_ld, S2, c.dr["sq"])
        dma_ser(g, c.s_ld, S1, c.dr["d_bo"].rearrange("(a d) -> a d", a=1).broadcast_to([16, D]))
        if vbq:
            g.wait_ge(c.s_ve.h, vbq)
        if teq:
            g.wait_ge(c.s_te.h, teq)
        dma_ser(g, c.s_ld, c.RB[0:1, 0:1024], c.dr["d_bq"].rearrange("(a d) -> a d", a=1))

    ldb = c.s_ld.n
    vb0 = c.s_ve.n

    @block.vector
    def _(v):
        v.wait_ge(c.s_ld.h, ldb)
        c.s_ve.inc(v.tensor_add(S1, S1, S2))

    tb0 = c.s_te.n

    @block.tensor
    def _(t):
        t.wait_ge(c.s_ld.h, ldb)
        for k in range(NKC):
            ins = t.transpose(bank(c, 6)[:, 16 * k:16 * (k + 1)],
                              S2[:, 128 * k:128 * (k + 1)], c.ID32[0:16, 0:16])
        c.s_te.inc(ins)

    @block.vector
    def _(v):
        v.wait_ge(c.s_te.h, tb0 + 1)
        c.s_ve.inc(v.tensor_copy(c.HT[:, 0:128], bank(c, 6)[:, 0:128]))

    vb1 = c.s_ve.n
    tb1 = c.s_te.n
    wsem = [c.s_wt, c.s_wu]
    dv1 = []

    @block.sync
    def _(sy):
        for k in range(NKC):
            if k >= 2:
                sy.wait_ge(c.s_te.h, tb1 + (k - 1))
            else:
                gate_wf(sy, c)
            dma_ser(sy, wsem[k % 2], c.WF[:, 1024 * (k % 2):1024 * (k % 2 + 1)],
                    c.dr["d_Wq"][128 * k:128 * (k + 1), :])
            dv1.append((wsem[k % 2], wsem[k % 2].n))

    @block.tensor
    def _(t):
        t.wait_ge(c.s_ve.h, vb1)
        for k in range(NKC):
            t.wait_ge(dv1[k][0].h, dv1[k][1])
            lhsT = c.HT[:, 16 * k:16 * (k + 1)]
            t.matmul(bank(c, 4)[0:16, 0:512], lhsT, c.WF[:, 1024 * (k % 2):1024 * (k % 2) + 512],
                     start=(k == 0), stop=False)
            ins = t.matmul(bank(c, 5)[0:16, 0:512], lhsT,
                           c.WF[:, 1024 * (k % 2) + 512:1024 * (k % 2 + 1)],
                           start=(k == 0), stop=False)
            c.s_te.inc(ins)
        t.wait_ge(c.s_ld.h, ldb)
        t.matmul(bank(c, 4)[0:16, 0:512], c.ONES32[:, 0:16], c.RB[0:1, 0:512],
                 start=False, stop=True)
        ins = t.matmul(bank(c, 5)[0:16, 0:512], c.ONES32[:, 0:16], c.RB[0:1, 512:1024],
                       start=False, stop=True)
        c.s_te.inc(ins)

    tb2 = c.s_te.n

    @block.vector
    def _(v):
        v.wait_ge(c.s_te.h, tb2)
        c.s_ve.inc(v.tensor_copy(S2, c.PS[0:16, 512 * 4:512 * 4 + 1024]))

    vb2 = c.s_ve.n
    tb3 = c.s_te.n

    @block.tensor
    def _(t):
        t.wait_ge(c.s_ve.h, vb2)
        for k in range(NKC):
            ins = t.transpose(bank(c, 6)[:, 16 * k:16 * (k + 1)],
                              S2[:, 128 * k:128 * (k + 1)], c.ID32[0:16, 0:16])
        c.s_te.inc(ins)

    @block.vector
    def _(v):
        v.wait_ge(c.s_te.h, tb3 + 1)
        c.s_ve.inc(v.tensor_copy(c.QSL[:, 0:128], bank(c, 6)[:, 0:128]))

    load_bias_row(c, block, c.dr["d_bv"], gates=((c.s_te, c.s_te.n),))
    load_bias_fm(c, block, c.dr["d_bk"], 2)

    for q in range(4):
        wr_k = stream_w_bf16(c, block, c.dr["d_Wk"][:, 256 * q:256 * (q + 1)], NKC, 0, wc=256)
        wr_v = stream_w_bf16(c, block, c.dr["d_Wv"][:, 256 * q:256 * (q + 1)], NKC, 2048, wc=256)
        emit_proj_fm(c, block, c.A3[:, 4096:8192], 0, 0, wr_k,
                     lambda k, lo, hi: c.XN[:, 2048 * k + lo:2048 * k + hi],
                     [2 * q, 2 * q + 1], bias_col=2)
        emit_vprime_proj(c, block, c.A3[:, 8192:12352], 4, 2048, wr_v,
                         lambda k, lo, hi: c.XN[:, 2048 * k + lo:2048 * k + hi],
                         256, None, 256 * q, zb=0)
        for jj in range(2):
            tb = c.s_te.n
            sb = c.s_se.n
            vb = c.s_ve.n
            ch = 2 * q + jj

            @block.tensor
            def _(t, jj=jj, ch=ch, tb=tb, sb=sb, vb=vb):
                t.wait_ge(c.s_ve.h, vb)
                for kt in range(NT):
                    lA = c.A3[0:64, 4096 + 2048 * jj + 128 * kt: 4096 + 2048 * jj + 128 * (kt + 1)]
                    lB = c.A3[64:128, 4096 + 2048 * jj + 128 * kt: 4096 + 2048 * jj + 128 * (kt + 1)]
                    rA = c.QSL[0:64, 16 * ch:16 * (ch + 1)]
                    rB = c.QSL[64:128, 16 * ch:16 * (ch + 1)]
                    t.matmul(bank(c, 0)[:, 16 * kt:16 * (kt + 1)], lA, rA, start=True, stop=True)
                    ins = t.matmul(bank(c, 1)[:, 16 * kt:16 * (kt + 1)], lB, rB, start=True, stop=True)
                c.s_te.inc(ins)
                t.wait_ge(c.s_se.h, sb + 1)
                for kt in range(NT):
                    for hh in range(2):
                        vsl = c.A3[:, 8192 + 260 * kt + 65 * (2 * jj + hh):
                                   8192 + 260 * kt + 65 * (2 * jj + hh) + 65]
                        psl = c.PT[:, 256 * hh + 16 * kt: 256 * hh + 16 * (kt + 1)]
                        t.matmul(bank(c, 2 + hh)[0:16, 0:65], psl, vsl,
                                 start=(kt == 0), stop=False)
                for hh in range(2):
                    ins = t.matmul(bank(c, 2 + hh)[0:16, 0:65],
                                   c.ZSB[0:1, 0:16], c.RBb[0:1, 0:65],
                                   start=False, stop=True)
                c.s_te.inc(ins)

            @block.scalar
            def _(s, tb=tb):
                s.wait_ge(c.s_te.h, tb + 1)
                s.activation(c.PT[:, 0:256], bank(c, 0)[:, 0:256], AF.Exp, scale=0.125)
                c.s_se.inc(s.activation(c.PT[:, 256:512], bank(c, 1)[:, 0:256], AF.Exp, scale=0.125))

            @block.vector
            def _(v, jj=jj, ch=ch, tb=tb):
                v.wait_ge(c.s_te.h, tb + 2)
                rc = c.FSM[0:16, 86:88]
                v.reciprocal(rc, c.PS[0:16, 1088:1601:512])
                for hh in range(2):
                    h = 2 * ch + hh
                    ins = v.tensor_scalar(
                        c.SL[0:16, 2048 + 64 * h:2048 + 64 * (h + 1)],
                        bank(c, 2 + hh)[0:16, 0:64],
                        rc[:, hh:hh + 1], None, ALU.mult)
                c.s_ve.inc(ins)

    vb3 = c.s_ve.n
    tb4 = c.s_te.n

    @block.tensor
    def _(t):
        t.wait_ge(c.s_ve.h, vb3)
        for k in range(NKC):
            ins = t.transpose(bank(c, 6)[:, 16 * k:16 * (k + 1)],
                              c.SL[0:16, 2048 + 128 * k:2048 + 128 * (k + 1)], c.ID32[0:16, 0:16])
        c.s_te.inc(ins)

    @block.vector
    def _(v):
        v.wait_ge(c.s_te.h, tb4 + 1)
        c.s_ve.inc(v.tensor_copy(c.HT[:, 0:128], bank(c, 6)[:, 0:128]))

    vb4 = c.s_ve.n
    tb5 = c.s_te.n
    dv2 = []

    @block.sync
    def _(sy):
        for k in range(NKC):
            if k >= 2:
                sy.wait_ge(c.s_te.h, tb5 + (k - 1))
            else:
                gate_wf(sy, c)
            dma_ser(sy, wsem[k % 2], c.WF[:, 1024 * (k % 2):1024 * (k % 2 + 1)],
                    c.dr["d_Wo"][128 * k:128 * (k + 1), :])
            dv2.append((wsem[k % 2], wsem[k % 2].n))

    @block.tensor
    def _(t):
        t.wait_ge(c.s_ve.h, vb4)
        for k in range(NKC):
            t.wait_ge(dv2[k][0].h, dv2[k][1])
            lhsT = c.HT[:, 16 * k:16 * (k + 1)]
            t.matmul(bank(c, 4)[0:16, 0:512], lhsT, c.WF[:, 1024 * (k % 2):1024 * (k % 2) + 512],
                     start=(k == 0), stop=False)
            ins = t.matmul(bank(c, 5)[0:16, 0:512], lhsT,
                           c.WF[:, 1024 * (k % 2) + 512:1024 * (k % 2 + 1)],
                           start=(k == 0), stop=False)
            if k < NKC - 1:
                c.s_te.inc(ins)
        t.matmul(bank(c, 4)[0:16, 0:512], c.ZS32[0:1, 0:16], c.RB[0:1, 0:512],
                 start=False, stop=True)
        ins = t.matmul(bank(c, 5)[0:16, 0:512], c.ZS32[0:1, 0:16], c.RB[0:1, 0:512],
                       start=False, stop=True)
        c.s_te.inc(ins)

    tb6 = c.s_te.n

    @block.vector
    def _(v):
        v.wait_ge(c.s_te.h, tb6)
        c.s_ve.inc(v.tensor_add(c.SL[0:16, 0:1024], c.SL[0:16, 1024:2048],
                                c.PS[0:16, 512 * 4:512 * 4 + 1024]))


def emit_l0ffn_experts(c, block):
    nc = c.nc
    UG = c.A3[:].bitcast(F32)[:, 0:1536]
    SLo = c.SL[0:16, 0:1024]
    S1 = c.SL[0:16, 1024:2048]
    S2 = c.SL[0:16, 2048:3072]
    S3 = c.EPI[0:16, 1024:2048]
    vb0 = c.s_ve.n
    gb0 = c.s_ge.n

    @block.gpsimd
    def _(g):
        g.wait_ge(c.s_ve.h, vb0)
        dma_ser(g, c.s_ld, c.ag1_in.ap(), SLo)
        g.wait_ge(c.s_ld.h, c.s_ld.n)
        from concourse.aot_env import allow_ici_all_to_all
        with allow_ici_all_to_all():
            c.s_cc.inc(g.collective_compute(
                "AllToAll", ALU.bypass, replica_groups=[list(range(NCORES))],
                ins=[c.ag1_in.ap().opt()], outs=[c.ag1_out.ap().opt()]))
        g.wait_ge(c.s_cc.h, c.s_cc.n)
        dma_ser(g, c.s_ld, S2, c.ag1_out.ap()[0:16, :])
        dma_ser(g, c.s_ld, S1, c.dr["e_ln_g"].rearrange("(a e) d -> a e d", a=1).broadcast_to([8, 2, D]))
        dma_ser(g, c.s_ld, S3, c.dr["e_ln_b"].rearrange("(a e) d -> a e d", a=1).broadcast_to([8, 2, D]))
        g.wait_ge(c.s_ld.h, c.s_ld.n)
        c.s_ge.inc(g.nop())

    ge1 = c.s_ge.n
    sb0 = c.s_se.n
    essq = c.FSM[0:16, 80:81]
    esum = c.FSM[0:16, 81:82]
    enegm = c.FSM[0:16, 82:83]
    evar = c.FSM[0:16, 83:84]
    er = c.FSM[0:16, 84:85]
    et = c.FSM[0:16, 85:86]

    @block.scalar
    def _(s):
        s.wait_ge(c.s_ge.h, ge1)
        c.s_se.inc(s.activation(c.EPI[0:16, 0:1024], S2, AF.Square, accum_out=essq))

    @block.vector
    def _(v):
        v.wait_ge(c.s_ge.h, ge1)
        v.reduce_sum(esum, S2, axis=AX)
        v.wait_ge(c.s_se.h, sb0 + 1)
        v.tensor_scalar(enegm, esum, -1.0 / D, None, ALU.mult)
        v.tensor_mul(et, enegm, enegm)
        v.tensor_scalar(evar, essq, 1.0 / D, None, ALU.mult)
        c.s_ve.inc(v.tensor_sub(evar, evar, et))

    sb1 = c.s_se.n
    vb1 = c.s_ve.n

    @block.scalar
    def _(s):
        s.wait_ge(c.s_ve.h, vb1)
        s.activation(evar, evar, AF.Ln, bias=c.EPSC[0:16, :])
        c.s_se.inc(s.activation(er, evar, AF.Exp, scale=-0.5))

    @block.vector
    def _(v):
        v.wait_ge(c.s_se.h, sb1 + 1)
        v.tensor_scalar(S2, S2, enegm, er, ALU.add, ALU.mult)
        v.tensor_mul(S2, S2, S1)
        c.s_ve.inc(v.tensor_add(S2, S2, S3))

    vb2 = c.s_ve.n
    tb0 = c.s_te.n

    @block.tensor
    def _(t):
        t.wait_ge(c.s_ve.h, vb2)
        for k in range(NKC):
            ins = t.transpose(bank(c, 6)[:, 16 * k:16 * (k + 1)],
                              S2[:, 128 * k:128 * (k + 1)], c.ID32[0:16, 0:16])
        c.s_te.inc(ins)

    @block.vector
    def _(v):
        v.wait_ge(c.s_te.h, tb0 + 1)
        c.s_ve.inc(v.tensor_copy(c.HT[:, 0:128], bank(c, 6)[:, 0:128]))

    vbH = c.s_ve.n

    mats = [("eW1", "eb1", 0, 0), ("eWg", "ebg", 0, 512), ("eW1", "eb1", 1, 1024), ("eWg", "ebg", 1, 1536)]
    tbU = c.s_te.n
    wsem = [c.s_wt, c.s_wu]
    dvU = []
    ndma = [0]

    @block.sync
    def _(sy):
        for (wname, bname, e, off) in mats:
            for k in range(NKC):
                for g4 in range(4):
                    if ndma[0] >= 2:
                        sy.wait_ge(c.s_te.h, tbU + (ndma[0] - 1))
                    else:
                        gate_wf(sy, c)
                    dma_ser(sy, wsem[ndma[0] % 2],
                            c.WF[:, 1024 * (ndma[0] % 2):1024 * (ndma[0] % 2 + 1)],
                            c.dr[wname][e, 128 * k:128 * (k + 1), 1024 * g4:1024 * (g4 + 1)])
                    dvU.append((wsem[ndma[0] % 2], wsem[ndma[0] % 2].n))
                    ndma[0] += 1

    nmm = [0]

    @block.tensor
    def _(t):
        t.wait_ge(c.s_ve.h, vbH)
        pall = c.PS[:, 0:2048]
        for (wname, bname, e, off) in mats:
            for k in range(NKC):
                for g4 in range(4):
                    t.wait_ge(dvU[nmm[0]][0].h, dvU[nmm[0]][1])
                    rhs = c.HT[:, 16 * k + e:16 * k + 16:2]
                    for ft in range(8):
                        fcol = off + 8 * (8 * g4 + ft)
                        ins = t.matmul(pall[:, fcol:fcol + 8],
                                       c.WF[:, 1024 * (nmm[0] % 2) + 128 * ft:
                                            1024 * (nmm[0] % 2) + 128 * (ft + 1)],
                                       rhs, start=(k == 0 and g4 == 0 and ft == 0),
                                       stop=False)
                    nmm[0] += 1
                    c.s_te.inc(ins)

    tbB = c.s_te.n
    ldB = c.s_ld.n

    @block.gpsimd
    def _(g):
        nq = 0
        for i, (wname, bname, e, off) in enumerate(mats):
            for qtr in range(4):
                if nq:
                    g.wait_ge(c.s_te.h, tbB + nq)
                dma_ser(g, c.s_ld, c.RB[0:1, 0:1024],
                        c.dr[bname][e, 1024 * qtr:1024 * (qtr + 1)].rearrange("(a d) -> a d", a=1))
                nq += 1

    @block.tensor
    def _(t):
        nb = 0
        pall = c.PS[:, 0:2048]
        for i, (wname, bname, e, off) in enumerate(mats):
            for qtr in range(4):
                t.wait_ge(c.s_ld.h, ldB + 16 * (nb + 1))
                for fi in range(8):
                    ft = 8 * qtr + fi
                    fcol = off + 8 * ft
                    ins = t.matmul(pall[:, fcol:fcol + 8],
                                   c.RB[0:1, 128 * fi:128 * (fi + 1)],
                                   c.ONES32[0:1, 0:8], start=False,
                                   stop=(qtr == 3 and fi == 7))
                nb += 1
                c.s_te.inc(ins)

    tb1 = c.s_te.n
    sb2 = c.s_se.n

    @block.scalar
    def _(s):
        s.wait_ge(c.s_te.h, tb1)
        for b4 in range(4):
            ins = s.activation(c.EPI[:, 256 * b4:256 * (b4 + 1)],
                               c.PS[:, 512 * b4:512 * b4 + 256], AF.Exp, scale=-1.0)
        c.s_se.inc(ins)

    vb3 = c.s_ve.n

    @block.vector
    def _(v):
        v.wait_ge(c.s_se.h, sb2 + 1)
        v.tensor_scalar(c.EPI[:, 1024:2048], c.EPI[:, 0:1024], 1.0, None, ALU.add)
        v.reciprocal(c.EPI[:, 0:1024], c.EPI[:, 1024:2048])
        pst = c.PS[:, 0:2048].rearrange("p (b x) -> p b x", x=512)[:, :, 0:256]
        v.tensor_mul(UG[:, 0:1024].rearrange("p (b x) -> p b x", x=256), pst,
                     c.EPI[:, 0:1024].rearrange("p (b x) -> p b x", x=256))
        v.tensor_mul(UG[:, 1024:1280], UG[:, 0:256], UG[:, 256:512])
        c.s_ve.inc(v.tensor_mul(UG[:, 1280:1536], UG[:, 512:768], UG[:, 768:1024]))

    vb4 = c.s_ve.n
    tb2 = c.s_te.n
    dv3 = []
    nd2 = [0]

    @block.sync
    def _(sy):
        for e in range(2):
            for ft in range(32):
                if nd2[0] >= 2:
                    sy.wait_ge(c.s_te.h, tb2 + (nd2[0] - 1))
                else:
                    gate_wf(sy, c)
                dma_ser(sy, wsem[nd2[0] % 2],
                        c.WF[:, 1024 * (nd2[0] % 2):1024 * (nd2[0] % 2 + 1)],
                        c.dr["eW2"][e, 128 * ft:128 * (ft + 1), :])
                dv3.append((wsem[nd2[0] % 2], wsem[nd2[0] % 2].n))
                nd2[0] += 1

    @block.tensor
    def _(t):
        t.wait_ge(c.s_ve.h, vb4)
        nm = 0
        for e in range(2):
            zb = 4 + 2 * e
            for ft in range(32):
                t.wait_ge(dv3[nm][0].h, dv3[nm][1])
                lhsT = UG[:, 1024 + 256 * e + 8 * ft: 1024 + 256 * e + 8 * (ft + 1)]
                t.matmul(bank(c, zb)[0:8, 0:512], lhsT,
                         c.WF[:, 1024 * (nm % 2):1024 * (nm % 2) + 512],
                         start=(ft == 0), stop=False)
                ins = t.matmul(bank(c, zb + 1)[0:8, 0:512], lhsT,
                               c.WF[:, 1024 * (nm % 2) + 512:1024 * (nm % 2 + 1)],
                               start=(ft == 0), stop=False)
                nm += 1
                if ft < 31:
                    c.s_te.inc(ins)
            t.matmul(bank(c, zb)[0:8, 0:512], c.ZS32[0:1, 0:8], c.RB[0:1, 0:512],
                     start=False, stop=True)
            ins = t.matmul(bank(c, zb + 1)[0:8, 0:512], c.ZS32[0:1, 0:8], c.RB[0:1, 0:512],
                           start=False, stop=True)
            c.s_te.inc(ins)

    tb3 = c.s_te.n

    @block.vector
    def _(v):
        v.wait_ge(c.s_te.h, tb3)
        v.tensor_copy(S1[0:8, :], c.PS[0:8, 512 * 4:512 * 4 + 1024])
        c.s_ve.inc(v.tensor_copy(S2[0:8, :], c.PS[0:8, 512 * 6:512 * 6 + 1024]))

    vb5 = c.s_ve.n
    gb1 = c.s_ge.n

    @block.gpsimd
    def _(g):
        g.wait_ge(c.s_ve.h, vb5)
        zi = c.agz_in.ap().rearrange("(b two) d -> b two d", two=2)
        dma_ser(g, c.s_ld, zi[:, 0, :], S1[0:8, :])
        dma_ser(g, c.s_ld, zi[:, 1, :], S2[0:8, :])
        g.wait_ge(c.s_ld.h, c.s_ld.n)
        from concourse.aot_env import allow_ici_all_to_all
        with allow_ici_all_to_all():
            c.s_cc.inc(g.collective_compute(
                "AllToAll", ALU.bypass, replica_groups=[list(range(NCORES))],
                ins=[c.agz_in.ap().opt()], outs=[c.agz_out.ap().opt()]))
        g.wait_ge(c.s_cc.h, c.s_cc.n)
        dma_ser(g, c.s_ld, S2, c.agz_out.ap()[0:16, :])
        dma_ser(g, c.s_ld, S1, c.dr["eb2"])
        g.wait_ge(c.s_ld.h, c.s_ld.n)
        c.s_ge.inc(g.nop())

    ge2 = c.s_ge.n

    @block.vector
    def _(v):
        v.wait_ge(c.s_ge.h, ge2)
        v.tensor_add(S2, S2, S1)
        c.s_ve.inc(v.tensor_add(SLo, SLo, S2))



def emit_l0ffn_cmha(c, block):
    nc = c.nc
    SLo = c.SL[0:16, 0:1024]
    S2 = c.SL[0:16, 2048:3072]
    vb0 = c.s_ve.n
    tb0 = c.s_te.n

    @block.tensor
    def _(t):
        t.wait_ge(c.s_ve.h, vb0)
        for k in range(NKC):
            ins = t.transpose(bank(c, 6)[:, 16 * k:16 * (k + 1)],
                              SLo[:, 128 * k:128 * (k + 1)], c.ID32[0:16, 0:16])
        c.s_te.inc(ins)

    @block.vector
    def _(v):
        v.wait_ge(c.s_te.h, tb0 + 1)
        c.s_ve.inc(v.tensor_copy(c.QSL[:, 128:256], bank(c, 6)[:, 0:128]))

    load_bias_row(c, block, c.dr["c_bk"], gates=((c.s_te, c.s_te.n),))
    vbT = c.s_ve.n

    def tm_out_slot_proj(wname, bias_rbb_off, zbank):
        tbp = c.s_te.n
        vcur = c.s_ve.n
        wr = stream_w_rot(c, block, c.dr[wname], NKC, 0, 1024,
                          c.s_te, lambda k: tbp + k + 1)

        @block.tensor
        def _(t):
            t.wait_ge(c.s_ve.h, vcur)
            for k in range(NKC):
                t.wait_ge(wr[k][0].h, wr[k][1])
                lhsT = c.QSL[:, 128 + 16 * k:128 + 16 * (k + 1)]
                t.matmul(bank(c, zbank)[0:16, 0:512], lhsT, c.WS[:, 1024 * (k % 2):1024 * (k % 2) + 512],
                         start=(k == 0), stop=False)
                ins = t.matmul(bank(c, zbank + 1)[0:16, 0:512], lhsT,
                               c.WS[:, 1024 * (k % 2) + 512:1024 * (k % 2 + 1)],
                               start=(k == 0), stop=False)
                c.s_te.inc(ins)
            t.matmul(bank(c, zbank)[0:16, 0:512], c.ONESb[0:1, 0:16],
                     c.RBb[0:1, bias_rbb_off:bias_rbb_off + 512], start=False, stop=True)
            ins = t.matmul(bank(c, zbank + 1)[0:16, 0:512], c.ONESb[0:1, 0:16],
                           c.RBb[0:1, bias_rbb_off + 512:bias_rbb_off + 1024],
                           start=False, stop=True)
            c.s_te.inc(ins)
        return c.s_te.n

    tK = tm_out_slot_proj("c_Wk", 0, 4)

    @block.vector
    def _(v):
        v.wait_ge(c.s_te.h, tK)
        c.s_ve.inc(v.tensor_copy(S2, c.PS[0:16, 512 * 4:512 * 4 + 1024]))

    vb1 = c.s_ve.n
    tb1 = c.s_te.n

    @block.tensor
    def _(t):
        t.wait_ge(c.s_ve.h, vb1)
        for k in range(NKC):
            ins = t.transpose(bank(c, 6)[:, 16 * k:16 * (k + 1)],
                              S2[:, 128 * k:128 * (k + 1)], c.ID32[0:16, 0:16])
        c.s_te.inc(ins)

    @block.vector
    def _(v):
        v.wait_ge(c.s_te.h, tb1 + 1)
        c.s_ve.inc(v.tensor_copy(c.QSL[:, 0:128], bank(c, 6)[:, 0:128]))

    load_bias_row(c, block, c.dr["c_bv"], gates=((c.s_te, c.s_te.n),))
    vbT2 = c.s_ve.n
    tV = tm_out_slot_proj("c_Wv", 0, 4)

    @block.vector
    def _(v):
        v.wait_ge(c.s_te.h, tV)
        dv = c.SL[:].bitcast(BF16)[0:16, 2048:3088].rearrange("p (h e) -> p h e", e=65)
        v.tensor_copy(dv[:, :, 0:64],
                      c.PS[0:16, 512 * 4:512 * 4 + 1024].rearrange("p (h e) -> p h e", e=64))
        c.s_ve.inc(v.memset(dv[:, :, 64:65], 1.0))

    load_bias_fm(c, block, c.dr["c_bq"], 3)
    for q in range(4):
        wr_q = stream_w_bf16(c, block, c.dr["c_Wq"][:, 256 * q:256 * (q + 1)], NKC, 0, wc=256,
                             gate0=(c.s_te, c.s_te.n))
        emit_proj_fm(c, block, c.A3[:, 0:4096], 0, 0, wr_q,
                     lambda k, lo, hi: c.XN[:, 2048 * k + lo:2048 * k + hi],
                     [2 * q, 2 * q + 1], bias_col=3, wslice=256)
        for jj in range(2):
            ch = 2 * q + jj
            tb = c.s_te.n
            sb = c.s_se.n
            vb = c.s_ve.n
            gb = c.s_ge.n

            @block.tensor
            def _(t, jj=jj, ch=ch, tb=tb, sb=sb, vb=vb):
                t.wait_ge(c.s_ve.h, vb)
                for w in range(4):
                    if w > 0:
                        t.wait_ge(c.s_ve.h, vb + 3 * w)
                    kA = c.QSL[0:64, 16 * ch:16 * (ch + 1)]
                    kB = c.QSL[64:128, 16 * ch:16 * (ch + 1)]
                    rA = c.A3[0:64, 2048 * jj + 512 * w:2048 * jj + 512 * (w + 1)]
                    rB = c.A3[64:128, 2048 * jj + 512 * w:2048 * jj + 512 * (w + 1)]
                    if w >= 1:
                        t.wait_ge(c.s_se.h, sb + w)
                    t.matmul(bank(c, 0)[0:16, 0:512], kA, rA, start=True, stop=True)
                    ins = t.matmul(bank(c, 1)[0:16, 0:512], kB, rB, start=True, stop=True)
                    c.s_te.inc(ins)
                    t.wait_ge(c.s_se.h, sb + w + 1)
                    for hh in range(2):
                        vsl = c.SL[:].bitcast(BF16)[0:16, 2048 + 65 * (2 * ch + hh):
                                                    2048 + 65 * (2 * ch + hh) + 65]
                        psl = c.PT[0:16, 512 * hh:512 * hh + 512]
                        ins = t.matmul(bank(c, 4 + hh)[0:65, 0:512], vsl, psl,
                                       start=True, stop=True)
                    c.s_te.inc(ins)
                    t.wait_ge(c.s_ve.h, vb + 3 * w + 1)
                    rr = c.EPI[0:1, 1024:2048]
                    t.matmul(bank(c, 6)[0:64, 0:512], c.ONES32[:, 0:64], rr[:, 0:512],
                             start=True, stop=True)
                    ins = t.matmul(bank(c, 7)[0:64, 0:512], c.ONES32[:, 0:64], rr[:, 512:1024],
                                   start=True, stop=True)
                    c.s_te.inc(ins)

            @block.scalar
            def _(s, tb=tb, sb=sb):
                for w in range(4):
                    s.wait_ge(c.s_te.h, tb + 3 * w + 1)
                    s.activation(c.PT[0:16, 0:512], bank(c, 0)[0:16, 0:512], AF.Exp, scale=0.125)
                    c.s_se.inc(s.activation(c.PT[0:16, 512:1024], bank(c, 1)[0:16, 0:512],
                                            AF.Exp, scale=0.125))

            @block.vector
            def _(v, ch=ch, tb=tb, vb=vb, gb=gb):
                for w in range(4):
                    v.wait_ge(c.s_te.h, tb + 3 * w + 2)
                    rr = c.EPI[0:1, 1024:2048]
                    c.s_ve.inc(v.reciprocal(rr, c.PS[64:65, 512 * 4:512 * 4 + 1024]))
                    v.wait_ge(c.s_te.h, tb + 3 * w + 3)
                    bc = c.EPI[0:64, 1024:2048]
                    c.s_ve.inc(v.tensor_copy(bc, c.PS[0:64, 512 * 6:512 * 6 + 1024]))
                    if gb + w > 0:
                        v.wait_ge(c.s_ge.h, gb + w)
                    c.s_ve.inc(v.tensor_mul(c.AMN[0:64, 0:1024],
                                            c.PS[0:64, 512 * 4:512 * 4 + 1024], bc))

            @block.gpsimd
            def _(g, ch=ch, vb=vb):
                for w in range(4):
                    g.wait_ge(c.s_ve.h, vb + 3 * w + 3)
                    dstA = c.A2[0:64, 2048 * ch + 512 * w:2048 * ch + 512 * (w + 1)]
                    dstB = c.A2[64:128, 2048 * ch + 512 * w:2048 * ch + 512 * (w + 1)]
                    dma_ser(g, c.s_st, dstA, c.AMN[0:64, 0:512])
                    dma_ser(g, c.s_st, dstB, c.AMN[0:64, 512:1024])
                    g.wait_ge(c.s_st.h, c.s_st.n)
                    c.s_ge.inc(g.nop())



def emit_l1ffn(c, block):
    nc = c.nc
    ldb = c.s_ld.n

    @block.gpsimd
    def _(g):
        dma_ser(g, c.s_ld, c.FSM[:, 0:16], c.dr["f_W1"])
        dma_ser(g, c.s_ld, c.FSM[:, 16:32], c.dr["f_b1"])
        dma_ser(g, c.s_ld, c.FSM[:, 32:48], c.dr["f_W2"])
        dma_ser(g, c.s_ld, c.FSM[:, 64:65], c.dr["f_bin"].rearrange("(d a) -> d a", a=1))
        dma_ser(g, c.s_ld, c.FSM[:, 65:66], c.dr["f_b2"].rearrange("(d a) -> d a", a=1))
        gate_wf(g, c)
        dma_ser(g, c.s_ld, c.WF[0:64, 0:1024], c.dr["f_Wout"])

    load_bias_row(c, block, c.dr["f_bout"], gates=((c.s_te, c.s_te.n),))

    @block.vector
    def _(v):
        v.wait_ge(c.s_ld.h, ldb + 96)
        v.scalar_tensor_tensor(c.FSM[:, 48:64], c.FSM[:, 0:16], c.FSM[:, 64:65],
                               c.FSM[:, 16:32], ALU.mult, ALU.add)
        c.s_ve.inc(v.tensor_copy(c.WS[0:64, 0:1024], c.WF[0:64, 0:1024]))

    vbW = c.s_ve.n

    @block.gpsimd
    def _(g):
        g.wait_ge(c.s_ve.h, vbW)
        dma_ser(g, c.s_ld, c.WS[64:65, 0:1024], c.RBb[0:1, 0:1024])

    ldW = c.s_ld.n

    wr_in = stream_w_bf16(c, block, c.dr["f_Win"], NKC, 1024, wc=64)

    tb0 = c.s_te.n
    vb0 = c.s_ve.n

    @block.tensor
    def _(t):
        t.wait_ge(c.s_ve.h, vb0)
        for k in range(NKC):
            t.wait_ge(wr_in[k][0].h, wr_in[k][1])
            lhsT = c.WS[:, 1024 + 64 * k:1024 + 64 * (k + 1)]
            for w in range(4):
                t.matmul(c.PS[0:64, 512 * w:512 * (w + 1)], lhsT,
                         c.XN[:, 2048 * k + 512 * w:2048 * k + 512 * (w + 1)],
                         start=(k == 0), stop=False)
        for w in range(4):
            ins = t.matmul(c.PS[0:64, 512 * w:512 * (w + 1)], c.ZSB[0:1, 0:64],
                           c.RBb[0:1, 0:512], start=False, stop=True)
        c.s_te.inc(ins)

    sb0 = c.s_se.n
    vb1 = c.s_ve.n

    @block.scalar
    def _(s):
        s.wait_ge(c.s_te.h, tb0 + 1)
        for i in range(16):
            if i >= 1:
                s.wait_ge(c.s_ve.h, vb1 + i)
            ins = s.activation(c.EPI[0:64, 0:2048], c.PS[0:64, 0:2048], AF.Relu,
                               scale=c.FSM[:, i:i + 1], bias=c.FSM[:, 48 + i:48 + i + 1])
            c.s_se.inc(ins)

    gA3 = c.A3[:].bitcast(F32)[0:64, 0:2048]
    GBB = c.A3[0:65, 4096:6144]

    @block.vector
    def _(v):
        for i in range(16):
            v.wait_ge(c.s_se.h, sb0 + i + 1)
            if i == 0:
                ins = v.tensor_scalar(gA3, c.EPI[0:64, 0:2048],
                                      c.FSM[:, 32:33], None, ALU.mult)
            else:
                ins = v.scalar_tensor_tensor(gA3, c.EPI[0:64, 0:2048],
                                             c.FSM[:, 32 + i:32 + i + 1],
                                             gA3, ALU.mult, ALU.add)
            c.s_ve.inc(ins)
        v.tensor_scalar(GBB[0:64, :], gA3, c.FSM[:, 65:66], None, ALU.add)
        c.s_ve.inc(v.memset(GBB[64:65, :], 1.0))

    vbG = c.s_ve.n

    GBB2 = c.A3[0:65, 4096:6144]

    def tile_mms(t, j, b0):
        if j == 0:
            t.wait_ge(c.s_ld.h, ldW)
        lhsT = GBB2[:, 128 * j:128 * (j + 1)]
        t.matmul(bank(c, b0)[:, 0:512], lhsT, c.WS[0:65, 0:512], start=True, stop=True)
        return t.matmul(bank(c, b0 + 1)[:, 0:512], lhsT, c.WS[0:65, 512:1024],
                        start=True, stop=True)

    emit_tmout_sandwich(c, block, tile_mms, (c.s_ve, vbG), c.STATS[:, 112:128])


def emit_tmout_sandwich(c, block, tile_mms, gate, sums_dst):
    nc = c.nc
    tb = c.s_te.n
    vb = c.s_ve.n
    sb = c.s_se.n
    negm2 = c.STATS[:, 96:97]
    r2 = c.STATS[:, 97:98]
    v1t = c.STATS[:, 98:99]
    v2t = c.STATS[:, 99:100]
    ssq = c.STATS[:, 100:101]
    ssm = c.STATS[:, 101:102]
    junk = c.EPI[:, 0:1024]
    zn = c.EPI[:, 1024:2048]

    @block.tensor
    def _(t):
        if gate is not None:
            t.wait_ge(gate[0].h, gate[1])
        if vb:
            t.wait_ge(c.s_ve.h, vb)
        for j in range(NT):
            b0 = 0 if j % 2 == 0 else 2
            if j >= 2:
                t.wait_ge(c.s_ve.h, vb + 4 * (j - 1))
            ins = tile_mms(t, j, b0)
            c.s_te.inc(ins)

    lvl = OP_LVL

    @block.vector
    def _(v):
        for j in range(NT):
            b0 = 0 if j % 2 == 0 else 2
            pz = c.PS[:, 512 * b0:512 * b0 + 1024]
            v.wait_ge(c.s_te.h, tb + j + 1)
            if lvl >= 2:
                c.s_ve.inc(v.reduce_sum(ssm, pz, axis=AX))
            else:
                c.s_ve.inc(v.tensor_copy(zn, pz[:, 0:1024]))
            if lvl >= 2:
                v.wait_ge(c.s_se.h, sb + 3 * j + 1)
            v.tensor_scalar(negm2, ssm, -1.0 / D, None, ALU.mult)
            v.tensor_mul(v2t, negm2, negm2)
            v.tensor_scalar(v1t, ssq, 1.0 / D, None, ALU.mult)
            c.s_ve.inc(v.tensor_sub(v1t, v1t, v2t))
            if lvl >= 3:
                v.wait_ge(c.s_se.h, sb + 3 * j + 3)
                v.tensor_scalar(zn, pz, negm2, r2, ALU.add, ALU.mult)
            if lvl >= 4:
                ins = v.scalar_tensor_tensor(
                    xtile(c, j), zn, 1.0, xtile(c, j), ALU.mult, ALU.add,
                    accum_out=(sums_dst[:, j:j + 1] if sums_dst is not None else None))
                c.s_ve.inc(ins)
            else:
                c.s_ve.inc(v.nop())
            v.wait_ge(c.s_ve.h, c.s_ve.n)
            c.s_ve.inc(v.nop())

    @block.scalar
    def _(s):
        for j in range(NT):
            b0 = 0 if j % 2 == 0 else 2
            pz = c.PS[:, 512 * b0:512 * b0 + 1024]
            s.wait_ge(c.s_te.h, tb + j + 1)
            s.wait_ge(c.s_ve.h, vb + 4 * j + 1)
            if lvl >= 2:
                c.s_se.inc(s.activation(junk, pz, AF.Square, accum_out=ssq))
            else:
                c.s_se.inc(s.nop())
            if lvl >= 3:
                s.wait_ge(c.s_ve.h, vb + 4 * j + 2)
                c.s_se.inc(s.activation(v1t, v1t, AF.Ln, bias=c.EPSC[:]))
                c.s_se.inc(s.activation(r2, v1t, AF.Exp, scale=-0.5))
            else:
                c.s_se.inc(s.nop())
                c.s_se.inc(s.nop())



def emit_l0_attn(c, block, upto=4):
    load_bias_fm(c, block, c.dr["l0_bq"], 0)
    load_bias_fm(c, block, c.dr["l0_bk"], 1)
    load_bias_row(c, block, c.dr["l0_bv"])
    xn = lambda k, lo, hi: c.XN[:, 2048 * k + lo:2048 * k + hi]
    for q in range(4):
        wrq = stream_w_bf16(c, block, c.dr["l0_WqT"][:, 256 * q:256 * (q + 1)], NKC, 0,
                            wc=256, gate0=(c.s_te, c.s_te.n))
        wrk = stream_w_bf16(c, block, c.dr["l0_WkT"][:, 256 * q:256 * (q + 1)], NKC, 2048, wc=256)
        wrv = stream_w_bf16(c, block, c.dr["l0_WvT"][:, 256 * q:256 * (q + 1)], NKC, 4096, wc=256)
        emit_proj_fm(c, block, c.A3[:, 0:4096], 0, 0, wrq, xn, [2 * q, 2 * q + 1], bias_col=0)
        emit_proj_fm(c, block, c.A3[:, 4096:8192], 0, 2048, wrk, xn, [2 * q, 2 * q + 1], bias_col=1)
        emit_vprime_proj(c, block, c.A3[:, 8192:12352], 4, 4096, wrv, xn, 256,
                         None, 256 * q, zb=0)
        if upto >= 3:
            emit_attention_quarter(c, block, q, 0.125, c.A2)
    if upto >= 4:
        load_bias_row(c, block, c.dr["l0_bo"], gates=((c.s_te, c.s_te.n),))
        wro = stream_w_bf16(c, block, c.dr["l0_WoT"], NKC, 0, wc=1024,
                            gate0=(c.s_te, c.s_te.n), dst=c.A3)
        if OP_LVL >= 1:
            emit_oproj_sandwich(c, block, c.A2, wro, 0, c.STATS[:, 112:128],
                                gate=(c.s_ge, c.s_ge.n))


def emit_l1_attn(c, block):
    @block.vector
    def _(v):
        c.s_ve.inc(v.memset(c.RBb[0:1, 0:1024], 0.0))

    xn = lambda k, lo, hi: c.XN[:, 2048 * k + lo:2048 * k + hi]
    for q in range(4):
        wrq = stream_masked_w(c, block, c.dr["l1_qkvT"][:, 256 * q:256 * (q + 1)],
                              c.dr["l1_qkvmT"][:, 256 * q:256 * (q + 1)], NKC, 0, 256,
                              gate0=(c.s_te, c.s_te.n))
        wrk = stream_masked_w(c, block, c.dr["l1_qkvT"][:, 1024 + 256 * q:1024 + 256 * (q + 1)],
                              c.dr["l1_qkvmT"][:, 1024 + 256 * q:1024 + 256 * (q + 1)],
                              NKC, 2048, 256)
        wrv = stream_masked_w(c, block, c.dr["l1_qkvT"][:, 2048 + 256 * q:2048 + 256 * (q + 1)],
                              c.dr["l1_qkvmT"][:, 2048 + 256 * q:2048 + 256 * (q + 1)],
                              NKC, 4096, 256)
        emit_proj_fm(c, block, c.A3[:, 0:4096], 0, 0, wrq, xn, [2 * q, 2 * q + 1], bias_col=None)
        emit_proj_fm(c, block, c.A3[:, 4096:8192], 0, 2048, wrk, xn, [2 * q, 2 * q + 1], bias_col=None)
        emit_vprime_proj(c, block, c.A3[:, 8192:12352], 4, 4096, wrv, xn, 256,
                         None, 256 * q, zb=0)
        emit_attention_quarter(c, block, q, 1.0 / 32, c.A2)
    wro = stream_masked_w(c, block, c.dr["l1_outT"], c.dr["l1_outmT"], NKC, 0, 1024,
                          gate0=(c.s_te, c.s_te.n), dst=c.A3)
    teb = c.s_te.n
    ldb = c.s_ld.n

    @block.gpsimd
    def _(g):
        g.wait_ge(c.s_te.h, teb)
        dma_ser(g, c.s_ld, c.RB[0:1, 0:1024], c.dr["l1_ob"].rearrange("(a d) -> a d", a=1))
        dma_ser(g, c.s_ld, c.EPI[0:1, 0:1024], c.dr["l1_obm"].rearrange("(a d) -> a d", a=1))
        g.wait_ge(c.s_ld.h, c.s_ld.n)
        g.tensor_scalar(c.EPI[0:1, 0:1024], c.EPI[0:1, 0:1024], 0.0, None, ALU.is_gt)
        c.s_ge.inc(g.tensor_tensor(c.RBb[0:1, 0:1024], c.EPI[0:1, 0:1024],
                                   c.RB[0:1, 0:1024], ALU.mult))

    emit_oproj_sandwich(c, block, c.A2, wro, 0, c.STATS[:, 112:128],
                        gate=(c.s_ge, c.s_ge.n))



def emit_all(c, block):
    emit_init(c, block)

    ph = 99 if DEV_PHASE is None else DEV_PHASE
    if ph >= 1:
        negm, r = emit_ln_stats(c, block, lambda j: xtile(c, j), wait_sems=(c.s_ld,))
        emit_ln_apply_transpose(c, block, lambda j: xtile(c, j), negm, r, c.XN, c.A2)
    if ph >= 2:
        emit_l0_attn(c, block, upto=min(ph, 4))
    if N_SUB >= 2:
        negm, r = emit_ln_stats(c, block, lambda j: xtile(c, j),
                                have_sums=c.STATS[:, 112:128], wait_sems=(c.s_ve,))
        emit_ln_apply_transpose(c, block, lambda j: xtile(c, j), negm, r, c.XN, c.A2)
        emit_l0ffn_slots(c, block)
        if DEBUG_DUMPS:
            vdump = c.s_ve.n

            @block.gpsimd
            def _(g):
                g.wait_ge(c.s_ve.h, vdump)
                dma_ser(g, c.s_st, c.dbg_o, c.SL[0:16, 0:1024])
                g.wait_ge(c.s_st.h, c.s_st.n)
                c.s_ge.inc(g.nop())
        emit_l0ffn_experts(c, block)
        if DEBUG_DUMPS:
            vdump2 = c.s_ve.n

            @block.gpsimd
            def _(g):
                g.wait_ge(c.s_ve.h, vdump2)
                dma_ser(g, c.s_st, c.dbg_of, c.SL[0:16, 0:1024])
                dma_ser(g, c.s_st, c.dbg_h, c.SL[0:16, 2048:3072])
                g.wait_ge(c.s_st.h, c.s_st.n)
                c.s_ge.inc(g.nop())
        emit_l0ffn_cmha(c, block)
        if DEBUG_DUMPS:
            vdump3 = c.s_ge.n

            @block.gpsimd
            def _(g):
                g.wait_ge(c.s_ge.h, vdump3)
                dma_ser(g, c.s_st, c.dbg_a2, c.A2[:])
                dma_ser(g, c.s_st, c.dbg_ks, c.QSL[:])
                g.wait_ge(c.s_st.h, c.s_st.n)
                c.s_ge.inc(g.nop())
        load_bias_row(c, block, c.dr["c_bo"], gates=((c.s_te, c.s_te.n),))
        wr_co = stream_w_bf16(c, block, c.dr["c_Wo"], NKC, 0, wc=1024,
                              gate0=(c.s_te, c.s_te.n), dst=c.A3)
        emit_oproj_sandwich(c, block, c.A2, wr_co, 0, c.STATS[:, 112:128],
                            gate=(c.s_ge, c.s_ge.n))
    if N_SUB >= 3:
        negm, r = emit_ln_stats(c, block, lambda j: xtile(c, j),
                                have_sums=c.STATS[:, 112:128], wait_sems=(c.s_ve,))
        emit_ln_apply_transpose(c, block, lambda j: xtile(c, j), negm, r, c.XN, c.A2)
        emit_l1_attn(c, block)
    if N_SUB >= 4:
        negm, r = emit_ln_stats(c, block, lambda j: xtile(c, j),
                                have_sums=c.STATS[:, 112:128], wait_sems=(c.s_ve,))
        emit_ln_apply_transpose(c, block, lambda j: xtile(c, j), negm, r, c.XN, c.A2)
        emit_l1ffn(c, block)

    vfin = c.s_ve.n

    @block.gpsimd
    def _(g):
        g.wait_ge(c.s_ve.h, vfin)
        for j in range(NT):
            dma_ser(g, c.s_st, c.out[128 * j:128 * (j + 1), :], xtile(c, j))
        g.wait_ge(c.s_st.h, c.s_st.n)


def build_program():
    nc, c = build_nc()
    with nc.Block() as block:
        emit_all(c, wrap_block(block, c))
    return nc, c



def _prep_inputs(inputs):
    f = np.ascontiguousarray
    g = {}
    g["l0_WqT"] = f(inputs["L0_attn_Wq"].T)
    g["l0_WkT"] = f(inputs["L0_attn_Wk"].T)
    g["l0_WvT"] = f(inputs["L0_attn_Wv"].T)
    g["l0_WoT"] = f(inputs["L0_attn_Wo"].T)
    g["l0_bq"] = inputs["L0_attn_bq"]
    g["l0_bk"] = inputs["L0_attn_bk"]
    g["l0_bv"] = inputs["L0_attn_bv"]
    g["l0_bo"] = inputs["L0_attn_bo"]
    g["sq"] = f(inputs["L0_ffn_slot_query"][0])
    for p in ("d", "c"):
        for m in ("Wq", "Wk", "Wv", "Wo"):
            g[f"{p}_{m}"] = f(inputs[f"L0_ffn_{p}_{m}"])
        for m in ("bq", "bk", "bv", "bo"):
            g[f"{p}_{m}"] = inputs[f"L0_ffn_{p}_{m}"]
    g["eb2"] = inputs["L0_ffn_e_b2"]
    g["l1_qkvT"] = f(inputs["L1_attn_qkv_weight"].T)
    g["l1_qkvmT"] = f(inputs["L1_attn_qkv_weight_mask"].T)
    g["l1_outT"] = f(inputs["L1_attn_out_weight"].T)
    g["l1_outmT"] = f(inputs["L1_attn_out_weight_mask"].T)
    g["l1_ob"] = inputs["L1_attn_out_bias"]
    g["l1_obm"] = inputs["L1_attn_out_bias_mask"]
    g["f_Win"] = inputs["L1_ffn_Win"]
    g["f_bin"] = inputs["L1_ffn_bin_"]
    g["f_W1"] = inputs["L1_ffn_W1"]
    g["f_b1"] = inputs["L1_ffn_b1"]
    g["f_W2"] = inputs["L1_ffn_W2"]
    g["f_b2"] = inputs["L1_ffn_b2"]
    g["f_Wout"] = inputs["L1_ffn_Wout"]
    g["f_bout"] = inputs["L1_ffn_bout"]
    y = inputs["L0_attn_y"]
    in_maps = []
    for cid in range(NCORES):
        m = dict(g)
        m["y"] = f(y[cid])
        m["e_ln_g"] = f(inputs["L0_ffn_e_ln_g"][2 * cid:2 * cid + 2])
        m["e_ln_b"] = f(inputs["L0_ffn_e_ln_b"][2 * cid:2 * cid + 2])
        m["eW1"] = f(inputs["L0_ffn_e_W1"][2 * cid:2 * cid + 2])
        m["eWg"] = f(inputs["L0_ffn_e_Wg"][2 * cid:2 * cid + 2])
        m["eW2"] = f(inputs["L0_ffn_e_W2"][2 * cid:2 * cid + 2])
        m["eb1"] = f(inputs["L0_ffn_e_b1"][2 * cid:2 * cid + 2])
        m["ebg"] = f(inputs["L0_ffn_e_bg"][2 * cid:2 * cid + 2])
        in_maps.append(m)
    return in_maps


def kernel(**inputs):
    in_maps = _prep_inputs(inputs)
    nc, _ = build_program()
    res = run_bass_kernel_spmd(nc, in_maps, core_ids=list(range(NCORES)))
    return np.stack([res.results[cid]["out"] for cid in range(NCORES)], axis=0)


# version 04 (final)
# speedup vs baseline: 1.0824x; 1.0617x over previous
import numpy as np
import concourse.bass as bass
import concourse.mybir as mybir
from concourse.bass_utils import run_bass_kernel_spmd

F32 = mybir.dt.float32
BF16 = mybir.dt.bfloat16
AF = mybir.ActivationFunctionType
ALU = mybir.AluOpType
AX = mybir.AxisListType.X

NCORES = 8
T = 2048
D = 1024
H = 16
DH = 64
NT = T // 128
NKC = D // 128
EPS = 1e-5

N_SUB = 4
DEBUG_DUMPS = False
SELF_EDGES = "v"
DEV_PHASE = None
OP_LVL = 4


_PENDING = {}
_SEEN = {}


def traceback_stack():
    import traceback
    return traceback.format_stack()[-6:-2]



class Sem:
    def __init__(self, nc, name):
        self.h = nc.alloc_semaphore(name)
        self.n = 0

    def inc(self, instr, k=1):
        key = id(instr)
        if key in _SEEN and _SEEN[key][0] is self and _SEEN[key][2] is instr:
            import traceback
            raise RuntimeError("double Sem.inc on same instruction: first at\n"
                               + "".join(_SEEN[key][1]))
        _SEEN[key] = (self, traceback_stack(), instr)
        instr.then_inc(self.h, k)
        self.n += k
        w = _PENDING.pop(id(instr), None)
        if w is not None:
            w._last = (self, self.n)
            w._pending = None
        return instr

    def wait(self, eng):
        if self.n:
            eng.wait_ge(self.h, self.n)


class Ctx:
    pass


def build_nc():
    nc = bass.Bass("TRN2", target_bir_lowering=False, debug=False)
    c = Ctx()
    c.nc = nc

    dr = {}

    def din(name, shape, dt=F32):
        dr[name] = nc.dram_tensor(name, list(shape), dt, kind="ExternalInput").ap()
        return dr[name]

    din("y", (T, D))
    din("l0_WqT", (D, D)); din("l0_WkT", (D, D)); din("l0_WvT", (D, D)); din("l0_WoT", (D, D))
    din("l0_bq", (D,)); din("l0_bk", (D,)); din("l0_bv", (D,)); din("l0_bo", (D,))
    din("sq", (16, D))
    for p in ("d", "c"):
        for m in ("Wq", "Wk", "Wv", "Wo"):
            din(f"{p}_{m}", (D, D))
        for m in ("bq", "bk", "bv", "bo"):
            din(f"{p}_{m}", (D,))
    din("e_ln_g", (2, D)); din("e_ln_b", (2, D))
    din("eW1", (2, D, 4096)); din("eWg", (2, D, 4096)); din("eW2", (2, 4096, D))
    din("eb1", (2, 4096)); din("ebg", (2, 4096)); din("eb2", (16, D))
    din("l1_qkvT", (D, 3 * D)); din("l1_qkvmT", (D, 3 * D))
    din("l1_outT", (D, D)); din("l1_outmT", (D, D))
    din("l1_ob", (D,)); din("l1_obm", (D,))
    din("f_Win", (D, 64)); din("f_bin", (64,))
    din("f_W1", (64, 16)); din("f_b1", (64, 16)); din("f_W2", (64, 16)); din("f_b2", (64,))
    din("f_Wout", (64, D)); din("f_bout", (D,))
    c.dr = dr

    c.out = nc.dram_tensor("out", [T, D], F32, kind="ExternalOutput").ap()
    if DEBUG_DUMPS:
        c.dbg_o = nc.dram_tensor("dbg_o", [16, D], F32, kind="ExternalOutput").ap()
        c.dbg_of = nc.dram_tensor("dbg_of", [16, D], F32, kind="ExternalOutput").ap()
        c.dbg_h = nc.dram_tensor("dbg_h", [16, D], F32, kind="ExternalOutput").ap()
        c.dbg_a2 = nc.dram_tensor("dbg_a2", [128, 16384], BF16, kind="ExternalOutput").ap()
        c.dbg_ks = nc.dram_tensor("dbg_ks", [128, 256], BF16, kind="ExternalOutput").ap()

    c.ag1_in = nc.dram_tensor("ag1_in", [16, D], F32)
    c.ag1_out = nc.dram_tensor("ag1_out", [16, D], F32)
    c.agz_in = nc.dram_tensor("agz_in", [16, D], F32)
    c.agz_out = nc.dram_tensor("agz_out", [16, D], F32)

    A = nc.alloc_sbuf_tensor
    c.X = A("X", [128, NT * D], F32)
    c.XN = A("XN", [128, NKC * T], BF16)
    c.A2 = A("A2", [128, NKC * T], BF16)
    c.A3 = A("A3", [128, 12352], BF16)
    c.WS = A("WS", [128, 6144], BF16)
    c.WF = A("WF", [128, 2048], F32)
    c.PT = A("PT", [128, 2048], BF16)
    c.EPI = A("EPI", [128, 2048], F32)
    c.STATS = A("STATS", [128, 128], F32)
    c.RB = A("RB", [1, 1024], F32)
    c.RBb = A("RBb", [1, 2048], BF16)
    c.ID32 = A("ID32", [16, 16], F32)
    c.IDb = A("IDb", [128, 128], BF16)
    c.ONESb = A("ONESb", [1, 128], BF16)
    c.ONES32 = A("ONES32", [1, 64], F32)
    c.SL = A("SL", [16, 3072], F32)
    c.HT = A("HT", [128, 128], F32)
    c.BSM = A("BSM", [128, 32], F32)
    c.QSL = A("QSL", [128, 256], BF16)
    c.FSM = A("FSM", [64, 96], F32)
    c.EPSC = A("EPSC", [128, 1], F32)
    c.ZSB = A("ZSB", [1, 128], BF16)
    c.ZS32 = A("ZS32", [1, 16], F32)
    c.AMN = c.EPI[:].bitcast(BF16)

    c.PS = nc.alloc_psum_tensor("PS", [128, 4096], F32)
    c.PSb = c.PS[:].bitcast(BF16)

    c.s_ld = Sem(nc, "s_ld")
    c.s_wt = Sem(nc, "s_wt")
    c.s_wu = Sem(nc, "s_wu")
    c.s_te = Sem(nc, "s_te")
    c.s_ve = Sem(nc, "s_ve")
    c.s_se = Sem(nc, "s_se")
    c.s_ge = Sem(nc, "s_ge")
    c.s_st = Sem(nc, "s_st")
    c.s_cc = Sem(nc, "s_cc")
    c.s_vv = Sem(nc, "s_vv")
    c.s_ss = Sem(nc, "s_ss")
    c.s_gg = Sem(nc, "s_gg")
    c.wv = EngWrap(c.s_vv, "v")
    c.ws_ = EngWrap(c.s_ss, "s")
    c.wg = EngWrap(c.s_gg, "g")
    return nc, c


_WRAP_OPS = {
    "tensor_copy", "tensor_tensor", "tensor_add", "tensor_sub", "tensor_mul",
    "tensor_max", "tensor_scalar", "tensor_single_scalar", "tensor_scalar_add",
    "tensor_scalar_sub", "tensor_scalar_mul", "tensor_scalar_max",
    "tensor_scalar_min", "tensor_relu", "tensor_reduce", "reduce_sum",
    "reduce_max", "scalar_tensor_tensor", "tensor_tensor_scan", "memset",
    "memzero", "reciprocal", "select", "bn_stats", "bn_aggr", "iota",
    "affine_select", "activation", "sign", "square", "sqrt", "mul", "add",
    "copy", "transpose",
}


class EngWrap:

    def __init__(self, sem, tag):
        self._e = None
        self._s = sem
        self._tag = tag
        self._pending = None
        self._last = None

    def _flush(self):
        p = self._pending
        if p is not None and id(p) in _PENDING:
            _PENDING.pop(id(p))
            p.then_inc(self._s.h, 1)
            self._s.n += 1
            self._last = (self._s, self._s.n)
        self._pending = None

    def __getattr__(self, name):
        fn = getattr(self._e, name)
        if name not in _WRAP_OPS or self._tag not in SELF_EDGES:
            return fn

        def call(*a, **k):
            self._flush()
            if self._last is not None:
                self._e.wait_ge(self._last[0].h, self._last[1])
            ins = fn(*a, **k)
            self._pending = ins
            _PENDING[id(ins)] = self
            return ins
        return call


def wrap_block(block, c):
    class B:
        pass
    b = B()

    def mk(section, w):
        def deco(f):
            def g(eng):
                w._e = eng
                return f(w)
            return section(g)
        return deco
    b.vector = mk(block.vector, c.wv)
    b.scalar = mk(block.scalar, c.ws_)
    b.gpsimd = mk(block.gpsimd, c.wg)
    b.tensor = block.tensor
    b.sync = block.sync
    return b


def gate_wf(eng, c):
    if c.s_ve.n:
        eng.wait_ge(c.s_ve.h, c.s_ve.n)
    if c.s_te.n:
        eng.wait_ge(c.s_te.h, c.s_te.n)
    if c.s_ge.n:
        eng.wait_ge(c.s_ge.h, c.s_ge.n)


def dma_ser(eng, sem, out, in_, **kw):
    if sem.n:
        eng.wait_ge(sem.h, sem.n)
    return sem.inc(eng.dma_start(out=out, in_=in_, **kw), 16)


def bank(c, b, n=1):
    return c.PS[:, 512 * b: 512 * (b + n)]


def xtile(c, j):
    return c.X[:, 1024 * j: 1024 * (j + 1)]


def emit_init(c, block):
    nc = c.nc

    @block.gpsimd
    def _(g):
        for j in range(NT):
            dma_ser(g, c.s_ld, xtile(c, j), c.dr["y"][128 * j:128 * (j + 1), :])
        c.s_ge.inc(g.memset(c.ID32[:], 0.0))
        c.s_ge.inc(g.memset(c.IDb[:], 0.0))
        g.wait_ge(c.s_ge.h, c.s_ge.n)
        c.s_ge.inc(g.affine_select(out=c.ID32[:], in_=c.ID32[:], compare_op=ALU.not_equal,
                                   fill=1.0, base=0, pattern=[[-1, 16]], channel_multiplier=1))
        c.s_ge.inc(g.affine_select(out=c.IDb[:], in_=c.IDb[:], compare_op=ALU.not_equal,
                                   fill=1.0, base=0, pattern=[[-1, 128]], channel_multiplier=1))
        g.wait_ge(c.s_ge.h, c.s_ge.n)
        c.s_ge.inc(g.nop())

    @block.vector
    def _(v):
        v.memset(c.ONESb[:], 1.0)
        v.memset(c.ONES32[0:1, 0:64], 1.0)
        v.memset(c.EPSC[:], EPS)
        v.memset(c.ZSB[:], 0.0)
        v.memset(c.ZS32[:], 0.0)
        v.memset(c.RBb[0:1, :], 0.0)
        v.memset(c.RB[0:1, :], 0.0)
        c.s_ve.inc(v.nop())

    @block.scalar
    def _(s):
        s.add_instruction(mybir.InstLoadActFuncSet(
            name=nc.get_next_instruction_name(), act_func_set_id=6, ins=[], outs=[]))
        c.s_se.inc(s.nop())

    @block.tensor
    def _(t):
        t.wait_ge(c.s_ge.h, c.s_ge.n)


def emit_ln_stats(c, block, src_tiles, have_sums=None, wait_sems=()):
    nc = c.nc
    ssum = c.STATS[:, 32:48]
    ssq = c.STATS[:, 48:64]
    negm = c.STATS[:, 0:16]
    r = c.STATS[:, 16:32]
    t1 = c.STATS[:, 64:80]
    t2 = c.STATS[:, 80:96]
    junk = c.EPI[:, 0:1024]

    if have_sums is None:
        @block.vector
        def _(v):
            for s in wait_sems:
                s.wait(v)
            for j in range(NT):
                ins = v.reduce_sum(ssum[:, j:j + 1], src_tiles(j), axis=AX)
            c.s_ve.inc(ins)

    @block.scalar
    def _(s):
        for sm in wait_sems:
            sm.wait(s)
        for j in range(NT):
            if j:
                s.wait_ge(c.s_se.h, c.s_se.n)
            c.s_se.inc(s.activation(junk[:], src_tiles(j), AF.Square, accum_out=ssq[:, j:j + 1]))

    sums = have_sums if have_sums is not None else ssum

    @block.vector
    def _(v):
        v.wait_ge(c.s_ve.h, c.s_ve.n)
        c.s_se.wait(v)
        v.tensor_scalar(negm[:], sums, -1.0 / D, None, ALU.mult)
        v.tensor_mul(t2[:], negm[:], negm[:])
        v.tensor_scalar(t1[:], ssq[:], 1.0 / D, None, ALU.mult)
        c.s_ve.inc(v.tensor_sub(t1[:], t1[:], t2[:]))

    @block.scalar
    def _(s):
        c.s_ve.wait(s)
        s.activation(t1[:], t1[:], AF.Ln, bias=c.EPSC[:])
        c.s_se.inc(s.activation(r[:], t1[:], AF.Exp, scale=-0.5))
    return negm, r


def emit_ln_apply_transpose(c, block, src_tiles, negm, r, dst_fm, stage):
    nc = c.nc
    vb = c.s_ve.n
    tb = c.s_te.n

    @block.vector
    def _(v):
        c.s_se.wait(v)
        for j in range(NT):
            ins = v.tensor_scalar(stage[:, 1024 * j:1024 * (j + 1)], src_tiles(j),
                                  negm[:, j:j + 1], r[:, j:j + 1], ALU.add, ALU.mult)
            c.s_ve.inc(ins)

    @block.tensor
    def _(t):
        gi = 0
        for j in range(NT):
            t.wait_ge(c.s_ve.h, vb + j + 1)
            for g in range(2):
                b = 6 + gi % 2
                if gi >= 2:
                    t.wait_ge(c.s_ve.h, vb + 16 + (gi - 1))
                for kk in range(4):
                    k = 4 * g + kk
                    ins = t.transpose(c.PSb[:, 1024 * b + 128 * kk:1024 * b + 128 * (kk + 1)],
                                      stage[:, 1024 * j + 128 * k: 1024 * j + 128 * (k + 1)],
                                      c.IDb[:])
                c.s_te.inc(ins)
                gi += 1

    @block.vector
    def _(v):
        gi = 0
        vw = dst_fm.rearrange("p (k t) -> p k t", t=T)
        for j in range(NT):
            for g in range(2):
                b = 6 + gi % 2
                v.wait_ge(c.s_te.h, tb + gi + 1)
                dst = vw[:, 4 * g:4 * g + 4, 128 * j:128 * (j + 1)]
                c.s_ve.inc(v.tensor_copy(dst, c.PSb[:, 1024 * b:1024 * b + 512]
                                         .rearrange("p (a t) -> p a t", a=4)))
                gi += 1


def stream_w_bf16(c, block, dram_ap, n_chunks, ws_off, wc=1024, gate0=None, dst=None):
    nc = c.nc
    vb = c.s_ve.n
    wsem = [c.s_wt, c.s_wu]
    dvals = []

    @block.sync
    def _(sy):
        for k in range(n_chunks):
            if k >= 2:
                sy.wait_ge(c.s_ve.h, vb + (k - 1))
            else:
                gate_wf(sy, c)
            dma_ser(sy, wsem[k % 2], c.WF[:, 1024 * (k % 2):1024 * (k % 2) + wc],
                    dram_ap[128 * k:128 * (k + 1), :])
            dvals.append((wsem[k % 2], wsem[k % 2].n))

    ready = []

    @block.vector
    def _(v):
        for k in range(n_chunks):
            v.wait_ge(dvals[k][0].h, dvals[k][1])
            if k == 0 and gate0 is not None:
                v.wait_ge(gate0[0].h, gate0[1])
            dd = dst if dst is not None else c.WS
            ins = v.tensor_copy(dd[:, ws_off + wc * k: ws_off + wc * (k + 1)],
                                c.WF[:, 1024 * (k % 2):1024 * (k % 2) + wc])
            c.s_ve.inc(ins)
            ready.append((c.s_ve, c.s_ve.n))
    return ready


def load_bias_fm(c, block, dram_ap, col):
    @block.gpsimd
    def _(g):
        vw = dram_ap.rearrange("(k p) -> p k", p=128)
        dma_ser(g, c.s_ld, c.BSM[:, 8 * col:8 * col + 8], vw,
                allow_slow_non_contiguous=True)


def load_bias_row(c, block, dram_ap, slot=0, length=D, make_bf=True, gates=()):
    vb = c.s_ve.n

    tb = c.s_te.n

    @block.gpsimd
    def _(g):
        if vb:
            g.wait_ge(c.s_ve.h, vb)
        if tb:
            g.wait_ge(c.s_te.h, tb)
        for sm, val in gates:
            if val:
                g.wait_ge(sm.h, val)
        src = dram_ap.rearrange("(a d) -> a d", a=1) if len(dram_ap.shape) == 1 else dram_ap
        dma_ser(g, c.s_ld, c.RB[0:1, 0:length], src)

    if make_bf:
        @block.vector
        def _(v):
            c.s_ld.wait(v)
            for sm, val in gates:
                if val:
                    v.wait_ge(sm.h, val)
            c.s_ve.inc(v.tensor_copy(c.RBb[0:1, 0:length], c.RB[0:1, 0:length]))



def emit_proj_fm(c, block, dst_fm, dst_chunk0, ws_off, wt_ready, src_chunk, e_chunks,
                 bias_col, wslice=128):
    nc = c.nc
    tb = c.s_te.n
    vb = c.s_ve.n
    wtot = 128 * len(e_chunks)

    @block.tensor
    def _(t):
        if vb:
            t.wait_ge(c.s_ve.h, vb)
        for jj, j in enumerate(e_chunks):
            b0 = 0 if jj % 2 == 0 else 4
            if jj >= 2:
                t.wait_ge(c.s_ve.h, vb + jj - 1)
            for k in range(NKC):
                if wt_ready is not None and jj == 0:
                    t.wait_ge(wt_ready[k][0].h, wt_ready[k][1])
                lhsT = c.WS[:, ws_off + wtot * k + 128 * jj: ws_off + wtot * k + 128 * (jj + 1)]
                for w in range(4):
                    t.matmul(bank(c, b0 + w)[:, 0:512], lhsT,
                             src_chunk(k, 512 * w, 512 * (w + 1)),
                             start=(k == 0), stop=False)
            for w in range(4):
                ins = t.matmul(bank(c, b0 + w)[:, 0:512], c.ZSB[0:1, 0:128],
                               c.RBb[0:1, 0:512], start=False, stop=True)
            c.s_te.inc(ins)

    ldn = c.s_ld.n

    @block.vector
    def _(v):
        if bias_col is not None and ldn:
            v.wait_ge(c.s_ld.h, ldn)
        vw = dst_fm.rearrange("p (k t) -> p k t", t=T)
        for jj, j in enumerate(e_chunks):
            b0 = 0 if jj % 2 == 0 else 4
            v.wait_ge(c.s_te.h, tb + jj + 1)
            pv = c.PS[:, 512 * b0:512 * b0 + 2048]
            dst = vw[:, dst_chunk0 + jj, :]
            if bias_col is not None:
                ins = v.tensor_scalar(dst, pv, c.BSM[:, 8 * bias_col + j:8 * bias_col + j + 1],
                                      None, ALU.add)
            else:
                ins = v.tensor_copy(dst, pv)
            c.s_ve.inc(ins)


def emit_vprime_proj(c, block, dst, nheads, ws_off, wt_ready, src_chunk, wc,
                     bias_slot, bias_off, zb=0):
    nc = c.nc
    tb = c.s_te.n
    vb = c.s_ve.n
    CW = 65 * nheads

    @block.tensor
    def _(t):
        if vb:
            t.wait_ge(c.s_ve.h, vb)
        for j in range(NT):
            b = zb + (j % 2)
            if j >= 2:
                t.wait_ge(c.s_ve.h, vb + j - 1)
            for k in range(NKC):
                if wt_ready is not None and j == 0:
                    t.wait_ge(wt_ready[k][0].h, wt_ready[k][1])
                t.matmul(bank(c, b)[:, 0:wc], src_chunk(k, 128 * j, 128 * (j + 1)),
                         c.WS[:, ws_off + wc * k: ws_off + wc * (k + 1)],
                         start=(k == 0), stop=False)
            ins = t.matmul(bank(c, b)[:, 0:wc], c.ONESb[:, 0:128],
                           c.RBb[0:1, bias_off:bias_off + wc], start=False, stop=True)
            c.s_te.inc(ins)

    @block.vector
    def _(v):
        dv = dst.rearrange("p (j h e) -> p j h e", h=nheads, e=65)
        for j in range(NT):
            b = zb + (j % 2)
            v.wait_ge(c.s_te.h, tb + j + 1)
            v.tensor_copy(dv[:, j, :, 0:64],
                          bank(c, b)[:, 0:wc].rearrange("p (h e) -> p h e", e=64))
            c.s_ve.inc(v.memset(dv[:, j, :, 64:65], 1.0))



def emit_attention_quarter(c, block, q, scale, attn_dst):
    nc = c.nc
    NKT = 16

    for jj in range(2):
        cglob = 2 * q + jj
        tb = c.s_te.n
        vb = c.s_ve.n
        sb = c.s_se.n

        @block.tensor
        def _(t):
            if vb:
                t.wait_ge(c.s_ve.h, vb)
            for w in range(4):
                for kp in range(NKT // 2):
                    if not (w == 0 and kp == 0):
                        t.wait_ge(c.s_se.h, sb + 8 * w + kp)
                    if w > 0 and kp == 0:
                        t.wait_ge(c.s_ve.h, vb + 3 * w)
                    for i in range(2):
                        kt = 2 * kp + i
                        lA = c.A3[0:64, 4096 + 2048 * jj + 128 * kt: 4096 + 2048 * jj + 128 * (kt + 1)]
                        lB = c.A3[64:128, 4096 + 2048 * jj + 128 * kt: 4096 + 2048 * jj + 128 * (kt + 1)]
                        rA = c.A3[0:64, 2048 * jj + 512 * w: 2048 * jj + 512 * (w + 1)]
                        rB = c.A3[64:128, 2048 * jj + 512 * w: 2048 * jj + 512 * (w + 1)]
                        t.matmul(bank(c, 2 * i)[:, 0:512], lA, rA, start=True, stop=True)
                        ins = t.matmul(bank(c, 2 * i + 1)[:, 0:512], lB, rB, start=True, stop=True)
                    c.s_te.inc(ins)
                    t.wait_ge(c.s_se.h, sb + 8 * w + kp + 1)
                    for i in range(2):
                        kt = 2 * kp + i
                        vA = c.A3[:, 8192 + 260 * kt + 65 * (2 * jj): 8192 + 260 * kt + 65 * (2 * jj) + 65]
                        vB = c.A3[:, 8192 + 260 * kt + 65 * (2 * jj + 1): 8192 + 260 * kt + 65 * (2 * jj + 1) + 65]
                        pA = c.PT[:, 1024 * i: 1024 * i + 512]
                        pB = c.PT[:, 1024 * i + 512: 1024 * i + 1024]
                        t.matmul(bank(c, 4)[0:65, 0:512], vA, pA, start=(kt == 0), stop=False)
                        ins = t.matmul(bank(c, 5)[0:65, 0:512], vB, pB, start=(kt == 0), stop=False)
                    if kp < NKT // 2 - 1:
                        c.s_te.inc(ins)
                t.matmul(bank(c, 4)[0:65, 0:512], c.ZSB[0:1, 0:65],
                         c.RBb[0:1, 0:512], start=False, stop=True)
                ins = t.matmul(bank(c, 5)[0:65, 0:512], c.ZSB[0:1, 0:65],
                               c.RBb[0:1, 0:512], start=False, stop=True)
                c.s_te.inc(ins)
                t.wait_ge(c.s_ve.h, vb + 3 * w + 1)
                rr = c.EPI[0:1, 1024:2048]
                t.matmul(bank(c, 6)[0:64, 0:512], c.ONES32[:, 0:64], rr[:, 0:512],
                         start=True, stop=True)
                ins = t.matmul(bank(c, 7)[0:64, 0:512], c.ONES32[:, 0:64], rr[:, 512:1024],
                               start=True, stop=True)
                c.s_te.inc(ins)

        @block.scalar
        def _(s):
            for w in range(4):
                for kp in range(NKT // 2):
                    s.wait_ge(c.s_te.h, tb + 17 * w + 2 * kp + 1)
                    ins = s.activation(c.PT[:, 0:2048], c.PS[:, 0:2048],
                                       AF.Exp, scale=scale)
                    c.s_se.inc(ins)

        gb = c.s_ge.n

        @block.vector
        def _(v):
            for w in range(4):
                v.wait_ge(c.s_te.h, tb + 17 * w + 16)
                rr = c.EPI[0:1, 1024:2048]
                c.s_ve.inc(v.reciprocal(rr, c.PS[64:65, 512 * 4:512 * 4 + 1024]))
                v.wait_ge(c.s_te.h, tb + 17 * w + 17)
                bc = c.EPI[0:64, 1024:2048]
                c.s_ve.inc(v.tensor_copy(bc, c.PS[0:64, 512 * 6:512 * 6 + 1024]))
                if gb + w > 0:
                    v.wait_ge(c.s_ge.h, gb + w)
                c.s_ve.inc(v.tensor_mul(c.AMN[0:64, 0:1024], c.PS[0:64, 512 * 4:512 * 4 + 1024], bc))

        @block.gpsimd
        def _(g):
            for w in range(4):
                g.wait_ge(c.s_ve.h, vb + 3 * w + 3)
                dstA = attn_dst[0:64, 2048 * cglob + 512 * w: 2048 * cglob + 512 * (w + 1)]
                dstB = attn_dst[64:128, 2048 * cglob + 512 * w: 2048 * cglob + 512 * (w + 1)]
                dma_ser(g, c.s_st, dstA, c.AMN[0:64, 0:512])
                dma_ser(g, c.s_st, dstB, c.AMN[0:64, 512:1024])
                g.wait_ge(c.s_st.h, c.s_st.n)
                c.s_ge.inc(g.nop())


def emit_oproj_sandwich(c, block, attn_src, wt_ready_o, bias_row_off, sums_dst, gate=None):
    nc = c.nc
    tb = c.s_te.n
    vb = c.s_ve.n
    sb = c.s_se.n
    asrc = attn_src.rearrange("p (k t) -> p k t", t=T)
    negm2 = c.STATS[:, 96:97]
    r2 = c.STATS[:, 97:98]
    v1t = c.STATS[:, 98:99]
    v2t = c.STATS[:, 99:100]
    ssq = c.STATS[:, 100:101]
    ssm = c.STATS[:, 101:102]
    junk = c.EPI[:, 0:1024]
    zn = c.EPI[:, 1024:2048]

    @block.tensor
    def _(t):
        if gate is not None:
            t.wait_ge(gate[0].h, gate[1])
        if vb:
            t.wait_ge(c.s_ve.h, vb)
        for j in range(NT):
            b0 = 0 if j % 2 == 0 else 2
            if j >= 2:
                t.wait_ge(c.s_ve.h, vb + 4 * (j - 1))
            for k in range(NKC):
                if wt_ready_o is not None and j == 0:
                    t.wait_ge(wt_ready_o[k][0].h, wt_ready_o[k][1])
                lhsT = asrc[:, k, 128 * j:128 * (j + 1)]
                t.matmul(bank(c, b0)[:, 0:512], lhsT, c.A3[:, 1024 * k:1024 * k + 512],
                         start=(k == 0), stop=False)
                t.matmul(bank(c, b0 + 1)[:, 0:512], lhsT, c.A3[:, 1024 * k + 512:1024 * (k + 1)],
                         start=(k == 0), stop=False)
            t.matmul(bank(c, b0)[:, 0:512], c.ONESb[:, 0:128],
                     c.RBb[0:1, bias_row_off:bias_row_off + 512], start=False, stop=True)
            ins = t.matmul(bank(c, b0 + 1)[:, 0:512], c.ONESb[:, 0:128],
                           c.RBb[0:1, bias_row_off + 512:bias_row_off + 1024],
                           start=False, stop=True)
            c.s_te.inc(ins)

    @block.vector
    def _(v):
        for j in range(NT):
            b0 = 0 if j % 2 == 0 else 2
            pz = c.PS[:, 512 * b0:512 * b0 + 1024]
            v.wait_ge(c.s_te.h, tb + j + 1)
            c.s_ve.inc(v.reduce_sum(ssm, pz, axis=AX))
            v.wait_ge(c.s_se.h, sb + 3 * j + 1)
            v.tensor_scalar(negm2, ssm, -1.0 / D, None, ALU.mult)
            v.tensor_mul(v2t, negm2, negm2)
            v.tensor_scalar(v1t, ssq, 1.0 / D, None, ALU.mult)
            c.s_ve.inc(v.tensor_sub(v1t, v1t, v2t))
            v.wait_ge(c.s_se.h, sb + 3 * j + 3)
            v.tensor_scalar(zn, pz, negm2, r2, ALU.add, ALU.mult)
            ins = v.scalar_tensor_tensor(
                xtile(c, j), zn, 1.0, xtile(c, j), ALU.mult, ALU.add,
                accum_out=(sums_dst[:, j:j + 1] if sums_dst is not None else None))
            c.s_ve.inc(ins)
            v.wait_ge(c.s_ve.h, c.s_ve.n)
            c.s_ve.inc(v.nop())

    @block.scalar
    def _(s):
        for j in range(NT):
            b0 = 0 if j % 2 == 0 else 2
            pz = c.PS[:, 512 * b0:512 * b0 + 1024]
            s.wait_ge(c.s_te.h, tb + j + 1)
            s.wait_ge(c.s_ve.h, vb + 4 * j + 1)
            c.s_se.inc(s.activation(junk, pz, AF.Square, accum_out=ssq))
            s.wait_ge(c.s_ve.h, vb + 4 * j + 2)
            c.s_se.inc(s.activation(v1t, v1t, AF.Ln, bias=c.EPSC[:]))
            c.s_se.inc(s.activation(r2, v1t, AF.Exp, scale=-0.5))



def stream_w_rot(c, block, dram_ap, n_chunks, ws_off, wc, gate_sem, gate_counts):
    nc = c.nc
    wsem = [c.s_wt, c.s_wu]
    dvals = []

    @block.sync
    def _(sy):
        for k in range(n_chunks):
            if k >= 2:
                sy.wait_ge(gate_sem.h, gate_counts(k - 2))
            else:
                gate_wf(sy, c)
            dma_ser(sy, wsem[k % 2], c.WF[:, 1024 * (k % 2):1024 * (k % 2) + wc],
                    dram_ap[128 * k:128 * (k + 1), :])
            dvals.append((wsem[k % 2], wsem[k % 2].n))

    ready = []

    @block.vector
    def _(v):
        for k in range(n_chunks):
            v.wait_ge(dvals[k][0].h, dvals[k][1])
            if k >= 2:
                v.wait_ge(gate_sem.h, gate_counts(k - 2))
            ins = v.tensor_copy(c.WS[:, ws_off + wc * (k % 2): ws_off + wc * (k % 2) + wc],
                                c.WF[:, 1024 * (k % 2):1024 * (k % 2) + wc])
            c.s_ve.inc(ins)
            ready.append((c.s_ve, c.s_ve.n))
    return ready


def stream_masked_w(c, block, dram_w, dram_m, n_chunks, ws_off, wc, gate0=None, dst=None):
    nc = c.nc
    gb = c.s_ge.n
    dvals = []
    assert wc <= 1024

    @block.sync
    def _(sy):
        for k in range(n_chunks):
            if k >= 1:
                sy.wait_ge(c.s_ge.h, gb + k)
            else:
                gate_wf(sy, c)
            dma_ser(sy, c.s_wt, c.WF[:, 0:wc], dram_w[128 * k:128 * (k + 1), :])
            dma_ser(sy, c.s_wt, c.WF[:, 1024:1024 + wc], dram_m[128 * k:128 * (k + 1), :])
            dvals.append((c.s_wt, c.s_wt.n))

    ready = []

    @block.gpsimd
    def _(g):
        for k in range(n_chunks):
            g.wait_ge(dvals[k][0].h, dvals[k][1])
            if k == 0 and gate0 is not None:
                g.wait_ge(gate0[0].h, gate0[1])
            dd = dst if dst is not None else c.WS
            g.tensor_scalar(c.WF[:, 1024:1024 + wc], c.WF[:, 1024:1024 + wc],
                            0.0, None, ALU.is_gt)
            ins = g.tensor_tensor(dd[:, ws_off + wc * k: ws_off + wc * (k + 1)],
                                  c.WF[:, 1024:1024 + wc], c.WF[:, 0:wc], ALU.mult)
            c.s_ge.inc(ins)
            ready.append((c.s_ge, c.s_ge.n))
    return ready



def emit_l0ffn_slots(c, block):
    nc = c.nc
    SLo = c.SL[0:16, 0:1024]
    S1 = c.SL[0:16, 1024:2048]
    S2 = c.SL[0:16, 2048:3072]

    vbq = c.s_ve.n
    teq = c.s_te.n

    @block.gpsimd
    def _(g):
        dma_ser(g, c.s_ld, S2, c.dr["sq"])
        dma_ser(g, c.s_ld, S1, c.dr["d_bo"].rearrange("(a d) -> a d", a=1).broadcast_to([16, D]))
        if vbq:
            g.wait_ge(c.s_ve.h, vbq)
        if teq:
            g.wait_ge(c.s_te.h, teq)
        dma_ser(g, c.s_ld, c.RB[0:1, 0:1024], c.dr["d_bq"].rearrange("(a d) -> a d", a=1))

    ldb = c.s_ld.n
    vb0 = c.s_ve.n

    @block.vector
    def _(v):
        v.wait_ge(c.s_ld.h, ldb)
        c.s_ve.inc(v.tensor_add(S1, S1, S2))

    tb0 = c.s_te.n

    @block.tensor
    def _(t):
        t.wait_ge(c.s_ld.h, ldb)
        for k in range(NKC):
            ins = t.transpose(bank(c, 6)[:, 16 * k:16 * (k + 1)],
                              S2[:, 128 * k:128 * (k + 1)], c.ID32[0:16, 0:16])
        c.s_te.inc(ins)

    @block.vector
    def _(v):
        v.wait_ge(c.s_te.h, tb0 + 1)
        c.s_ve.inc(v.tensor_copy(c.HT[:, 0:128], bank(c, 6)[:, 0:128]))

    vb1 = c.s_ve.n
    tb1 = c.s_te.n
    wsem = [c.s_wt, c.s_wu]
    dv1 = []

    @block.sync
    def _(sy):
        for k in range(NKC):
            if k >= 2:
                sy.wait_ge(c.s_te.h, tb1 + (k - 1))
            else:
                gate_wf(sy, c)
            dma_ser(sy, wsem[k % 2], c.WF[:, 1024 * (k % 2):1024 * (k % 2 + 1)],
                    c.dr["d_Wq"][128 * k:128 * (k + 1), :])
            dv1.append((wsem[k % 2], wsem[k % 2].n))

    @block.tensor
    def _(t):
        t.wait_ge(c.s_ve.h, vb1)
        for k in range(NKC):
            t.wait_ge(dv1[k][0].h, dv1[k][1])
            lhsT = c.HT[:, 16 * k:16 * (k + 1)]
            t.matmul(bank(c, 4)[0:16, 0:512], lhsT, c.WF[:, 1024 * (k % 2):1024 * (k % 2) + 512],
                     start=(k == 0), stop=False)
            ins = t.matmul(bank(c, 5)[0:16, 0:512], lhsT,
                           c.WF[:, 1024 * (k % 2) + 512:1024 * (k % 2 + 1)],
                           start=(k == 0), stop=False)
            c.s_te.inc(ins)
        t.wait_ge(c.s_ld.h, ldb)
        t.matmul(bank(c, 4)[0:16, 0:512], c.ONES32[:, 0:16], c.RB[0:1, 0:512],
                 start=False, stop=True)
        ins = t.matmul(bank(c, 5)[0:16, 0:512], c.ONES32[:, 0:16], c.RB[0:1, 512:1024],
                       start=False, stop=True)
        c.s_te.inc(ins)

    tb2 = c.s_te.n

    @block.vector
    def _(v):
        v.wait_ge(c.s_te.h, tb2)
        c.s_ve.inc(v.tensor_copy(S2, c.PS[0:16, 512 * 4:512 * 4 + 1024]))

    vb2 = c.s_ve.n
    tb3 = c.s_te.n

    @block.tensor
    def _(t):
        t.wait_ge(c.s_ve.h, vb2)
        for k in range(NKC):
            ins = t.transpose(bank(c, 6)[:, 16 * k:16 * (k + 1)],
                              S2[:, 128 * k:128 * (k + 1)], c.ID32[0:16, 0:16])
        c.s_te.inc(ins)

    @block.vector
    def _(v):
        v.wait_ge(c.s_te.h, tb3 + 1)
        c.s_ve.inc(v.tensor_copy(c.QSL[:, 0:128], bank(c, 6)[:, 0:128]))

    load_bias_row(c, block, c.dr["d_bv"], gates=((c.s_te, c.s_te.n),))
    load_bias_fm(c, block, c.dr["d_bk"], 2)

    for q in range(4):
        wr_k = stream_w_bf16(c, block, c.dr["d_Wk"][:, 256 * q:256 * (q + 1)], NKC, 0, wc=256)
        wr_v = stream_w_bf16(c, block, c.dr["d_Wv"][:, 256 * q:256 * (q + 1)], NKC, 2048, wc=256)
        emit_proj_fm(c, block, c.A3[:, 4096:8192], 0, 0, wr_k,
                     lambda k, lo, hi: c.XN[:, 2048 * k + lo:2048 * k + hi],
                     [2 * q, 2 * q + 1], bias_col=2)
        emit_vprime_proj(c, block, c.A3[:, 8192:12352], 4, 2048, wr_v,
                         lambda k, lo, hi: c.XN[:, 2048 * k + lo:2048 * k + hi],
                         256, None, 256 * q, zb=0)
        for jj in range(2):
            tb = c.s_te.n
            sb = c.s_se.n
            vb = c.s_ve.n
            ch = 2 * q + jj

            @block.tensor
            def _(t, jj=jj, ch=ch, tb=tb, sb=sb, vb=vb):
                t.wait_ge(c.s_ve.h, vb)
                for kt in range(NT):
                    lA = c.A3[0:64, 4096 + 2048 * jj + 128 * kt: 4096 + 2048 * jj + 128 * (kt + 1)]
                    lB = c.A3[64:128, 4096 + 2048 * jj + 128 * kt: 4096 + 2048 * jj + 128 * (kt + 1)]
                    rA = c.QSL[0:64, 16 * ch:16 * (ch + 1)]
                    rB = c.QSL[64:128, 16 * ch:16 * (ch + 1)]
                    t.matmul(bank(c, 0)[:, 16 * kt:16 * (kt + 1)], lA, rA, start=True, stop=True)
                    ins = t.matmul(bank(c, 1)[:, 16 * kt:16 * (kt + 1)], lB, rB, start=True, stop=True)
                c.s_te.inc(ins)
                t.wait_ge(c.s_se.h, sb + 1)
                for kt in range(NT):
                    for hh in range(2):
                        vsl = c.A3[:, 8192 + 260 * kt + 65 * (2 * jj + hh):
                                   8192 + 260 * kt + 65 * (2 * jj + hh) + 65]
                        psl = c.PT[:, 256 * hh + 16 * kt: 256 * hh + 16 * (kt + 1)]
                        t.matmul(bank(c, 2 + hh)[0:16, 0:65], psl, vsl,
                                 start=(kt == 0), stop=False)
                for hh in range(2):
                    ins = t.matmul(bank(c, 2 + hh)[0:16, 0:65],
                                   c.ZSB[0:1, 0:16], c.RBb[0:1, 0:65],
                                   start=False, stop=True)
                c.s_te.inc(ins)

            @block.scalar
            def _(s, tb=tb):
                s.wait_ge(c.s_te.h, tb + 1)
                s.activation(c.PT[:, 0:256], bank(c, 0)[:, 0:256], AF.Exp, scale=0.125)
                c.s_se.inc(s.activation(c.PT[:, 256:512], bank(c, 1)[:, 0:256], AF.Exp, scale=0.125))

            @block.vector
            def _(v, jj=jj, ch=ch, tb=tb):
                v.wait_ge(c.s_te.h, tb + 2)
                rc = c.FSM[0:16, 86:88]
                v.reciprocal(rc, c.PS[0:16, 1088:1601:512])
                for hh in range(2):
                    h = 2 * ch + hh
                    ins = v.tensor_scalar(
                        c.SL[0:16, 2048 + 64 * h:2048 + 64 * (h + 1)],
                        bank(c, 2 + hh)[0:16, 0:64],
                        rc[:, hh:hh + 1], None, ALU.mult)
                c.s_ve.inc(ins)

    vb3 = c.s_ve.n
    tb4 = c.s_te.n

    @block.tensor
    def _(t):
        t.wait_ge(c.s_ve.h, vb3)
        for k in range(NKC):
            ins = t.transpose(bank(c, 6)[:, 16 * k:16 * (k + 1)],
                              c.SL[0:16, 2048 + 128 * k:2048 + 128 * (k + 1)], c.ID32[0:16, 0:16])
        c.s_te.inc(ins)

    @block.vector
    def _(v):
        v.wait_ge(c.s_te.h, tb4 + 1)
        c.s_ve.inc(v.tensor_copy(c.HT[:, 0:128], bank(c, 6)[:, 0:128]))

    vb4 = c.s_ve.n
    tb5 = c.s_te.n
    dv2 = []

    @block.sync
    def _(sy):
        for k in range(NKC):
            if k >= 2:
                sy.wait_ge(c.s_te.h, tb5 + (k - 1))
            else:
                gate_wf(sy, c)
            dma_ser(sy, wsem[k % 2], c.WF[:, 1024 * (k % 2):1024 * (k % 2 + 1)],
                    c.dr["d_Wo"][128 * k:128 * (k + 1), :])
            dv2.append((wsem[k % 2], wsem[k % 2].n))

    @block.tensor
    def _(t):
        t.wait_ge(c.s_ve.h, vb4)
        for k in range(NKC):
            t.wait_ge(dv2[k][0].h, dv2[k][1])
            lhsT = c.HT[:, 16 * k:16 * (k + 1)]
            t.matmul(bank(c, 4)[0:16, 0:512], lhsT, c.WF[:, 1024 * (k % 2):1024 * (k % 2) + 512],
                     start=(k == 0), stop=False)
            ins = t.matmul(bank(c, 5)[0:16, 0:512], lhsT,
                           c.WF[:, 1024 * (k % 2) + 512:1024 * (k % 2 + 1)],
                           start=(k == 0), stop=False)
            if k < NKC - 1:
                c.s_te.inc(ins)
        t.matmul(bank(c, 4)[0:16, 0:512], c.ZS32[0:1, 0:16], c.RB[0:1, 0:512],
                 start=False, stop=True)
        ins = t.matmul(bank(c, 5)[0:16, 0:512], c.ZS32[0:1, 0:16], c.RB[0:1, 0:512],
                       start=False, stop=True)
        c.s_te.inc(ins)

    tb6 = c.s_te.n

    @block.vector
    def _(v):
        v.wait_ge(c.s_te.h, tb6)
        c.s_ve.inc(v.tensor_add(c.SL[0:16, 0:1024], c.SL[0:16, 1024:2048],
                                c.PS[0:16, 512 * 4:512 * 4 + 1024]))


def emit_l0ffn_experts(c, block):
    nc = c.nc
    UG = c.A3[:].bitcast(F32)[:, 0:1536]
    SLo = c.SL[0:16, 0:1024]
    S1 = c.SL[0:16, 1024:2048]
    S2 = c.SL[0:16, 2048:3072]
    S3 = c.EPI[0:16, 1024:2048]
    vb0 = c.s_ve.n
    gb0 = c.s_ge.n

    @block.gpsimd
    def _(g):
        g.wait_ge(c.s_ve.h, vb0)
        dma_ser(g, c.s_ld, c.ag1_in.ap(), SLo)
        g.wait_ge(c.s_ld.h, c.s_ld.n)
        from concourse.aot_env import allow_ici_all_to_all
        with allow_ici_all_to_all():
            c.s_cc.inc(g.collective_compute(
                "AllToAll", ALU.bypass, replica_groups=[list(range(NCORES))],
                ins=[c.ag1_in.ap().opt()], outs=[c.ag1_out.ap().opt()]))
        g.wait_ge(c.s_cc.h, c.s_cc.n)
        dma_ser(g, c.s_ld, S2, c.ag1_out.ap()[0:16, :])
        dma_ser(g, c.s_ld, S1, c.dr["e_ln_g"].rearrange("(a e) d -> a e d", a=1).broadcast_to([8, 2, D]))
        dma_ser(g, c.s_ld, S3, c.dr["e_ln_b"].rearrange("(a e) d -> a e d", a=1).broadcast_to([8, 2, D]))
        g.wait_ge(c.s_ld.h, c.s_ld.n)
        c.s_ge.inc(g.nop())

    ge1 = c.s_ge.n
    sb0 = c.s_se.n
    essq = c.FSM[0:16, 80:81]
    esum = c.FSM[0:16, 81:82]
    enegm = c.FSM[0:16, 82:83]
    evar = c.FSM[0:16, 83:84]
    er = c.FSM[0:16, 84:85]
    et = c.FSM[0:16, 85:86]

    @block.scalar
    def _(s):
        s.wait_ge(c.s_ge.h, ge1)
        c.s_se.inc(s.activation(c.EPI[0:16, 0:1024], S2, AF.Square, accum_out=essq))

    @block.vector
    def _(v):
        v.wait_ge(c.s_ge.h, ge1)
        v.reduce_sum(esum, S2, axis=AX)
        v.wait_ge(c.s_se.h, sb0 + 1)
        v.tensor_scalar(enegm, esum, -1.0 / D, None, ALU.mult)
        v.tensor_mul(et, enegm, enegm)
        v.tensor_scalar(evar, essq, 1.0 / D, None, ALU.mult)
        c.s_ve.inc(v.tensor_sub(evar, evar, et))

    sb1 = c.s_se.n
    vb1 = c.s_ve.n

    @block.scalar
    def _(s):
        s.wait_ge(c.s_ve.h, vb1)
        s.activation(evar, evar, AF.Ln, bias=c.EPSC[0:16, :])
        c.s_se.inc(s.activation(er, evar, AF.Exp, scale=-0.5))

    @block.vector
    def _(v):
        v.wait_ge(c.s_se.h, sb1 + 1)
        v.tensor_scalar(S2, S2, enegm, er, ALU.add, ALU.mult)
        v.tensor_mul(S2, S2, S1)
        c.s_ve.inc(v.tensor_add(S2, S2, S3))

    vb2 = c.s_ve.n
    tb0 = c.s_te.n

    @block.tensor
    def _(t):
        t.wait_ge(c.s_ve.h, vb2)
        for k in range(NKC):
            ins = t.transpose(bank(c, 6)[:, 16 * k:16 * (k + 1)],
                              S2[:, 128 * k:128 * (k + 1)], c.ID32[0:16, 0:16])
        c.s_te.inc(ins)

    @block.vector
    def _(v):
        v.wait_ge(c.s_te.h, tb0 + 1)
        c.s_ve.inc(v.tensor_copy(c.HT[:, 0:128], bank(c, 6)[:, 0:128]))

    vbH = c.s_ve.n

    mats = [("eW1", "eb1", 0, 0), ("eWg", "ebg", 0, 512), ("eW1", "eb1", 1, 1024), ("eWg", "ebg", 1, 1536)]
    tbU = c.s_te.n
    wsem = [c.s_wt, c.s_wu]
    dvU = []
    ndma = [0]

    @block.sync
    def _(sy):
        for (wname, bname, e, off) in mats:
            for k in range(NKC):
                for g4 in range(4):
                    if ndma[0] >= 2:
                        sy.wait_ge(c.s_te.h, tbU + (ndma[0] - 1))
                    else:
                        gate_wf(sy, c)
                    dma_ser(sy, wsem[ndma[0] % 2],
                            c.WF[:, 1024 * (ndma[0] % 2):1024 * (ndma[0] % 2 + 1)],
                            c.dr[wname][e, 128 * k:128 * (k + 1), 1024 * g4:1024 * (g4 + 1)])
                    dvU.append((wsem[ndma[0] % 2], wsem[ndma[0] % 2].n))
                    ndma[0] += 1

    nmm = [0]

    @block.tensor
    def _(t):
        t.wait_ge(c.s_ve.h, vbH)
        pall = c.PS[:, 0:2048]
        for (wname, bname, e, off) in mats:
            for k in range(NKC):
                for g4 in range(4):
                    t.wait_ge(dvU[nmm[0]][0].h, dvU[nmm[0]][1])
                    rhs = c.HT[:, 16 * k + e:16 * k + 16:2]
                    for ft in range(8):
                        fcol = off + 8 * (8 * g4 + ft)
                        ins = t.matmul(pall[:, fcol:fcol + 8],
                                       c.WF[:, 1024 * (nmm[0] % 2) + 128 * ft:
                                            1024 * (nmm[0] % 2) + 128 * (ft + 1)],
                                       rhs, start=(k == 0 and g4 == 0 and ft == 0),
                                       stop=False)
                    nmm[0] += 1
                    c.s_te.inc(ins)

    tbB = c.s_te.n
    ldB = c.s_ld.n

    @block.gpsimd
    def _(g):
        nq = 0
        for i, (wname, bname, e, off) in enumerate(mats):
            for qtr in range(4):
                if nq:
                    g.wait_ge(c.s_te.h, tbB + nq)
                dma_ser(g, c.s_ld, c.RB[0:1, 0:1024],
                        c.dr[bname][e, 1024 * qtr:1024 * (qtr + 1)].rearrange("(a d) -> a d", a=1))
                nq += 1

    @block.tensor
    def _(t):
        nb = 0
        pall = c.PS[:, 0:2048]
        for i, (wname, bname, e, off) in enumerate(mats):
            for qtr in range(4):
                t.wait_ge(c.s_ld.h, ldB + 16 * (nb + 1))
                for fi in range(8):
                    ft = 8 * qtr + fi
                    fcol = off + 8 * ft
                    ins = t.matmul(pall[:, fcol:fcol + 8],
                                   c.RB[0:1, 128 * fi:128 * (fi + 1)],
                                   c.ONES32[0:1, 0:8], start=False,
                                   stop=(qtr == 3 and fi == 7))
                nb += 1
                c.s_te.inc(ins)

    tb1 = c.s_te.n
    sb2 = c.s_se.n

    @block.scalar
    def _(s):
        s.wait_ge(c.s_te.h, tb1)
        for b4 in range(4):
            ins = s.activation(c.EPI[:, 256 * b4:256 * (b4 + 1)],
                               c.PS[:, 512 * b4:512 * b4 + 256], AF.Exp, scale=-1.0)
        c.s_se.inc(ins)

    vb3 = c.s_ve.n

    @block.vector
    def _(v):
        v.wait_ge(c.s_se.h, sb2 + 1)
        v.tensor_scalar(c.EPI[:, 1024:2048], c.EPI[:, 0:1024], 1.0, None, ALU.add)
        v.reciprocal(c.EPI[:, 0:1024], c.EPI[:, 1024:2048])
        pst = c.PS[:, 0:2048].rearrange("p (b x) -> p b x", x=512)[:, :, 0:256]
        v.tensor_mul(UG[:, 0:1024].rearrange("p (b x) -> p b x", x=256), pst,
                     c.EPI[:, 0:1024].rearrange("p (b x) -> p b x", x=256))
        v.tensor_mul(UG[:, 1024:1280], UG[:, 0:256], UG[:, 256:512])
        c.s_ve.inc(v.tensor_mul(UG[:, 1280:1536], UG[:, 512:768], UG[:, 768:1024]))

    vb4 = c.s_ve.n
    tb2 = c.s_te.n
    dv3 = []
    nd2 = [0]

    @block.sync
    def _(sy):
        for e in range(2):
            for ft in range(32):
                if nd2[0] >= 2:
                    sy.wait_ge(c.s_te.h, tb2 + (nd2[0] - 1))
                else:
                    gate_wf(sy, c)
                dma_ser(sy, wsem[nd2[0] % 2],
                        c.WF[:, 1024 * (nd2[0] % 2):1024 * (nd2[0] % 2 + 1)],
                        c.dr["eW2"][e, 128 * ft:128 * (ft + 1), :])
                dv3.append((wsem[nd2[0] % 2], wsem[nd2[0] % 2].n))
                nd2[0] += 1

    @block.tensor
    def _(t):
        t.wait_ge(c.s_ve.h, vb4)
        nm = 0
        for e in range(2):
            zb = 4 + 2 * e
            for ft in range(32):
                t.wait_ge(dv3[nm][0].h, dv3[nm][1])
                lhsT = UG[:, 1024 + 256 * e + 8 * ft: 1024 + 256 * e + 8 * (ft + 1)]
                t.matmul(bank(c, zb)[0:8, 0:512], lhsT,
                         c.WF[:, 1024 * (nm % 2):1024 * (nm % 2) + 512],
                         start=(ft == 0), stop=False)
                ins = t.matmul(bank(c, zb + 1)[0:8, 0:512], lhsT,
                               c.WF[:, 1024 * (nm % 2) + 512:1024 * (nm % 2 + 1)],
                               start=(ft == 0), stop=False)
                nm += 1
                if ft < 31:
                    c.s_te.inc(ins)
            t.matmul(bank(c, zb)[0:8, 0:512], c.ZS32[0:1, 0:8], c.RB[0:1, 0:512],
                     start=False, stop=True)
            ins = t.matmul(bank(c, zb + 1)[0:8, 0:512], c.ZS32[0:1, 0:8], c.RB[0:1, 0:512],
                           start=False, stop=True)
            c.s_te.inc(ins)

    tb3 = c.s_te.n

    @block.vector
    def _(v):
        v.wait_ge(c.s_te.h, tb3)
        v.tensor_copy(S1[0:8, :], c.PS[0:8, 512 * 4:512 * 4 + 1024])
        c.s_ve.inc(v.tensor_copy(S2[0:8, :], c.PS[0:8, 512 * 6:512 * 6 + 1024]))

    vb5 = c.s_ve.n
    gb1 = c.s_ge.n

    @block.gpsimd
    def _(g):
        g.wait_ge(c.s_ve.h, vb5)
        zi = c.agz_in.ap().rearrange("(b two) d -> b two d", two=2)
        dma_ser(g, c.s_ld, zi[:, 0, :], S1[0:8, :])
        dma_ser(g, c.s_ld, zi[:, 1, :], S2[0:8, :])
        g.wait_ge(c.s_ld.h, c.s_ld.n)
        from concourse.aot_env import allow_ici_all_to_all
        with allow_ici_all_to_all():
            c.s_cc.inc(g.collective_compute(
                "AllToAll", ALU.bypass, replica_groups=[list(range(NCORES))],
                ins=[c.agz_in.ap().opt()], outs=[c.agz_out.ap().opt()]))
        g.wait_ge(c.s_cc.h, c.s_cc.n)
        dma_ser(g, c.s_ld, S2, c.agz_out.ap()[0:16, :])
        dma_ser(g, c.s_ld, S1, c.dr["eb2"])
        g.wait_ge(c.s_ld.h, c.s_ld.n)
        c.s_ge.inc(g.nop())

    ge2 = c.s_ge.n

    @block.vector
    def _(v):
        v.wait_ge(c.s_ge.h, ge2)
        v.tensor_add(S2, S2, S1)
        c.s_ve.inc(v.tensor_add(SLo, SLo, S2))



def emit_l0ffn_cmha(c, block):
    nc = c.nc
    SLo = c.SL[0:16, 0:1024]
    S2 = c.SL[0:16, 2048:3072]
    vb0 = c.s_ve.n
    tb0 = c.s_te.n

    @block.tensor
    def _(t):
        t.wait_ge(c.s_ve.h, vb0)
        for k in range(NKC):
            ins = t.transpose(bank(c, 6)[:, 16 * k:16 * (k + 1)],
                              SLo[:, 128 * k:128 * (k + 1)], c.ID32[0:16, 0:16])
        c.s_te.inc(ins)

    @block.vector
    def _(v):
        v.wait_ge(c.s_te.h, tb0 + 1)
        c.s_ve.inc(v.tensor_copy(c.QSL[:, 128:256], bank(c, 6)[:, 0:128]))

    load_bias_row(c, block, c.dr["c_bk"], gates=((c.s_te, c.s_te.n),))
    vbT = c.s_ve.n

    def tm_out_slot_proj(wname, bias_rbb_off, zbank):
        tbp = c.s_te.n
        vcur = c.s_ve.n
        wr = stream_w_rot(c, block, c.dr[wname], NKC, 0, 1024,
                          c.s_te, lambda k: tbp + k + 1)

        @block.tensor
        def _(t):
            t.wait_ge(c.s_ve.h, vcur)
            for k in range(NKC):
                t.wait_ge(wr[k][0].h, wr[k][1])
                lhsT = c.QSL[:, 128 + 16 * k:128 + 16 * (k + 1)]
                t.matmul(bank(c, zbank)[0:16, 0:512], lhsT, c.WS[:, 1024 * (k % 2):1024 * (k % 2) + 512],
                         start=(k == 0), stop=False)
                ins = t.matmul(bank(c, zbank + 1)[0:16, 0:512], lhsT,
                               c.WS[:, 1024 * (k % 2) + 512:1024 * (k % 2 + 1)],
                               start=(k == 0), stop=False)
                c.s_te.inc(ins)
            t.matmul(bank(c, zbank)[0:16, 0:512], c.ONESb[0:1, 0:16],
                     c.RBb[0:1, bias_rbb_off:bias_rbb_off + 512], start=False, stop=True)
            ins = t.matmul(bank(c, zbank + 1)[0:16, 0:512], c.ONESb[0:1, 0:16],
                           c.RBb[0:1, bias_rbb_off + 512:bias_rbb_off + 1024],
                           start=False, stop=True)
            c.s_te.inc(ins)
        return c.s_te.n

    tK = tm_out_slot_proj("c_Wk", 0, 4)

    @block.vector
    def _(v):
        v.wait_ge(c.s_te.h, tK)
        c.s_ve.inc(v.tensor_copy(S2, c.PS[0:16, 512 * 4:512 * 4 + 1024]))

    vb1 = c.s_ve.n
    tb1 = c.s_te.n

    @block.tensor
    def _(t):
        t.wait_ge(c.s_ve.h, vb1)
        for k in range(NKC):
            ins = t.transpose(bank(c, 6)[:, 16 * k:16 * (k + 1)],
                              S2[:, 128 * k:128 * (k + 1)], c.ID32[0:16, 0:16])
        c.s_te.inc(ins)

    @block.vector
    def _(v):
        v.wait_ge(c.s_te.h, tb1 + 1)
        c.s_ve.inc(v.tensor_copy(c.QSL[:, 0:128], bank(c, 6)[:, 0:128]))

    load_bias_row(c, block, c.dr["c_bv"], gates=((c.s_te, c.s_te.n),))
    vbT2 = c.s_ve.n
    tV = tm_out_slot_proj("c_Wv", 0, 4)

    @block.vector
    def _(v):
        v.wait_ge(c.s_te.h, tV)
        dv = c.SL[:].bitcast(BF16)[0:16, 2048:3088].rearrange("p (h e) -> p h e", e=65)
        v.tensor_copy(dv[:, :, 0:64],
                      c.PS[0:16, 512 * 4:512 * 4 + 1024].rearrange("p (h e) -> p h e", e=64))
        c.s_ve.inc(v.memset(dv[:, :, 64:65], 1.0))

    load_bias_fm(c, block, c.dr["c_bq"], 3)
    for q in range(4):
        wr_q = stream_w_bf16(c, block, c.dr["c_Wq"][:, 256 * q:256 * (q + 1)], NKC, 0, wc=256,
                             gate0=(c.s_te, c.s_te.n))
        emit_proj_fm(c, block, c.A3[:, 0:4096], 0, 0, wr_q,
                     lambda k, lo, hi: c.XN[:, 2048 * k + lo:2048 * k + hi],
                     [2 * q, 2 * q + 1], bias_col=3, wslice=256)
        for jj in range(2):
            ch = 2 * q + jj
            tb = c.s_te.n
            sb = c.s_se.n
            vb = c.s_ve.n
            gb = c.s_ge.n

            @block.tensor
            def _(t, jj=jj, ch=ch, tb=tb, sb=sb, vb=vb):
                t.wait_ge(c.s_ve.h, vb)
                for w in range(4):
                    if w > 0:
                        t.wait_ge(c.s_ve.h, vb + 3 * w)
                    kA = c.QSL[0:64, 16 * ch:16 * (ch + 1)]
                    kB = c.QSL[64:128, 16 * ch:16 * (ch + 1)]
                    rA = c.A3[0:64, 2048 * jj + 512 * w:2048 * jj + 512 * (w + 1)]
                    rB = c.A3[64:128, 2048 * jj + 512 * w:2048 * jj + 512 * (w + 1)]
                    if w >= 1:
                        t.wait_ge(c.s_se.h, sb + w)
                    t.matmul(bank(c, 0)[0:16, 0:512], kA, rA, start=True, stop=True)
                    ins = t.matmul(bank(c, 1)[0:16, 0:512], kB, rB, start=True, stop=True)
                    c.s_te.inc(ins)
                    t.wait_ge(c.s_se.h, sb + w + 1)
                    for hh in range(2):
                        vsl = c.SL[:].bitcast(BF16)[0:16, 2048 + 65 * (2 * ch + hh):
                                                    2048 + 65 * (2 * ch + hh) + 65]
                        psl = c.PT[0:16, 512 * hh:512 * hh + 512]
                        ins = t.matmul(bank(c, 4 + hh)[0:65, 0:512], vsl, psl,
                                       start=True, stop=True)
                    c.s_te.inc(ins)
                    t.wait_ge(c.s_ve.h, vb + 3 * w + 1)
                    rr = c.EPI[0:1, 1024:2048]
                    t.matmul(bank(c, 6)[0:64, 0:512], c.ONES32[:, 0:64], rr[:, 0:512],
                             start=True, stop=True)
                    ins = t.matmul(bank(c, 7)[0:64, 0:512], c.ONES32[:, 0:64], rr[:, 512:1024],
                                   start=True, stop=True)
                    c.s_te.inc(ins)

            @block.scalar
            def _(s, tb=tb, sb=sb):
                for w in range(4):
                    s.wait_ge(c.s_te.h, tb + 3 * w + 1)
                    s.activation(c.PT[0:16, 0:512], bank(c, 0)[0:16, 0:512], AF.Exp, scale=0.125)
                    c.s_se.inc(s.activation(c.PT[0:16, 512:1024], bank(c, 1)[0:16, 0:512],
                                            AF.Exp, scale=0.125))

            @block.vector
            def _(v, ch=ch, tb=tb, vb=vb, gb=gb):
                for w in range(4):
                    v.wait_ge(c.s_te.h, tb + 3 * w + 2)
                    rr = c.EPI[0:1, 1024:2048]
                    c.s_ve.inc(v.reciprocal(rr, c.PS[64:65, 512 * 4:512 * 4 + 1024]))
                    v.wait_ge(c.s_te.h, tb + 3 * w + 3)
                    bc = c.EPI[0:64, 1024:2048]
                    c.s_ve.inc(v.tensor_copy(bc, c.PS[0:64, 512 * 6:512 * 6 + 1024]))
                    if gb + w > 0:
                        v.wait_ge(c.s_ge.h, gb + w)
                    c.s_ve.inc(v.tensor_mul(c.AMN[0:64, 0:1024],
                                            c.PS[0:64, 512 * 4:512 * 4 + 1024], bc))

            @block.gpsimd
            def _(g, ch=ch, vb=vb):
                for w in range(4):
                    g.wait_ge(c.s_ve.h, vb + 3 * w + 3)
                    dstA = c.A2[0:64, 2048 * ch + 512 * w:2048 * ch + 512 * (w + 1)]
                    dstB = c.A2[64:128, 2048 * ch + 512 * w:2048 * ch + 512 * (w + 1)]
                    dma_ser(g, c.s_st, dstA, c.AMN[0:64, 0:512])
                    dma_ser(g, c.s_st, dstB, c.AMN[0:64, 512:1024])
                    g.wait_ge(c.s_st.h, c.s_st.n)
                    c.s_ge.inc(g.nop())



def emit_l1ffn(c, block):
    nc = c.nc
    ldb = c.s_ld.n

    @block.gpsimd
    def _(g):
        dma_ser(g, c.s_ld, c.FSM[:, 0:16], c.dr["f_W1"])
        dma_ser(g, c.s_ld, c.FSM[:, 16:32], c.dr["f_b1"])
        dma_ser(g, c.s_ld, c.FSM[:, 32:48], c.dr["f_W2"])
        dma_ser(g, c.s_ld, c.FSM[:, 64:65], c.dr["f_bin"].rearrange("(d a) -> d a", a=1))
        dma_ser(g, c.s_ld, c.FSM[:, 65:66], c.dr["f_b2"].rearrange("(d a) -> d a", a=1))
        gate_wf(g, c)
        dma_ser(g, c.s_ld, c.WF[0:64, 0:1024], c.dr["f_Wout"])

    load_bias_row(c, block, c.dr["f_bout"], gates=((c.s_te, c.s_te.n),))

    @block.vector
    def _(v):
        v.wait_ge(c.s_ld.h, ldb + 96)
        v.scalar_tensor_tensor(c.FSM[:, 48:64], c.FSM[:, 0:16], c.FSM[:, 64:65],
                               c.FSM[:, 16:32], ALU.mult, ALU.add)
        c.s_ve.inc(v.tensor_copy(c.WS[0:64, 0:1024], c.WF[0:64, 0:1024]))

    vbW = c.s_ve.n

    @block.gpsimd
    def _(g):
        g.wait_ge(c.s_ve.h, vbW)
        dma_ser(g, c.s_ld, c.WS[64:65, 0:1024], c.RBb[0:1, 0:1024])

    ldW = c.s_ld.n

    wr_in = stream_w_bf16(c, block, c.dr["f_Win"], NKC, 1024, wc=64)

    tb0 = c.s_te.n
    vb0 = c.s_ve.n

    @block.tensor
    def _(t):
        t.wait_ge(c.s_ve.h, vb0)
        for k in range(NKC):
            t.wait_ge(wr_in[k][0].h, wr_in[k][1])
            lhsT = c.WS[:, 1024 + 64 * k:1024 + 64 * (k + 1)]
            for w in range(4):
                t.matmul(c.PS[0:64, 512 * w:512 * (w + 1)], lhsT,
                         c.XN[:, 2048 * k + 512 * w:2048 * k + 512 * (w + 1)],
                         start=(k == 0), stop=False)
        for w in range(4):
            ins = t.matmul(c.PS[0:64, 512 * w:512 * (w + 1)], c.ZSB[0:1, 0:64],
                           c.RBb[0:1, 0:512], start=False, stop=True)
        c.s_te.inc(ins)

    sb0 = c.s_se.n
    vb1 = c.s_ve.n

    @block.scalar
    def _(s):
        s.wait_ge(c.s_te.h, tb0 + 1)
        for i in range(16):
            if i >= 1:
                s.wait_ge(c.s_ve.h, vb1 + i)
            ins = s.activation(c.EPI[0:64, 0:2048], c.PS[0:64, 0:2048], AF.Relu,
                               scale=c.FSM[:, i:i + 1], bias=c.FSM[:, 48 + i:48 + i + 1])
            c.s_se.inc(ins)

    gA3 = c.A3[:].bitcast(F32)[0:64, 0:2048]
    GBB = c.A3[0:65, 4096:6144]

    @block.vector
    def _(v):
        for i in range(16):
            v.wait_ge(c.s_se.h, sb0 + i + 1)
            if i == 0:
                ins = v.tensor_scalar(gA3, c.EPI[0:64, 0:2048],
                                      c.FSM[:, 32:33], None, ALU.mult)
            else:
                ins = v.scalar_tensor_tensor(gA3, c.EPI[0:64, 0:2048],
                                             c.FSM[:, 32 + i:32 + i + 1],
                                             gA3, ALU.mult, ALU.add)
            c.s_ve.inc(ins)
        v.tensor_scalar(GBB[0:64, :], gA3, c.FSM[:, 65:66], None, ALU.add)
        c.s_ve.inc(v.memset(GBB[64:65, :], 1.0))

    vbG = c.s_ve.n

    GBB2 = c.A3[0:65, 4096:6144]

    def tile_mms(t, j, b0):
        if j == 0:
            t.wait_ge(c.s_ld.h, ldW)
        lhsT = GBB2[:, 128 * j:128 * (j + 1)]
        t.matmul(bank(c, b0)[:, 0:512], lhsT, c.WS[0:65, 0:512], start=True, stop=True)
        return t.matmul(bank(c, b0 + 1)[:, 0:512], lhsT, c.WS[0:65, 512:1024],
                        start=True, stop=True)

    emit_tmout_sandwich(c, block, tile_mms, (c.s_ve, vbG), c.STATS[:, 112:128])


def emit_tmout_sandwich(c, block, tile_mms, gate, sums_dst):
    nc = c.nc
    tb = c.s_te.n
    vb = c.s_ve.n
    sb = c.s_se.n
    negm2 = c.STATS[:, 96:97]
    r2 = c.STATS[:, 97:98]
    v1t = c.STATS[:, 98:99]
    v2t = c.STATS[:, 99:100]
    ssq = c.STATS[:, 100:101]
    ssm = c.STATS[:, 101:102]
    junk = c.EPI[:, 0:1024]
    zn = c.EPI[:, 1024:2048]

    @block.tensor
    def _(t):
        if gate is not None:
            t.wait_ge(gate[0].h, gate[1])
        if vb:
            t.wait_ge(c.s_ve.h, vb)
        for j in range(NT):
            b0 = 0 if j % 2 == 0 else 2
            if j >= 2:
                t.wait_ge(c.s_ve.h, vb + 4 * (j - 1))
            ins = tile_mms(t, j, b0)
            c.s_te.inc(ins)

    lvl = OP_LVL

    @block.vector
    def _(v):
        for j in range(NT):
            b0 = 0 if j % 2 == 0 else 2
            pz = c.PS[:, 512 * b0:512 * b0 + 1024]
            v.wait_ge(c.s_te.h, tb + j + 1)
            if lvl >= 2:
                c.s_ve.inc(v.reduce_sum(ssm, pz, axis=AX))
            else:
                c.s_ve.inc(v.tensor_copy(zn, pz[:, 0:1024]))
            if lvl >= 2:
                v.wait_ge(c.s_se.h, sb + 3 * j + 1)
            v.tensor_scalar(negm2, ssm, -1.0 / D, None, ALU.mult)
            v.tensor_mul(v2t, negm2, negm2)
            v.tensor_scalar(v1t, ssq, 1.0 / D, None, ALU.mult)
            c.s_ve.inc(v.tensor_sub(v1t, v1t, v2t))
            if lvl >= 3:
                v.wait_ge(c.s_se.h, sb + 3 * j + 3)
                v.tensor_scalar(zn, pz, negm2, r2, ALU.add, ALU.mult)
            if lvl >= 4:
                ins = v.scalar_tensor_tensor(
                    xtile(c, j), zn, 1.0, xtile(c, j), ALU.mult, ALU.add,
                    accum_out=(sums_dst[:, j:j + 1] if sums_dst is not None else None))
                c.s_ve.inc(ins)
            else:
                c.s_ve.inc(v.nop())
            v.wait_ge(c.s_ve.h, c.s_ve.n)
            c.s_ve.inc(v.nop())

    @block.scalar
    def _(s):
        for j in range(NT):
            b0 = 0 if j % 2 == 0 else 2
            pz = c.PS[:, 512 * b0:512 * b0 + 1024]
            s.wait_ge(c.s_te.h, tb + j + 1)
            s.wait_ge(c.s_ve.h, vb + 4 * j + 1)
            if lvl >= 2:
                c.s_se.inc(s.activation(junk, pz, AF.Square, accum_out=ssq))
            else:
                c.s_se.inc(s.nop())
            if lvl >= 3:
                s.wait_ge(c.s_ve.h, vb + 4 * j + 2)
                c.s_se.inc(s.activation(v1t, v1t, AF.Ln, bias=c.EPSC[:]))
                c.s_se.inc(s.activation(r2, v1t, AF.Exp, scale=-0.5))
            else:
                c.s_se.inc(s.nop())
                c.s_se.inc(s.nop())



def emit_l0_attn(c, block, upto=4):
    load_bias_fm(c, block, c.dr["l0_bq"], 0)
    load_bias_fm(c, block, c.dr["l0_bk"], 1)
    load_bias_row(c, block, c.dr["l0_bv"])
    xn = lambda k, lo, hi: c.XN[:, 2048 * k + lo:2048 * k + hi]
    for q in range(4):
        wrq = stream_w_bf16(c, block, c.dr["l0_WqT"][:, 256 * q:256 * (q + 1)], NKC, 0,
                            wc=256, gate0=(c.s_te, c.s_te.n))
        wrk = stream_w_bf16(c, block, c.dr["l0_WkT"][:, 256 * q:256 * (q + 1)], NKC, 2048, wc=256)
        wrv = stream_w_bf16(c, block, c.dr["l0_WvT"][:, 256 * q:256 * (q + 1)], NKC, 4096, wc=256)
        emit_proj_fm(c, block, c.A3[:, 0:4096], 0, 0, wrq, xn, [2 * q, 2 * q + 1], bias_col=0)
        emit_proj_fm(c, block, c.A3[:, 4096:8192], 0, 2048, wrk, xn, [2 * q, 2 * q + 1], bias_col=1)
        emit_vprime_proj(c, block, c.A3[:, 8192:12352], 4, 4096, wrv, xn, 256,
                         None, 256 * q, zb=0)
        if upto >= 3:
            emit_attention_quarter(c, block, q, 0.125, c.A2)
    if upto >= 4:
        load_bias_row(c, block, c.dr["l0_bo"], gates=((c.s_te, c.s_te.n),))
        wro = stream_w_bf16(c, block, c.dr["l0_WoT"], NKC, 0, wc=1024,
                            gate0=(c.s_te, c.s_te.n), dst=c.A3)
        if OP_LVL >= 1:
            emit_oproj_sandwich(c, block, c.A2, wro, 0, c.STATS[:, 112:128],
                                gate=(c.s_ge, c.s_ge.n))


def emit_l1_attn(c, block):
    @block.vector
    def _(v):
        c.s_ve.inc(v.memset(c.RBb[0:1, 0:1024], 0.0))

    xn = lambda k, lo, hi: c.XN[:, 2048 * k + lo:2048 * k + hi]
    for q in range(4):
        wrq = stream_masked_w(c, block, c.dr["l1_qkvT"][:, 256 * q:256 * (q + 1)],
                              c.dr["l1_qkvmT"][:, 256 * q:256 * (q + 1)], NKC, 0, 256,
                              gate0=(c.s_te, c.s_te.n))
        wrk = stream_masked_w(c, block, c.dr["l1_qkvT"][:, 1024 + 256 * q:1024 + 256 * (q + 1)],
                              c.dr["l1_qkvmT"][:, 1024 + 256 * q:1024 + 256 * (q + 1)],
                              NKC, 2048, 256)
        wrv = stream_masked_w(c, block, c.dr["l1_qkvT"][:, 2048 + 256 * q:2048 + 256 * (q + 1)],
                              c.dr["l1_qkvmT"][:, 2048 + 256 * q:2048 + 256 * (q + 1)],
                              NKC, 4096, 256)
        emit_proj_fm(c, block, c.A3[:, 0:4096], 0, 0, wrq, xn, [2 * q, 2 * q + 1], bias_col=None)
        emit_proj_fm(c, block, c.A3[:, 4096:8192], 0, 2048, wrk, xn, [2 * q, 2 * q + 1], bias_col=None)
        emit_vprime_proj(c, block, c.A3[:, 8192:12352], 4, 4096, wrv, xn, 256,
                         None, 256 * q, zb=0)
        emit_attention_quarter(c, block, q, 1.0 / 32, c.A2)
    wro = stream_masked_w(c, block, c.dr["l1_outT"], c.dr["l1_outmT"], NKC, 0, 1024,
                          gate0=(c.s_te, c.s_te.n), dst=c.A3)
    teb = c.s_te.n
    ldb = c.s_ld.n

    @block.gpsimd
    def _(g):
        g.wait_ge(c.s_te.h, teb)
        dma_ser(g, c.s_ld, c.RB[0:1, 0:1024], c.dr["l1_ob"].rearrange("(a d) -> a d", a=1))
        dma_ser(g, c.s_ld, c.EPI[0:1, 0:1024], c.dr["l1_obm"].rearrange("(a d) -> a d", a=1))
        g.wait_ge(c.s_ld.h, c.s_ld.n)
        g.tensor_scalar(c.EPI[0:1, 0:1024], c.EPI[0:1, 0:1024], 0.0, None, ALU.is_gt)
        c.s_ge.inc(g.tensor_tensor(c.RBb[0:1, 0:1024], c.EPI[0:1, 0:1024],
                                   c.RB[0:1, 0:1024], ALU.mult))

    emit_oproj_sandwich(c, block, c.A2, wro, 0, c.STATS[:, 112:128],
                        gate=(c.s_ge, c.s_ge.n))



def emit_all(c, block):
    emit_init(c, block)

    ph = 99 if DEV_PHASE is None else DEV_PHASE
    if ph >= 1:
        negm, r = emit_ln_stats(c, block, lambda j: xtile(c, j), wait_sems=(c.s_ld,))
        emit_ln_apply_transpose(c, block, lambda j: xtile(c, j), negm, r, c.XN, c.A2)
    if ph >= 2:
        emit_l0_attn(c, block, upto=min(ph, 4))
    if N_SUB >= 2:
        negm, r = emit_ln_stats(c, block, lambda j: xtile(c, j),
                                have_sums=c.STATS[:, 112:128], wait_sems=(c.s_ve,))
        emit_ln_apply_transpose(c, block, lambda j: xtile(c, j), negm, r, c.XN, c.A2)
        emit_l0ffn_slots(c, block)
        if DEBUG_DUMPS:
            vdump = c.s_ve.n

            @block.gpsimd
            def _(g):
                g.wait_ge(c.s_ve.h, vdump)
                dma_ser(g, c.s_st, c.dbg_o, c.SL[0:16, 0:1024])
                g.wait_ge(c.s_st.h, c.s_st.n)
                c.s_ge.inc(g.nop())
        emit_l0ffn_experts(c, block)
        if DEBUG_DUMPS:
            vdump2 = c.s_ve.n

            @block.gpsimd
            def _(g):
                g.wait_ge(c.s_ve.h, vdump2)
                dma_ser(g, c.s_st, c.dbg_of, c.SL[0:16, 0:1024])
                dma_ser(g, c.s_st, c.dbg_h, c.SL[0:16, 2048:3072])
                g.wait_ge(c.s_st.h, c.s_st.n)
                c.s_ge.inc(g.nop())
        emit_l0ffn_cmha(c, block)
        if DEBUG_DUMPS:
            vdump3 = c.s_ge.n

            @block.gpsimd
            def _(g):
                g.wait_ge(c.s_ge.h, vdump3)
                dma_ser(g, c.s_st, c.dbg_a2, c.A2[:])
                dma_ser(g, c.s_st, c.dbg_ks, c.QSL[:])
                g.wait_ge(c.s_st.h, c.s_st.n)
                c.s_ge.inc(g.nop())
        load_bias_row(c, block, c.dr["c_bo"], gates=((c.s_te, c.s_te.n),))
        wr_co = stream_w_bf16(c, block, c.dr["c_Wo"], NKC, 0, wc=1024,
                              gate0=(c.s_te, c.s_te.n), dst=c.A3)
        emit_oproj_sandwich(c, block, c.A2, wr_co, 0, c.STATS[:, 112:128],
                            gate=(c.s_ge, c.s_ge.n))
    if N_SUB >= 3:
        negm, r = emit_ln_stats(c, block, lambda j: xtile(c, j),
                                have_sums=c.STATS[:, 112:128], wait_sems=(c.s_ve,))
        emit_ln_apply_transpose(c, block, lambda j: xtile(c, j), negm, r, c.XN, c.A2)
        emit_l1_attn(c, block)
    if N_SUB >= 4:
        negm, r = emit_ln_stats(c, block, lambda j: xtile(c, j),
                                have_sums=c.STATS[:, 112:128], wait_sems=(c.s_ve,))
        emit_ln_apply_transpose(c, block, lambda j: xtile(c, j), negm, r, c.XN, c.A2)
        emit_l1ffn(c, block)

    vfin = c.s_ve.n

    @block.gpsimd
    def _(g):
        g.wait_ge(c.s_ve.h, vfin)
        for j in range(NT):
            dma_ser(g, c.s_st, c.out[128 * j:128 * (j + 1), :], xtile(c, j))
        g.wait_ge(c.s_st.h, c.s_st.n)


def build_program():
    nc, c = build_nc()
    with nc.Block() as block:
        emit_all(c, wrap_block(block, c))
    return nc, c



def _prep_inputs(inputs):
    f = np.ascontiguousarray
    g = {}
    g["l0_WqT"] = f(inputs["L0_attn_Wq"].T)
    g["l0_WkT"] = f(inputs["L0_attn_Wk"].T)
    g["l0_WvT"] = f(inputs["L0_attn_Wv"].T)
    g["l0_WoT"] = f(inputs["L0_attn_Wo"].T)
    g["l0_bq"] = inputs["L0_attn_bq"]
    g["l0_bk"] = inputs["L0_attn_bk"]
    g["l0_bv"] = inputs["L0_attn_bv"]
    g["l0_bo"] = inputs["L0_attn_bo"]
    g["sq"] = f(inputs["L0_ffn_slot_query"][0])
    for p in ("d", "c"):
        for m in ("Wq", "Wk", "Wv", "Wo"):
            g[f"{p}_{m}"] = f(inputs[f"L0_ffn_{p}_{m}"])
        for m in ("bq", "bk", "bv", "bo"):
            g[f"{p}_{m}"] = inputs[f"L0_ffn_{p}_{m}"]
    g["eb2"] = inputs["L0_ffn_e_b2"]
    g["l1_qkvT"] = f(inputs["L1_attn_qkv_weight"].T)
    g["l1_qkvmT"] = f(inputs["L1_attn_qkv_weight_mask"].T)
    g["l1_outT"] = f(inputs["L1_attn_out_weight"].T)
    g["l1_outmT"] = f(inputs["L1_attn_out_weight_mask"].T)
    g["l1_ob"] = inputs["L1_attn_out_bias"]
    g["l1_obm"] = inputs["L1_attn_out_bias_mask"]
    g["f_Win"] = inputs["L1_ffn_Win"]
    g["f_bin"] = inputs["L1_ffn_bin_"]
    g["f_W1"] = inputs["L1_ffn_W1"]
    g["f_b1"] = inputs["L1_ffn_b1"]
    g["f_W2"] = inputs["L1_ffn_W2"]
    g["f_b2"] = inputs["L1_ffn_b2"]
    g["f_Wout"] = inputs["L1_ffn_Wout"]
    g["f_bout"] = inputs["L1_ffn_bout"]
    y = inputs["L0_attn_y"]
    in_maps = []
    for cid in range(NCORES):
        m = dict(g)
        m["y"] = f(y[cid])
        m["e_ln_g"] = f(inputs["L0_ffn_e_ln_g"][2 * cid:2 * cid + 2])
        m["e_ln_b"] = f(inputs["L0_ffn_e_ln_b"][2 * cid:2 * cid + 2])
        m["eW1"] = f(inputs["L0_ffn_e_W1"][2 * cid:2 * cid + 2])
        m["eWg"] = f(inputs["L0_ffn_e_Wg"][2 * cid:2 * cid + 2])
        m["eW2"] = f(inputs["L0_ffn_e_W2"][2 * cid:2 * cid + 2])
        m["eb1"] = f(inputs["L0_ffn_e_b1"][2 * cid:2 * cid + 2])
        m["ebg"] = f(inputs["L0_ffn_e_bg"][2 * cid:2 * cid + 2])
        in_maps.append(m)
    return in_maps


def kernel(**inputs):
    in_maps = _prep_inputs(inputs)
    nc, _ = build_program()
    res = run_bass_kernel_spmd(nc, in_maps, core_ids=list(range(NCORES)))
    return np.stack([res.results[cid]["out"] for cid in range(NCORES)], axis=0)


# version 06
# speedup vs baseline: 1.1634x; 1.0749x over previous
import numpy as np
import concourse.bass as bass
import concourse.mybir as mybir
from concourse.bass_utils import run_bass_kernel_spmd

F32 = mybir.dt.float32
BF16 = mybir.dt.bfloat16
AF = mybir.ActivationFunctionType
ALU = mybir.AluOpType
AX = mybir.AxisListType.X

NCORES = 8
T = 2048
D = 1024
H = 16
DH = 64
NT = T // 128
NKC = D // 128
EPS = 1e-5

N_SUB = 4
DEBUG_DUMPS = False
SELF_EDGES = "v"
DEV_PHASE = None
OP_LVL = 4


_PENDING = {}
_SEEN = {}


def traceback_stack():
    import traceback
    return traceback.format_stack()[-6:-2]



class Sem:
    def __init__(self, nc, name):
        self.h = nc.alloc_semaphore(name)
        self.n = 0

    def inc(self, instr, k=1):
        key = id(instr)
        if key in _SEEN and _SEEN[key][0] is self and _SEEN[key][2] is instr:
            import traceback
            raise RuntimeError("double Sem.inc on same instruction: first at\n"
                               + "".join(_SEEN[key][1]))
        _SEEN[key] = (self, traceback_stack(), instr)
        instr.then_inc(self.h, k)
        self.n += k
        w = _PENDING.pop(id(instr), None)
        if w is not None:
            w._last = (self, self.n)
            w._pending = None
        return instr

    def wait(self, eng):
        if self.n:
            eng.wait_ge(self.h, self.n)


class Ctx:
    pass


def build_nc():
    nc = bass.Bass("TRN2", target_bir_lowering=False, debug=False)
    c = Ctx()
    c.nc = nc

    dr = {}

    def din(name, shape, dt=F32):
        dr[name] = nc.dram_tensor(name, list(shape), dt, kind="ExternalInput").ap()
        return dr[name]

    din("y", (T, D))
    din("l0_WqT", (D, D)); din("l0_WkT", (D, D)); din("l0_WvT", (D, D)); din("l0_WoT", (D, D))
    din("l0_bq", (D,)); din("l0_bk", (D,)); din("l0_bv", (D,)); din("l0_bo", (D,))
    din("sq", (16, D))
    for p in ("d", "c"):
        for m in ("Wq", "Wk", "Wv", "Wo"):
            din(f"{p}_{m}", (D, D))
        for m in ("bq", "bk", "bv", "bo"):
            din(f"{p}_{m}", (D,))
    din("e_ln_g", (2, D)); din("e_ln_b", (2, D))
    din("eW1", (2, D, 4096)); din("eWg", (2, D, 4096)); din("eW2", (2, 4096, D))
    din("eb1", (2, 4096)); din("ebg", (2, 4096)); din("eb2", (16, D))
    din("l1_qkvT", (D, 3 * D)); din("l1_qkvmT", (D, 3 * D))
    din("l1_outT", (D, D)); din("l1_outmT", (D, D))
    din("l1_ob", (D,)); din("l1_obm", (D,))
    din("f_Win", (D, 64)); din("f_bin", (64,))
    din("f_W1", (64, 16)); din("f_b1", (64, 16)); din("f_W2", (64, 16)); din("f_b2", (64,))
    din("f_Wout", (64, D)); din("f_bout", (D,))
    c.dr = dr

    c.out = nc.dram_tensor("out", [T, D], F32, kind="ExternalOutput").ap()
    if DEBUG_DUMPS:
        c.dbg_o = nc.dram_tensor("dbg_o", [16, D], F32, kind="ExternalOutput").ap()
        c.dbg_of = nc.dram_tensor("dbg_of", [16, D], F32, kind="ExternalOutput").ap()
        c.dbg_h = nc.dram_tensor("dbg_h", [16, D], F32, kind="ExternalOutput").ap()
        c.dbg_a2 = nc.dram_tensor("dbg_a2", [128, 16384], BF16, kind="ExternalOutput").ap()
        c.dbg_ks = nc.dram_tensor("dbg_ks", [128, 256], BF16, kind="ExternalOutput").ap()

    c.ag1_in = nc.dram_tensor("ag1_in", [16, D], F32)
    c.ag1_out = nc.dram_tensor("ag1_out", [16, D], F32)
    c.agz_in = nc.dram_tensor("agz_in", [16, D], F32)
    c.agz_out = nc.dram_tensor("agz_out", [16, D], F32)

    A = nc.alloc_sbuf_tensor
    c.X = A("X", [128, NT * D], F32)
    c.XN = A("XN", [128, NKC * T], BF16)
    c.A2 = A("A2", [128, NKC * T], BF16)
    c.A3 = A("A3", [128, 12352], BF16)
    c.WS = A("WS", [128, 6144], BF16)
    c.WF = A("WF", [128, 2048], F32)
    c.PT = A("PT", [128, 2048], BF16)
    c.EPI = A("EPI", [128, 2048], F32)
    c.STATS = A("STATS", [128, 128], F32)
    c.RB = A("RB", [1, 1024], F32)
    c.RBb = A("RBb", [1, 2048], BF16)
    c.ID32 = A("ID32", [16, 16], F32)
    c.IDb = A("IDb", [128, 128], BF16)
    c.ONESb = A("ONESb", [1, 128], BF16)
    c.ONES32 = A("ONES32", [1, 64], F32)
    c.SL = A("SL", [16, 3072], F32)
    c.HT = A("HT", [128, 128], F32)
    c.BSM = A("BSM", [128, 32], F32)
    c.QSL = A("QSL", [128, 256], BF16)
    c.FSM = A("FSM", [64, 96], F32)
    c.EPSC = A("EPSC", [128, 1], F32)
    c.ZSB = A("ZSB", [1, 128], BF16)
    c.ZS32 = A("ZS32", [1, 16], F32)
    c.AMN = c.EPI[:].bitcast(BF16)

    c.PS = nc.alloc_psum_tensor("PS", [128, 4096], F32)
    c.PSb = c.PS[:].bitcast(BF16)

    c.s_ld = Sem(nc, "s_ld")
    c.s_wt = Sem(nc, "s_wt")
    c.s_wu = Sem(nc, "s_wu")
    c.s_te = Sem(nc, "s_te")
    c.s_ve = Sem(nc, "s_ve")
    c.s_se = Sem(nc, "s_se")
    c.s_ge = Sem(nc, "s_ge")
    c.s_st = Sem(nc, "s_st")
    c.s_cc = Sem(nc, "s_cc")
    c.s_vv = Sem(nc, "s_vv")
    c.s_ss = Sem(nc, "s_ss")
    c.s_gg = Sem(nc, "s_gg")
    c.wv = EngWrap(c.s_vv, "v")
    c.ws_ = EngWrap(c.s_ss, "s")
    c.wg = EngWrap(c.s_gg, "g")
    return nc, c


_WRAP_OPS = {
    "tensor_copy", "tensor_tensor", "tensor_add", "tensor_sub", "tensor_mul",
    "tensor_max", "tensor_scalar", "tensor_single_scalar", "tensor_scalar_add",
    "tensor_scalar_sub", "tensor_scalar_mul", "tensor_scalar_max",
    "tensor_scalar_min", "tensor_relu", "tensor_reduce", "reduce_sum",
    "reduce_max", "scalar_tensor_tensor", "tensor_tensor_scan", "memset",
    "memzero", "reciprocal", "select", "bn_stats", "bn_aggr", "iota",
    "affine_select", "activation", "sign", "square", "sqrt", "mul", "add",
    "copy", "transpose",
}


class EngWrap:

    def __init__(self, sem, tag):
        self._e = None
        self._s = sem
        self._tag = tag
        self._pending = None
        self._last = None

    def _flush(self):
        p = self._pending
        if p is not None and id(p) in _PENDING:
            _PENDING.pop(id(p))
            p.then_inc(self._s.h, 1)
            self._s.n += 1
            self._last = (self._s, self._s.n)
        self._pending = None

    def __getattr__(self, name):
        fn = getattr(self._e, name)
        if name not in _WRAP_OPS or self._tag not in SELF_EDGES:
            return fn

        def call(*a, **k):
            self._flush()
            if self._last is not None:
                self._e.wait_ge(self._last[0].h, self._last[1])
            ins = fn(*a, **k)
            self._pending = ins
            _PENDING[id(ins)] = self
            return ins
        return call


def wrap_block(block, c):
    class B:
        pass
    b = B()

    def mk(section, w):
        def deco(f):
            def g(eng):
                w._e = eng
                return f(w)
            return section(g)
        return deco
    b.vector = mk(block.vector, c.wv)
    b.scalar = mk(block.scalar, c.ws_)
    b.gpsimd = mk(block.gpsimd, c.wg)
    b.tensor = block.tensor
    b.sync = block.sync
    return b


def gate_wf(eng, c):
    if c.s_ve.n:
        eng.wait_ge(c.s_ve.h, c.s_ve.n)
    if c.s_te.n:
        eng.wait_ge(c.s_te.h, c.s_te.n)
    if c.s_ge.n:
        eng.wait_ge(c.s_ge.h, c.s_ge.n)


def dma_ser(eng, sem, out, in_, **kw):
    if sem.n:
        eng.wait_ge(sem.h, sem.n)
    return sem.inc(eng.dma_start(out=out, in_=in_, **kw), 16)


def bank(c, b, n=1):
    return c.PS[:, 512 * b: 512 * (b + n)]


def xtile(c, j):
    return c.X[:, 1024 * j: 1024 * (j + 1)]


def emit_init(c, block):
    nc = c.nc

    @block.gpsimd
    def _(g):
        for j in range(NT):
            dma_ser(g, c.s_ld, xtile(c, j), c.dr["y"][128 * j:128 * (j + 1), :])
        c.s_ge.inc(g.memset(c.ID32[:], 0.0))
        c.s_ge.inc(g.memset(c.IDb[:], 0.0))
        g.wait_ge(c.s_ge.h, c.s_ge.n)
        c.s_ge.inc(g.affine_select(out=c.ID32[:], in_=c.ID32[:], compare_op=ALU.not_equal,
                                   fill=1.0, base=0, pattern=[[-1, 16]], channel_multiplier=1))
        c.s_ge.inc(g.affine_select(out=c.IDb[:], in_=c.IDb[:], compare_op=ALU.not_equal,
                                   fill=1.0, base=0, pattern=[[-1, 128]], channel_multiplier=1))
        g.wait_ge(c.s_ge.h, c.s_ge.n)
        c.s_ge.inc(g.nop())

    @block.vector
    def _(v):
        v.memset(c.ONESb[:], 1.0)
        v.memset(c.ONES32[0:1, 0:64], 1.0)
        v.memset(c.EPSC[:], EPS)
        v.memset(c.ZSB[:], 0.0)
        v.memset(c.ZS32[:], 0.0)
        v.memset(c.RBb[0:1, :], 0.0)
        v.memset(c.RB[0:1, :], 0.0)
        c.s_ve.inc(v.nop())

    @block.scalar
    def _(s):
        s.add_instruction(mybir.InstLoadActFuncSet(
            name=nc.get_next_instruction_name(), act_func_set_id=6, ins=[], outs=[]))
        c.s_se.inc(s.nop())

    @block.tensor
    def _(t):
        t.wait_ge(c.s_ge.h, c.s_ge.n)


def emit_ln_stats(c, block, src_tiles, have_sums=None, wait_sems=()):
    nc = c.nc
    ssum = c.STATS[:, 32:48]
    ssq = c.STATS[:, 48:64]
    negm = c.STATS[:, 0:16]
    r = c.STATS[:, 16:32]
    t1 = c.STATS[:, 64:80]
    t2 = c.STATS[:, 80:96]
    junk = c.EPI[:, 0:1024]

    if have_sums is None:
        @block.vector
        def _(v):
            for s in wait_sems:
                s.wait(v)
            for j in range(NT):
                ins = v.reduce_sum(ssum[:, j:j + 1], src_tiles(j), axis=AX)
            c.s_ve.inc(ins)

    @block.scalar
    def _(s):
        for sm in wait_sems:
            sm.wait(s)
        for j in range(NT):
            if j:
                s.wait_ge(c.s_se.h, c.s_se.n)
            c.s_se.inc(s.activation(junk[:], src_tiles(j), AF.Square, accum_out=ssq[:, j:j + 1]))

    sums = have_sums if have_sums is not None else ssum

    @block.vector
    def _(v):
        v.wait_ge(c.s_ve.h, c.s_ve.n)
        c.s_se.wait(v)
        v.tensor_scalar(negm[:], sums, -1.0 / D, None, ALU.mult)
        v.tensor_mul(t2[:], negm[:], negm[:])
        v.tensor_scalar(t1[:], ssq[:], 1.0 / D, None, ALU.mult)
        c.s_ve.inc(v.tensor_sub(t1[:], t1[:], t2[:]))

    @block.scalar
    def _(s):
        c.s_ve.wait(s)
        s.activation(t1[:], t1[:], AF.Ln, bias=c.EPSC[:])
        c.s_se.inc(s.activation(r[:], t1[:], AF.Exp, scale=-0.5))
    return negm, r


def emit_ln_apply_transpose(c, block, src_tiles, negm, r, dst_fm, stage):
    nc = c.nc
    vb = c.s_ve.n
    tb = c.s_te.n

    @block.vector
    def _(v):
        c.s_se.wait(v)
        for j in range(NT):
            ins = v.tensor_scalar(stage[:, 1024 * j:1024 * (j + 1)], src_tiles(j),
                                  negm[:, j:j + 1], r[:, j:j + 1], ALU.add, ALU.mult)
            c.s_ve.inc(ins)

    @block.tensor
    def _(t):
        gi = 0
        for j in range(NT):
            t.wait_ge(c.s_ve.h, vb + j + 1)
            for g in range(2):
                b = 6 + gi % 2
                if gi >= 2:
                    t.wait_ge(c.s_ve.h, vb + 16 + (gi - 1))
                for kk in range(4):
                    k = 4 * g + kk
                    ins = t.transpose(c.PSb[:, 1024 * b + 128 * kk:1024 * b + 128 * (kk + 1)],
                                      stage[:, 1024 * j + 128 * k: 1024 * j + 128 * (k + 1)],
                                      c.IDb[:])
                c.s_te.inc(ins)
                gi += 1

    @block.vector
    def _(v):
        gi = 0
        vw = dst_fm.rearrange("p (k t) -> p k t", t=T)
        for j in range(NT):
            for g in range(2):
                b = 6 + gi % 2
                v.wait_ge(c.s_te.h, tb + gi + 1)
                dst = vw[:, 4 * g:4 * g + 4, 128 * j:128 * (j + 1)]
                c.s_ve.inc(v.tensor_copy(dst, c.PSb[:, 1024 * b:1024 * b + 512]
                                         .rearrange("p (a t) -> p a t", a=4)))
                gi += 1


def stream_w_bf16(c, block, dram_ap, n_chunks, ws_off, wc=1024, gate0=None, dst=None):
    nc = c.nc
    vb = c.s_ve.n
    wsem = [c.s_wt, c.s_wu]
    dvals = []

    @block.sync
    def _(sy):
        for k in range(n_chunks):
            if k >= 2:
                sy.wait_ge(c.s_ve.h, vb + (k - 1))
            else:
                gate_wf(sy, c)
            dma_ser(sy, wsem[k % 2], c.WF[:, 1024 * (k % 2):1024 * (k % 2) + wc],
                    dram_ap[128 * k:128 * (k + 1), :])
            dvals.append((wsem[k % 2], wsem[k % 2].n))

    ready = []

    @block.vector
    def _(v):
        for k in range(n_chunks):
            v.wait_ge(dvals[k][0].h, dvals[k][1])
            if k == 0 and gate0 is not None:
                v.wait_ge(gate0[0].h, gate0[1])
            dd = dst if dst is not None else c.WS
            ins = v.tensor_copy(dd[:, ws_off + wc * k: ws_off + wc * (k + 1)],
                                c.WF[:, 1024 * (k % 2):1024 * (k % 2) + wc])
            c.s_ve.inc(ins)
            ready.append((c.s_ve, c.s_ve.n))
    return ready


def load_bias_fm(c, block, dram_ap, col):
    @block.gpsimd
    def _(g):
        vw = dram_ap.rearrange("(k p) -> p k", p=128)
        dma_ser(g, c.s_ld, c.BSM[:, 8 * col:8 * col + 8], vw,
                allow_slow_non_contiguous=True)


def load_bias_row(c, block, dram_ap, slot=0, length=D, make_bf=True, gates=()):
    vb = c.s_ve.n

    tb = c.s_te.n

    @block.gpsimd
    def _(g):
        if vb:
            g.wait_ge(c.s_ve.h, vb)
        if tb:
            g.wait_ge(c.s_te.h, tb)
        for sm, val in gates:
            if val:
                g.wait_ge(sm.h, val)
        src = dram_ap.rearrange("(a d) -> a d", a=1) if len(dram_ap.shape) == 1 else dram_ap
        dma_ser(g, c.s_ld, c.RB[0:1, 0:length], src)

    if make_bf:
        @block.vector
        def _(v):
            c.s_ld.wait(v)
            for sm, val in gates:
                if val:
                    v.wait_ge(sm.h, val)
            c.s_ve.inc(v.tensor_copy(c.RBb[0:1, 0:length], c.RB[0:1, 0:length]))



def emit_proj_fm(c, block, dst_fm, dst_chunk0, ws_off, wt_ready, src_chunk, e_chunks,
                 bias_col, wslice=128):
    nc = c.nc
    tb = c.s_te.n
    vb = c.s_ve.n
    wtot = 128 * len(e_chunks)

    @block.tensor
    def _(t):
        if vb:
            t.wait_ge(c.s_ve.h, vb)
        for jj, j in enumerate(e_chunks):
            b0 = 0 if jj % 2 == 0 else 4
            if jj >= 2:
                t.wait_ge(c.s_ve.h, vb + jj - 1)
            for k in range(NKC):
                if wt_ready is not None and jj == 0:
                    t.wait_ge(wt_ready[k][0].h, wt_ready[k][1])
                lhsT = c.WS[:, ws_off + wtot * k + 128 * jj: ws_off + wtot * k + 128 * (jj + 1)]
                for w in range(4):
                    t.matmul(bank(c, b0 + w)[:, 0:512], lhsT,
                             src_chunk(k, 512 * w, 512 * (w + 1)),
                             start=(k == 0), stop=False)
            for w in range(4):
                ins = t.matmul(bank(c, b0 + w)[:, 0:512], c.ZSB[0:1, 0:128],
                               c.RBb[0:1, 0:512], start=False, stop=True)
            c.s_te.inc(ins)

    ldn = c.s_ld.n

    @block.vector
    def _(v):
        if bias_col is not None and ldn:
            v.wait_ge(c.s_ld.h, ldn)
        vw = dst_fm.rearrange("p (k t) -> p k t", t=T)
        for jj, j in enumerate(e_chunks):
            b0 = 0 if jj % 2 == 0 else 4
            v.wait_ge(c.s_te.h, tb + jj + 1)
            pv = c.PS[:, 512 * b0:512 * b0 + 2048]
            dst = vw[:, dst_chunk0 + jj, :]
            if bias_col is not None:
                ins = v.tensor_scalar(dst, pv, c.BSM[:, 8 * bias_col + j:8 * bias_col + j + 1],
                                      None, ALU.add)
            else:
                ins = v.tensor_copy(dst, pv)
            c.s_ve.inc(ins)


def emit_vprime_proj(c, block, dst, nheads, ws_off, wt_ready, src_chunk, wc,
                     bias_slot, bias_off, zb=0):
    nc = c.nc
    tb = c.s_te.n
    vb = c.s_ve.n
    CW = 65 * nheads

    @block.tensor
    def _(t):
        if vb:
            t.wait_ge(c.s_ve.h, vb)
        for j in range(NT):
            b = zb + (j % 2)
            if j >= 2:
                t.wait_ge(c.s_ve.h, vb + j - 1)
            for k in range(NKC):
                if wt_ready is not None and j == 0:
                    t.wait_ge(wt_ready[k][0].h, wt_ready[k][1])
                t.matmul(bank(c, b)[:, 0:wc], src_chunk(k, 128 * j, 128 * (j + 1)),
                         c.WS[:, ws_off + wc * k: ws_off + wc * (k + 1)],
                         start=(k == 0), stop=False)
            ins = t.matmul(bank(c, b)[:, 0:wc], c.ONESb[:, 0:128],
                           c.RBb[0:1, bias_off:bias_off + wc], start=False, stop=True)
            c.s_te.inc(ins)

    @block.vector
    def _(v):
        dv = dst.rearrange("p (j h e) -> p j h e", h=nheads, e=65)
        for j in range(NT):
            b = zb + (j % 2)
            v.wait_ge(c.s_te.h, tb + j + 1)
            v.tensor_copy(dv[:, j, :, 0:64],
                          bank(c, b)[:, 0:wc].rearrange("p (h e) -> p h e", e=64))
            c.s_ve.inc(v.memset(dv[:, j, :, 64:65], 1.0))



def emit_attention_quarter(c, block, q, scale, attn_dst):
    nc = c.nc
    NKT = 16

    for jj in range(2):
        cglob = 2 * q + jj
        tb = c.s_te.n
        vb = c.s_ve.n
        sb = c.s_se.n

        @block.tensor
        def _(t):
            if vb:
                t.wait_ge(c.s_ve.h, vb)
            for w in range(4):
                for kp in range(NKT // 2):
                    if not (w == 0 and kp == 0):
                        t.wait_ge(c.s_se.h, sb + 16 * w + 2 * kp)
                    if w > 0 and kp == 0:
                        t.wait_ge(c.s_ve.h, vb + 3 * w)
                    for i in range(2):
                        kt = 2 * kp + i
                        lA = c.A3[0:64, 4096 + 2048 * jj + 128 * kt: 4096 + 2048 * jj + 128 * (kt + 1)]
                        lB = c.A3[64:128, 4096 + 2048 * jj + 128 * kt: 4096 + 2048 * jj + 128 * (kt + 1)]
                        rA = c.A3[0:64, 2048 * jj + 512 * w: 2048 * jj + 512 * (w + 1)]
                        rB = c.A3[64:128, 2048 * jj + 512 * w: 2048 * jj + 512 * (w + 1)]
                        t.matmul(bank(c, 2 * i)[:, 0:512], lA, rA, start=True, stop=True)
                        ins = t.matmul(bank(c, 2 * i + 1)[:, 0:512], lB, rB, start=True, stop=True)
                        c.s_te.inc(ins)
                    for i in range(2):
                        kt = 2 * kp + i
                        t.wait_ge(c.s_se.h, sb + 16 * w + 2 * kp + i + 1)
                        vA = c.A3[:, 8192 + 260 * kt + 65 * (2 * jj): 8192 + 260 * kt + 65 * (2 * jj) + 65]
                        vB = c.A3[:, 8192 + 260 * kt + 65 * (2 * jj + 1): 8192 + 260 * kt + 65 * (2 * jj + 1) + 65]
                        pA = c.PT[:, 1024 * i: 1024 * i + 512]
                        pB = c.PT[:, 1024 * i + 512: 1024 * i + 1024]
                        t.matmul(bank(c, 4)[0:65, 0:512], vA, pA, start=(kt == 0), stop=False)
                        ins = t.matmul(bank(c, 5)[0:65, 0:512], vB, pB, start=(kt == 0), stop=False)
                    if kp < NKT // 2 - 1:
                        c.s_te.inc(ins)
                t.matmul(bank(c, 4)[0:65, 0:512], c.ZSB[0:1, 0:65],
                         c.RBb[0:1, 0:512], start=False, stop=True)
                ins = t.matmul(bank(c, 5)[0:65, 0:512], c.ZSB[0:1, 0:65],
                               c.RBb[0:1, 0:512], start=False, stop=True)
                c.s_te.inc(ins)
                t.wait_ge(c.s_ve.h, vb + 3 * w + 1)
                rr = c.EPI[0:1, 1024:2048]
                t.matmul(bank(c, 6)[0:64, 0:512], c.ONES32[:, 0:64], rr[:, 0:512],
                         start=True, stop=True)
                ins = t.matmul(bank(c, 7)[0:64, 0:512], c.ONES32[:, 0:64], rr[:, 512:1024],
                               start=True, stop=True)
                c.s_te.inc(ins)

        @block.scalar
        def _(s):
            for w in range(4):
                for kp in range(NKT // 2):
                    for i in range(2):
                        s.wait_ge(c.s_te.h, tb + 25 * w + 3 * kp + i + 1)
                        ins = s.activation(c.PT[:, 1024 * i:1024 * (i + 1)],
                                           c.PS[:, 1024 * i:1024 * (i + 1)],
                                           AF.Exp, scale=scale)
                        c.s_se.inc(ins)

        gb = c.s_ge.n

        @block.vector
        def _(v):
            for w in range(4):
                v.wait_ge(c.s_te.h, tb + 25 * w + 24)
                rr = c.EPI[0:1, 1024:2048]
                c.s_ve.inc(v.reciprocal(rr, c.PS[64:65, 512 * 4:512 * 4 + 1024]))
                v.wait_ge(c.s_te.h, tb + 25 * w + 25)
                bc = c.EPI[0:64, 1024:2048]
                c.s_ve.inc(v.tensor_copy(bc, c.PS[0:64, 512 * 6:512 * 6 + 1024]))
                if gb + w > 0:
                    v.wait_ge(c.s_ge.h, gb + w)
                c.s_ve.inc(v.tensor_mul(c.AMN[0:64, 0:1024], c.PS[0:64, 512 * 4:512 * 4 + 1024], bc))

        @block.gpsimd
        def _(g):
            for w in range(4):
                g.wait_ge(c.s_ve.h, vb + 3 * w + 3)
                dstA = attn_dst[0:64, 2048 * cglob + 512 * w: 2048 * cglob + 512 * (w + 1)]
                dstB = attn_dst[64:128, 2048 * cglob + 512 * w: 2048 * cglob + 512 * (w + 1)]
                dma_ser(g, c.s_st, dstA, c.AMN[0:64, 0:512])
                dma_ser(g, c.s_st, dstB, c.AMN[0:64, 512:1024])
                g.wait_ge(c.s_st.h, c.s_st.n)
                c.s_ge.inc(g.nop())


def emit_oproj_sandwich(c, block, attn_src, wt_ready_o, bias_row_off, sums_dst, gate=None):
    nc = c.nc
    tb = c.s_te.n
    vb = c.s_ve.n
    sb = c.s_se.n
    asrc = attn_src.rearrange("p (k t) -> p k t", t=T)
    negm2 = c.STATS[:, 96:97]
    r2 = c.STATS[:, 97:98]
    v1t = c.STATS[:, 98:99]
    v2t = c.STATS[:, 99:100]
    ssq = c.STATS[:, 100:101]
    ssm = c.STATS[:, 101:102]
    junk = c.EPI[:, 0:1024]
    zn = c.EPI[:, 1024:2048]

    @block.tensor
    def _(t):
        if gate is not None:
            t.wait_ge(gate[0].h, gate[1])
        if vb:
            t.wait_ge(c.s_ve.h, vb)
        for j in range(NT):
            b0 = 0 if j % 2 == 0 else 2
            if j >= 2:
                t.wait_ge(c.s_ve.h, vb + 4 * (j - 1))
            for k in range(NKC):
                if wt_ready_o is not None and j == 0:
                    t.wait_ge(wt_ready_o[k][0].h, wt_ready_o[k][1])
                lhsT = asrc[:, k, 128 * j:128 * (j + 1)]
                t.matmul(bank(c, b0)[:, 0:512], lhsT, c.A3[:, 1024 * k:1024 * k + 512],
                         start=(k == 0), stop=False)
                t.matmul(bank(c, b0 + 1)[:, 0:512], lhsT, c.A3[:, 1024 * k + 512:1024 * (k + 1)],
                         start=(k == 0), stop=False)
            t.matmul(bank(c, b0)[:, 0:512], c.ONESb[:, 0:128],
                     c.RBb[0:1, bias_row_off:bias_row_off + 512], start=False, stop=True)
            ins = t.matmul(bank(c, b0 + 1)[:, 0:512], c.ONESb[:, 0:128],
                           c.RBb[0:1, bias_row_off + 512:bias_row_off + 1024],
                           start=False, stop=True)
            c.s_te.inc(ins)

    @block.vector
    def _(v):
        for j in range(NT):
            b0 = 0 if j % 2 == 0 else 2
            pz = c.PS[:, 512 * b0:512 * b0 + 1024]
            v.wait_ge(c.s_te.h, tb + j + 1)
            c.s_ve.inc(v.reduce_sum(ssm, pz, axis=AX))
            v.wait_ge(c.s_se.h, sb + 3 * j + 1)
            v.tensor_scalar(negm2, ssm, -1.0 / D, None, ALU.mult)
            v.tensor_mul(v2t, negm2, negm2)
            v.tensor_scalar(v1t, ssq, 1.0 / D, None, ALU.mult)
            c.s_ve.inc(v.tensor_sub(v1t, v1t, v2t))
            v.wait_ge(c.s_se.h, sb + 3 * j + 3)
            v.tensor_scalar(zn, pz, negm2, r2, ALU.add, ALU.mult)
            ins = v.scalar_tensor_tensor(
                xtile(c, j), zn, 1.0, xtile(c, j), ALU.mult, ALU.add,
                accum_out=(sums_dst[:, j:j + 1] if sums_dst is not None else None))
            c.s_ve.inc(ins)
            v.wait_ge(c.s_ve.h, c.s_ve.n)
            c.s_ve.inc(v.nop())

    @block.scalar
    def _(s):
        for j in range(NT):
            b0 = 0 if j % 2 == 0 else 2
            pz = c.PS[:, 512 * b0:512 * b0 + 1024]
            s.wait_ge(c.s_te.h, tb + j + 1)
            s.wait_ge(c.s_ve.h, vb + 4 * j + 1)
            c.s_se.inc(s.activation(junk, pz, AF.Square, accum_out=ssq))
            s.wait_ge(c.s_ve.h, vb + 4 * j + 2)
            c.s_se.inc(s.activation(v1t, v1t, AF.Ln, bias=c.EPSC[:]))
            c.s_se.inc(s.activation(r2, v1t, AF.Exp, scale=-0.5))



def stream_w_rot(c, block, dram_ap, n_chunks, ws_off, wc, gate_sem, gate_counts):
    nc = c.nc
    wsem = [c.s_wt, c.s_wu]
    dvals = []

    @block.sync
    def _(sy):
        for k in range(n_chunks):
            if k >= 2:
                sy.wait_ge(gate_sem.h, gate_counts(k - 2))
            else:
                gate_wf(sy, c)
            dma_ser(sy, wsem[k % 2], c.WF[:, 1024 * (k % 2):1024 * (k % 2) + wc],
                    dram_ap[128 * k:128 * (k + 1), :])
            dvals.append((wsem[k % 2], wsem[k % 2].n))

    ready = []

    @block.vector
    def _(v):
        for k in range(n_chunks):
            v.wait_ge(dvals[k][0].h, dvals[k][1])
            if k >= 2:
                v.wait_ge(gate_sem.h, gate_counts(k - 2))
            ins = v.tensor_copy(c.WS[:, ws_off + wc * (k % 2): ws_off + wc * (k % 2) + wc],
                                c.WF[:, 1024 * (k % 2):1024 * (k % 2) + wc])
            c.s_ve.inc(ins)
            ready.append((c.s_ve, c.s_ve.n))
    return ready


def stream_masked_w(c, block, dram_w, dram_m, n_chunks, ws_off, wc, gate0=None, dst=None):
    nc = c.nc
    gb = c.s_ge.n
    dvals = []
    assert wc <= 1024

    @block.sync
    def _(sy):
        for k in range(n_chunks):
            if k >= 1:
                sy.wait_ge(c.s_ge.h, gb + k)
            else:
                gate_wf(sy, c)
            dma_ser(sy, c.s_wt, c.WF[:, 0:wc], dram_w[128 * k:128 * (k + 1), :])
            dma_ser(sy, c.s_wt, c.WF[:, 1024:1024 + wc], dram_m[128 * k:128 * (k + 1), :])
            dvals.append((c.s_wt, c.s_wt.n))

    ready = []

    @block.gpsimd
    def _(g):
        for k in range(n_chunks):
            g.wait_ge(dvals[k][0].h, dvals[k][1])
            if k == 0 and gate0 is not None:
                g.wait_ge(gate0[0].h, gate0[1])
            dd = dst if dst is not None else c.WS
            g.tensor_scalar(c.WF[:, 1024:1024 + wc], c.WF[:, 1024:1024 + wc],
                            0.0, None, ALU.is_gt)
            ins = g.tensor_tensor(dd[:, ws_off + wc * k: ws_off + wc * (k + 1)],
                                  c.WF[:, 1024:1024 + wc], c.WF[:, 0:wc], ALU.mult)
            c.s_ge.inc(ins)
            ready.append((c.s_ge, c.s_ge.n))
    return ready



def emit_l0ffn_slots(c, block):
    nc = c.nc
    SLo = c.SL[0:16, 0:1024]
    S1 = c.SL[0:16, 1024:2048]
    S2 = c.SL[0:16, 2048:3072]

    vbq = c.s_ve.n
    teq = c.s_te.n

    @block.gpsimd
    def _(g):
        dma_ser(g, c.s_ld, S2, c.dr["sq"])
        dma_ser(g, c.s_ld, S1, c.dr["d_bo"].rearrange("(a d) -> a d", a=1).broadcast_to([16, D]))
        if vbq:
            g.wait_ge(c.s_ve.h, vbq)
        if teq:
            g.wait_ge(c.s_te.h, teq)
        dma_ser(g, c.s_ld, c.RB[0:1, 0:1024], c.dr["d_bq"].rearrange("(a d) -> a d", a=1))

    ldb = c.s_ld.n
    vb0 = c.s_ve.n

    @block.vector
    def _(v):
        v.wait_ge(c.s_ld.h, ldb)
        c.s_ve.inc(v.tensor_add(S1, S1, S2))

    tb0 = c.s_te.n

    @block.tensor
    def _(t):
        t.wait_ge(c.s_ld.h, ldb)
        for k in range(NKC):
            ins = t.transpose(bank(c, 6)[:, 16 * k:16 * (k + 1)],
                              S2[:, 128 * k:128 * (k + 1)], c.ID32[0:16, 0:16])
        c.s_te.inc(ins)

    @block.vector
    def _(v):
        v.wait_ge(c.s_te.h, tb0 + 1)
        c.s_ve.inc(v.tensor_copy(c.HT[:, 0:128], bank(c, 6)[:, 0:128]))

    vb1 = c.s_ve.n
    tb1 = c.s_te.n
    wsem = [c.s_wt, c.s_wu]
    dv1 = []

    @block.sync
    def _(sy):
        for k in range(NKC):
            if k >= 2:
                sy.wait_ge(c.s_te.h, tb1 + (k - 1))
            else:
                gate_wf(sy, c)
            dma_ser(sy, wsem[k % 2], c.WF[:, 1024 * (k % 2):1024 * (k % 2 + 1)],
                    c.dr["d_Wq"][128 * k:128 * (k + 1), :])
            dv1.append((wsem[k % 2], wsem[k % 2].n))

    @block.tensor
    def _(t):
        t.wait_ge(c.s_ve.h, vb1)
        for k in range(NKC):
            t.wait_ge(dv1[k][0].h, dv1[k][1])
            lhsT = c.HT[:, 16 * k:16 * (k + 1)]
            t.matmul(bank(c, 4)[0:16, 0:512], lhsT, c.WF[:, 1024 * (k % 2):1024 * (k % 2) + 512],
                     start=(k == 0), stop=False)
            ins = t.matmul(bank(c, 5)[0:16, 0:512], lhsT,
                           c.WF[:, 1024 * (k % 2) + 512:1024 * (k % 2 + 1)],
                           start=(k == 0), stop=False)
            c.s_te.inc(ins)
        t.wait_ge(c.s_ld.h, ldb)
        t.matmul(bank(c, 4)[0:16, 0:512], c.ONES32[:, 0:16], c.RB[0:1, 0:512],
                 start=False, stop=True)
        ins = t.matmul(bank(c, 5)[0:16, 0:512], c.ONES32[:, 0:16], c.RB[0:1, 512:1024],
                       start=False, stop=True)
        c.s_te.inc(ins)

    tb2 = c.s_te.n

    @block.vector
    def _(v):
        v.wait_ge(c.s_te.h, tb2)
        c.s_ve.inc(v.tensor_copy(S2, c.PS[0:16, 512 * 4:512 * 4 + 1024]))

    vb2 = c.s_ve.n
    tb3 = c.s_te.n

    @block.tensor
    def _(t):
        t.wait_ge(c.s_ve.h, vb2)
        for k in range(NKC):
            ins = t.transpose(bank(c, 6)[:, 16 * k:16 * (k + 1)],
                              S2[:, 128 * k:128 * (k + 1)], c.ID32[0:16, 0:16])
        c.s_te.inc(ins)

    @block.vector
    def _(v):
        v.wait_ge(c.s_te.h, tb3 + 1)
        c.s_ve.inc(v.tensor_copy(c.QSL[:, 0:128], bank(c, 6)[:, 0:128]))

    load_bias_row(c, block, c.dr["d_bv"], gates=((c.s_te, c.s_te.n),))
    load_bias_fm(c, block, c.dr["d_bk"], 2)

    for q in range(4):
        wr_k = stream_w_bf16(c, block, c.dr["d_Wk"][:, 256 * q:256 * (q + 1)], NKC, 0, wc=256)
        wr_v = stream_w_bf16(c, block, c.dr["d_Wv"][:, 256 * q:256 * (q + 1)], NKC, 2048, wc=256)
        emit_proj_fm(c, block, c.A3[:, 4096:8192], 0, 0, wr_k,
                     lambda k, lo, hi: c.XN[:, 2048 * k + lo:2048 * k + hi],
                     [2 * q, 2 * q + 1], bias_col=2)
        emit_vprime_proj(c, block, c.A3[:, 8192:12352], 4, 2048, wr_v,
                         lambda k, lo, hi: c.XN[:, 2048 * k + lo:2048 * k + hi],
                         256, None, 256 * q, zb=0)
        for jj in range(2):
            tb = c.s_te.n
            sb = c.s_se.n
            vb = c.s_ve.n
            ch = 2 * q + jj

            @block.tensor
            def _(t, jj=jj, ch=ch, tb=tb, sb=sb, vb=vb):
                t.wait_ge(c.s_ve.h, vb)
                for kt in range(NT):
                    lA = c.A3[0:64, 4096 + 2048 * jj + 128 * kt: 4096 + 2048 * jj + 128 * (kt + 1)]
                    lB = c.A3[64:128, 4096 + 2048 * jj + 128 * kt: 4096 + 2048 * jj + 128 * (kt + 1)]
                    rA = c.QSL[0:64, 16 * ch:16 * (ch + 1)]
                    rB = c.QSL[64:128, 16 * ch:16 * (ch + 1)]
                    t.matmul(bank(c, 0)[:, 16 * kt:16 * (kt + 1)], lA, rA, start=True, stop=True)
                    ins = t.matmul(bank(c, 1)[:, 16 * kt:16 * (kt + 1)], lB, rB, start=True, stop=True)
                c.s_te.inc(ins)
                t.wait_ge(c.s_se.h, sb + 1)
                for kt in range(NT):
                    for hh in range(2):
                        vsl = c.A3[:, 8192 + 260 * kt + 65 * (2 * jj + hh):
                                   8192 + 260 * kt + 65 * (2 * jj + hh) + 65]
                        psl = c.PT[:, 256 * hh + 16 * kt: 256 * hh + 16 * (kt + 1)]
                        t.matmul(bank(c, 2 + hh)[0:16, 0:65], psl, vsl,
                                 start=(kt == 0), stop=False)
                for hh in range(2):
                    ins = t.matmul(bank(c, 2 + hh)[0:16, 0:65],
                                   c.ZSB[0:1, 0:16], c.RBb[0:1, 0:65],
                                   start=False, stop=True)
                c.s_te.inc(ins)

            @block.scalar
            def _(s, tb=tb):
                s.wait_ge(c.s_te.h, tb + 1)
                s.activation(c.PT[:, 0:256], bank(c, 0)[:, 0:256], AF.Exp, scale=0.125)
                c.s_se.inc(s.activation(c.PT[:, 256:512], bank(c, 1)[:, 0:256], AF.Exp, scale=0.125))

            @block.vector
            def _(v, jj=jj, ch=ch, tb=tb):
                v.wait_ge(c.s_te.h, tb + 2)
                rc = c.FSM[0:16, 86:88]
                v.reciprocal(rc, c.PS[0:16, 1088:1601:512])
                for hh in range(2):
                    h = 2 * ch + hh
                    ins = v.tensor_scalar(
                        c.SL[0:16, 2048 + 64 * h:2048 + 64 * (h + 1)],
                        bank(c, 2 + hh)[0:16, 0:64],
                        rc[:, hh:hh + 1], None, ALU.mult)
                c.s_ve.inc(ins)

    vb3 = c.s_ve.n
    tb4 = c.s_te.n

    @block.tensor
    def _(t):
        t.wait_ge(c.s_ve.h, vb3)
        for k in range(NKC):
            ins = t.transpose(bank(c, 6)[:, 16 * k:16 * (k + 1)],
                              c.SL[0:16, 2048 + 128 * k:2048 + 128 * (k + 1)], c.ID32[0:16, 0:16])
        c.s_te.inc(ins)

    @block.vector
    def _(v):
        v.wait_ge(c.s_te.h, tb4 + 1)
        c.s_ve.inc(v.tensor_copy(c.HT[:, 0:128], bank(c, 6)[:, 0:128]))

    vb4 = c.s_ve.n
    tb5 = c.s_te.n
    dv2 = []

    @block.sync
    def _(sy):
        for k in range(NKC):
            if k >= 2:
                sy.wait_ge(c.s_te.h, tb5 + (k - 1))
            else:
                gate_wf(sy, c)
            dma_ser(sy, wsem[k % 2], c.WF[:, 1024 * (k % 2):1024 * (k % 2 + 1)],
                    c.dr["d_Wo"][128 * k:128 * (k + 1), :])
            dv2.append((wsem[k % 2], wsem[k % 2].n))

    @block.tensor
    def _(t):
        t.wait_ge(c.s_ve.h, vb4)
        for k in range(NKC):
            t.wait_ge(dv2[k][0].h, dv2[k][1])
            lhsT = c.HT[:, 16 * k:16 * (k + 1)]
            t.matmul(bank(c, 4)[0:16, 0:512], lhsT, c.WF[:, 1024 * (k % 2):1024 * (k % 2) + 512],
                     start=(k == 0), stop=False)
            ins = t.matmul(bank(c, 5)[0:16, 0:512], lhsT,
                           c.WF[:, 1024 * (k % 2) + 512:1024 * (k % 2 + 1)],
                           start=(k == 0), stop=False)
            if k < NKC - 1:
                c.s_te.inc(ins)
        t.matmul(bank(c, 4)[0:16, 0:512], c.ZS32[0:1, 0:16], c.RB[0:1, 0:512],
                 start=False, stop=True)
        ins = t.matmul(bank(c, 5)[0:16, 0:512], c.ZS32[0:1, 0:16], c.RB[0:1, 0:512],
                       start=False, stop=True)
        c.s_te.inc(ins)

    tb6 = c.s_te.n

    @block.vector
    def _(v):
        v.wait_ge(c.s_te.h, tb6)
        c.s_ve.inc(v.tensor_add(c.SL[0:16, 0:1024], c.SL[0:16, 1024:2048],
                                c.PS[0:16, 512 * 4:512 * 4 + 1024]))


def emit_l0ffn_experts(c, block):
    nc = c.nc
    UG = c.A3[:].bitcast(F32)[:, 0:1536]
    SLo = c.SL[0:16, 0:1024]
    S1 = c.SL[0:16, 1024:2048]
    S2 = c.SL[0:16, 2048:3072]
    S3 = c.EPI[0:16, 1024:2048]
    vb0 = c.s_ve.n
    gb0 = c.s_ge.n

    @block.gpsimd
    def _(g):
        g.wait_ge(c.s_ve.h, vb0)
        dma_ser(g, c.s_ld, c.ag1_in.ap(), SLo)
        g.wait_ge(c.s_ld.h, c.s_ld.n)
        from concourse.aot_env import allow_ici_all_to_all
        with allow_ici_all_to_all():
            c.s_cc.inc(g.collective_compute(
                "AllToAll", ALU.bypass, replica_groups=[list(range(NCORES))],
                ins=[c.ag1_in.ap().opt()], outs=[c.ag1_out.ap().opt()]))
        g.wait_ge(c.s_cc.h, c.s_cc.n)
        dma_ser(g, c.s_ld, S2, c.ag1_out.ap()[0:16, :])
        dma_ser(g, c.s_ld, S1, c.dr["e_ln_g"].rearrange("(a e) d -> a e d", a=1).broadcast_to([8, 2, D]))
        dma_ser(g, c.s_ld, S3, c.dr["e_ln_b"].rearrange("(a e) d -> a e d", a=1).broadcast_to([8, 2, D]))
        g.wait_ge(c.s_ld.h, c.s_ld.n)
        c.s_ge.inc(g.nop())

    ge1 = c.s_ge.n
    sb0 = c.s_se.n
    essq = c.FSM[0:16, 80:81]
    esum = c.FSM[0:16, 81:82]
    enegm = c.FSM[0:16, 82:83]
    evar = c.FSM[0:16, 83:84]
    er = c.FSM[0:16, 84:85]
    et = c.FSM[0:16, 85:86]

    @block.scalar
    def _(s):
        s.wait_ge(c.s_ge.h, ge1)
        c.s_se.inc(s.activation(c.EPI[0:16, 0:1024], S2, AF.Square, accum_out=essq))

    @block.vector
    def _(v):
        v.wait_ge(c.s_ge.h, ge1)
        v.reduce_sum(esum, S2, axis=AX)
        v.wait_ge(c.s_se.h, sb0 + 1)
        v.tensor_scalar(enegm, esum, -1.0 / D, None, ALU.mult)
        v.tensor_mul(et, enegm, enegm)
        v.tensor_scalar(evar, essq, 1.0 / D, None, ALU.mult)
        c.s_ve.inc(v.tensor_sub(evar, evar, et))

    sb1 = c.s_se.n
    vb1 = c.s_ve.n

    @block.scalar
    def _(s):
        s.wait_ge(c.s_ve.h, vb1)
        s.activation(evar, evar, AF.Ln, bias=c.EPSC[0:16, :])
        c.s_se.inc(s.activation(er, evar, AF.Exp, scale=-0.5))

    @block.vector
    def _(v):
        v.wait_ge(c.s_se.h, sb1 + 1)
        v.tensor_scalar(S2, S2, enegm, er, ALU.add, ALU.mult)
        v.tensor_mul(S2, S2, S1)
        c.s_ve.inc(v.tensor_add(S2, S2, S3))

    vb2 = c.s_ve.n
    tb0 = c.s_te.n

    @block.tensor
    def _(t):
        t.wait_ge(c.s_ve.h, vb2)
        for k in range(NKC):
            ins = t.transpose(bank(c, 6)[:, 16 * k:16 * (k + 1)],
                              S2[:, 128 * k:128 * (k + 1)], c.ID32[0:16, 0:16])
        c.s_te.inc(ins)

    @block.vector
    def _(v):
        v.wait_ge(c.s_te.h, tb0 + 1)
        c.s_ve.inc(v.tensor_copy(c.HT[:, 0:128], bank(c, 6)[:, 0:128]))

    vbH = c.s_ve.n

    mats = [("eW1", "eb1", 0, 0), ("eWg", "ebg", 0, 512), ("eW1", "eb1", 1, 1024), ("eWg", "ebg", 1, 1536)]
    tbU = c.s_te.n
    wsem = [c.s_wt, c.s_wu]
    dvU = []
    ndma = [0]

    @block.sync
    def _(sy):
        for (wname, bname, e, off) in mats:
            for k in range(NKC):
                for g4 in range(4):
                    if ndma[0] >= 2:
                        sy.wait_ge(c.s_te.h, tbU + (ndma[0] - 1))
                    else:
                        gate_wf(sy, c)
                    dma_ser(sy, wsem[ndma[0] % 2],
                            c.WF[:, 1024 * (ndma[0] % 2):1024 * (ndma[0] % 2 + 1)],
                            c.dr[wname][e, 128 * k:128 * (k + 1), 1024 * g4:1024 * (g4 + 1)])
                    dvU.append((wsem[ndma[0] % 2], wsem[ndma[0] % 2].n))
                    ndma[0] += 1

    nmm = [0]

    @block.tensor
    def _(t):
        t.wait_ge(c.s_ve.h, vbH)
        pall = c.PS[:, 0:2048]
        for (wname, bname, e, off) in mats:
            for k in range(NKC):
                for g4 in range(4):
                    t.wait_ge(dvU[nmm[0]][0].h, dvU[nmm[0]][1])
                    rhs = c.HT[:, 16 * k + e:16 * k + 16:2]
                    for ft in range(8):
                        fcol = off + 8 * (8 * g4 + ft)
                        ins = t.matmul(pall[:, fcol:fcol + 8],
                                       c.WF[:, 1024 * (nmm[0] % 2) + 128 * ft:
                                            1024 * (nmm[0] % 2) + 128 * (ft + 1)],
                                       rhs, start=(k == 0 and g4 == 0 and ft == 0),
                                       stop=False)
                    nmm[0] += 1
                    c.s_te.inc(ins)

    tbB = c.s_te.n
    ldB = c.s_ld.n

    @block.gpsimd
    def _(g):
        nq = 0
        for i, (wname, bname, e, off) in enumerate(mats):
            for qtr in range(4):
                if nq:
                    g.wait_ge(c.s_te.h, tbB + nq)
                dma_ser(g, c.s_ld, c.RB[0:1, 0:1024],
                        c.dr[bname][e, 1024 * qtr:1024 * (qtr + 1)].rearrange("(a d) -> a d", a=1))
                nq += 1

    @block.tensor
    def _(t):
        nb = 0
        pall = c.PS[:, 0:2048]
        for i, (wname, bname, e, off) in enumerate(mats):
            for qtr in range(4):
                t.wait_ge(c.s_ld.h, ldB + 16 * (nb + 1))
                for fi in range(8):
                    ft = 8 * qtr + fi
                    fcol = off + 8 * ft
                    ins = t.matmul(pall[:, fcol:fcol + 8],
                                   c.RB[0:1, 128 * fi:128 * (fi + 1)],
                                   c.ONES32[0:1, 0:8], start=False,
                                   stop=(qtr == 3 and fi == 7))
                nb += 1
                c.s_te.inc(ins)

    tb1 = c.s_te.n
    sb2 = c.s_se.n

    @block.scalar
    def _(s):
        s.wait_ge(c.s_te.h, tb1)
        for b4 in range(4):
            ins = s.activation(c.EPI[:, 256 * b4:256 * (b4 + 1)],
                               c.PS[:, 512 * b4:512 * b4 + 256], AF.Exp, scale=-1.0)
        c.s_se.inc(ins)

    vb3 = c.s_ve.n

    @block.vector
    def _(v):
        v.wait_ge(c.s_se.h, sb2 + 1)
        v.tensor_scalar(c.EPI[:, 1024:2048], c.EPI[:, 0:1024], 1.0, None, ALU.add)
        v.reciprocal(c.EPI[:, 0:1024], c.EPI[:, 1024:2048])
        pst = c.PS[:, 0:2048].rearrange("p (b x) -> p b x", x=512)[:, :, 0:256]
        v.tensor_mul(UG[:, 0:1024].rearrange("p (b x) -> p b x", x=256), pst,
                     c.EPI[:, 0:1024].rearrange("p (b x) -> p b x", x=256))
        v.tensor_mul(UG[:, 1024:1280], UG[:, 0:256], UG[:, 256:512])
        c.s_ve.inc(v.tensor_mul(UG[:, 1280:1536], UG[:, 512:768], UG[:, 768:1024]))

    vb4 = c.s_ve.n
    tb2 = c.s_te.n
    dv3 = []
    nd2 = [0]

    @block.sync
    def _(sy):
        for e in range(2):
            for ft in range(32):
                if nd2[0] >= 2:
                    sy.wait_ge(c.s_te.h, tb2 + (nd2[0] - 1))
                else:
                    gate_wf(sy, c)
                dma_ser(sy, wsem[nd2[0] % 2],
                        c.WF[:, 1024 * (nd2[0] % 2):1024 * (nd2[0] % 2 + 1)],
                        c.dr["eW2"][e, 128 * ft:128 * (ft + 1), :])
                dv3.append((wsem[nd2[0] % 2], wsem[nd2[0] % 2].n))
                nd2[0] += 1

    @block.tensor
    def _(t):
        t.wait_ge(c.s_ve.h, vb4)
        nm = 0
        for e in range(2):
            zb = 4 + 2 * e
            for ft in range(32):
                t.wait_ge(dv3[nm][0].h, dv3[nm][1])
                lhsT = UG[:, 1024 + 256 * e + 8 * ft: 1024 + 256 * e + 8 * (ft + 1)]
                t.matmul(bank(c, zb)[0:8, 0:512], lhsT,
                         c.WF[:, 1024 * (nm % 2):1024 * (nm % 2) + 512],
                         start=(ft == 0), stop=False)
                ins = t.matmul(bank(c, zb + 1)[0:8, 0:512], lhsT,
                               c.WF[:, 1024 * (nm % 2) + 512:1024 * (nm % 2 + 1)],
                               start=(ft == 0), stop=False)
                nm += 1
                if ft < 31:
                    c.s_te.inc(ins)
            t.matmul(bank(c, zb)[0:8, 0:512], c.ZS32[0:1, 0:8], c.RB[0:1, 0:512],
                     start=False, stop=True)
            ins = t.matmul(bank(c, zb + 1)[0:8, 0:512], c.ZS32[0:1, 0:8], c.RB[0:1, 0:512],
                           start=False, stop=True)
            c.s_te.inc(ins)

    tb3 = c.s_te.n

    @block.vector
    def _(v):
        v.wait_ge(c.s_te.h, tb3)
        v.tensor_copy(S1[0:8, :], c.PS[0:8, 512 * 4:512 * 4 + 1024])
        c.s_ve.inc(v.tensor_copy(S2[0:8, :], c.PS[0:8, 512 * 6:512 * 6 + 1024]))

    vb5 = c.s_ve.n
    gb1 = c.s_ge.n

    @block.gpsimd
    def _(g):
        g.wait_ge(c.s_ve.h, vb5)
        zi = c.agz_in.ap().rearrange("(b two) d -> b two d", two=2)
        dma_ser(g, c.s_ld, zi[:, 0, :], S1[0:8, :])
        dma_ser(g, c.s_ld, zi[:, 1, :], S2[0:8, :])
        g.wait_ge(c.s_ld.h, c.s_ld.n)
        from concourse.aot_env import allow_ici_all_to_all
        with allow_ici_all_to_all():
            c.s_cc.inc(g.collective_compute(
                "AllToAll", ALU.bypass, replica_groups=[list(range(NCORES))],
                ins=[c.agz_in.ap().opt()], outs=[c.agz_out.ap().opt()]))
        g.wait_ge(c.s_cc.h, c.s_cc.n)
        dma_ser(g, c.s_ld, S2, c.agz_out.ap()[0:16, :])
        dma_ser(g, c.s_ld, S1, c.dr["eb2"])
        g.wait_ge(c.s_ld.h, c.s_ld.n)
        c.s_ge.inc(g.nop())

    ge2 = c.s_ge.n

    @block.vector
    def _(v):
        v.wait_ge(c.s_ge.h, ge2)
        v.tensor_add(S2, S2, S1)
        c.s_ve.inc(v.tensor_add(SLo, SLo, S2))



def emit_l0ffn_cmha(c, block):
    nc = c.nc
    SLo = c.SL[0:16, 0:1024]
    S2 = c.SL[0:16, 2048:3072]
    vb0 = c.s_ve.n
    tb0 = c.s_te.n

    @block.tensor
    def _(t):
        t.wait_ge(c.s_ve.h, vb0)
        for k in range(NKC):
            ins = t.transpose(bank(c, 6)[:, 16 * k:16 * (k + 1)],
                              SLo[:, 128 * k:128 * (k + 1)], c.ID32[0:16, 0:16])
        c.s_te.inc(ins)

    @block.vector
    def _(v):
        v.wait_ge(c.s_te.h, tb0 + 1)
        c.s_ve.inc(v.tensor_copy(c.QSL[:, 128:256], bank(c, 6)[:, 0:128]))

    load_bias_row(c, block, c.dr["c_bk"], gates=((c.s_te, c.s_te.n),))
    vbT = c.s_ve.n

    def tm_out_slot_proj(wname, bias_rbb_off, zbank):
        tbp = c.s_te.n
        vcur = c.s_ve.n
        wr = stream_w_rot(c, block, c.dr[wname], NKC, 0, 1024,
                          c.s_te, lambda k: tbp + k + 1)

        @block.tensor
        def _(t):
            t.wait_ge(c.s_ve.h, vcur)
            for k in range(NKC):
                t.wait_ge(wr[k][0].h, wr[k][1])
                lhsT = c.QSL[:, 128 + 16 * k:128 + 16 * (k + 1)]
                t.matmul(bank(c, zbank)[0:16, 0:512], lhsT, c.WS[:, 1024 * (k % 2):1024 * (k % 2) + 512],
                         start=(k == 0), stop=False)
                ins = t.matmul(bank(c, zbank + 1)[0:16, 0:512], lhsT,
                               c.WS[:, 1024 * (k % 2) + 512:1024 * (k % 2 + 1)],
                               start=(k == 0), stop=False)
                c.s_te.inc(ins)
            t.matmul(bank(c, zbank)[0:16, 0:512], c.ONESb[0:1, 0:16],
                     c.RBb[0:1, bias_rbb_off:bias_rbb_off + 512], start=False, stop=True)
            ins = t.matmul(bank(c, zbank + 1)[0:16, 0:512], c.ONESb[0:1, 0:16],
                           c.RBb[0:1, bias_rbb_off + 512:bias_rbb_off + 1024],
                           start=False, stop=True)
            c.s_te.inc(ins)
        return c.s_te.n

    tK = tm_out_slot_proj("c_Wk", 0, 4)

    @block.vector
    def _(v):
        v.wait_ge(c.s_te.h, tK)
        c.s_ve.inc(v.tensor_copy(S2, c.PS[0:16, 512 * 4:512 * 4 + 1024]))

    vb1 = c.s_ve.n
    tb1 = c.s_te.n

    @block.tensor
    def _(t):
        t.wait_ge(c.s_ve.h, vb1)
        for k in range(NKC):
            ins = t.transpose(bank(c, 6)[:, 16 * k:16 * (k + 1)],
                              S2[:, 128 * k:128 * (k + 1)], c.ID32[0:16, 0:16])
        c.s_te.inc(ins)

    @block.vector
    def _(v):
        v.wait_ge(c.s_te.h, tb1 + 1)
        c.s_ve.inc(v.tensor_copy(c.QSL[:, 0:128], bank(c, 6)[:, 0:128]))

    load_bias_row(c, block, c.dr["c_bv"], gates=((c.s_te, c.s_te.n),))
    vbT2 = c.s_ve.n
    tV = tm_out_slot_proj("c_Wv", 0, 4)

    @block.vector
    def _(v):
        v.wait_ge(c.s_te.h, tV)
        dv = c.SL[:].bitcast(BF16)[0:16, 2048:3088].rearrange("p (h e) -> p h e", e=65)
        v.tensor_copy(dv[:, :, 0:64],
                      c.PS[0:16, 512 * 4:512 * 4 + 1024].rearrange("p (h e) -> p h e", e=64))
        c.s_ve.inc(v.memset(dv[:, :, 64:65], 1.0))

    load_bias_fm(c, block, c.dr["c_bq"], 3)
    for q in range(4):
        wr_q = stream_w_bf16(c, block, c.dr["c_Wq"][:, 256 * q:256 * (q + 1)], NKC, 0, wc=256,
                             gate0=(c.s_te, c.s_te.n))
        emit_proj_fm(c, block, c.A3[:, 0:4096], 0, 0, wr_q,
                     lambda k, lo, hi: c.XN[:, 2048 * k + lo:2048 * k + hi],
                     [2 * q, 2 * q + 1], bias_col=3, wslice=256)
        for jj in range(2):
            ch = 2 * q + jj
            tb = c.s_te.n
            sb = c.s_se.n
            vb = c.s_ve.n
            gb = c.s_ge.n

            @block.tensor
            def _(t, jj=jj, ch=ch, tb=tb, sb=sb, vb=vb):
                t.wait_ge(c.s_ve.h, vb)
                for w in range(4):
                    if w > 0:
                        t.wait_ge(c.s_ve.h, vb + 3 * w)
                    kA = c.QSL[0:64, 16 * ch:16 * (ch + 1)]
                    kB = c.QSL[64:128, 16 * ch:16 * (ch + 1)]
                    rA = c.A3[0:64, 2048 * jj + 512 * w:2048 * jj + 512 * (w + 1)]
                    rB = c.A3[64:128, 2048 * jj + 512 * w:2048 * jj + 512 * (w + 1)]
                    if w >= 1:
                        t.wait_ge(c.s_se.h, sb + w)
                    t.matmul(bank(c, 0)[0:16, 0:512], kA, rA, start=True, stop=True)
                    ins = t.matmul(bank(c, 1)[0:16, 0:512], kB, rB, start=True, stop=True)
                    c.s_te.inc(ins)
                    t.wait_ge(c.s_se.h, sb + w + 1)
                    for hh in range(2):
                        vsl = c.SL[:].bitcast(BF16)[0:16, 2048 + 65 * (2 * ch + hh):
                                                    2048 + 65 * (2 * ch + hh) + 65]
                        psl = c.PT[0:16, 512 * hh:512 * hh + 512]
                        ins = t.matmul(bank(c, 4 + hh)[0:65, 0:512], vsl, psl,
                                       start=True, stop=True)
                    c.s_te.inc(ins)
                    t.wait_ge(c.s_ve.h, vb + 3 * w + 1)
                    rr = c.EPI[0:1, 1024:2048]
                    t.matmul(bank(c, 6)[0:64, 0:512], c.ONES32[:, 0:64], rr[:, 0:512],
                             start=True, stop=True)
                    ins = t.matmul(bank(c, 7)[0:64, 0:512], c.ONES32[:, 0:64], rr[:, 512:1024],
                                   start=True, stop=True)
                    c.s_te.inc(ins)

            @block.scalar
            def _(s, tb=tb, sb=sb):
                for w in range(4):
                    s.wait_ge(c.s_te.h, tb + 3 * w + 1)
                    s.activation(c.PT[0:16, 0:512], bank(c, 0)[0:16, 0:512], AF.Exp, scale=0.125)
                    c.s_se.inc(s.activation(c.PT[0:16, 512:1024], bank(c, 1)[0:16, 0:512],
                                            AF.Exp, scale=0.125))

            @block.vector
            def _(v, ch=ch, tb=tb, vb=vb, gb=gb):
                for w in range(4):
                    v.wait_ge(c.s_te.h, tb + 3 * w + 2)
                    rr = c.EPI[0:1, 1024:2048]
                    c.s_ve.inc(v.reciprocal(rr, c.PS[64:65, 512 * 4:512 * 4 + 1024]))
                    v.wait_ge(c.s_te.h, tb + 3 * w + 3)
                    bc = c.EPI[0:64, 1024:2048]
                    c.s_ve.inc(v.tensor_copy(bc, c.PS[0:64, 512 * 6:512 * 6 + 1024]))
                    if gb + w > 0:
                        v.wait_ge(c.s_ge.h, gb + w)
                    c.s_ve.inc(v.tensor_mul(c.AMN[0:64, 0:1024],
                                            c.PS[0:64, 512 * 4:512 * 4 + 1024], bc))

            @block.gpsimd
            def _(g, ch=ch, vb=vb):
                for w in range(4):
                    g.wait_ge(c.s_ve.h, vb + 3 * w + 3)
                    dstA = c.A2[0:64, 2048 * ch + 512 * w:2048 * ch + 512 * (w + 1)]
                    dstB = c.A2[64:128, 2048 * ch + 512 * w:2048 * ch + 512 * (w + 1)]
                    dma_ser(g, c.s_st, dstA, c.AMN[0:64, 0:512])
                    dma_ser(g, c.s_st, dstB, c.AMN[0:64, 512:1024])
                    g.wait_ge(c.s_st.h, c.s_st.n)
                    c.s_ge.inc(g.nop())



def emit_l1ffn(c, block):
    nc = c.nc
    ldb = c.s_ld.n

    @block.gpsimd
    def _(g):
        dma_ser(g, c.s_ld, c.FSM[:, 0:16], c.dr["f_W1"])
        dma_ser(g, c.s_ld, c.FSM[:, 16:32], c.dr["f_b1"])
        dma_ser(g, c.s_ld, c.FSM[:, 32:48], c.dr["f_W2"])
        dma_ser(g, c.s_ld, c.FSM[:, 64:65], c.dr["f_bin"].rearrange("(d a) -> d a", a=1))
        dma_ser(g, c.s_ld, c.FSM[:, 65:66], c.dr["f_b2"].rearrange("(d a) -> d a", a=1))
        gate_wf(g, c)
        dma_ser(g, c.s_ld, c.WF[0:64, 0:1024], c.dr["f_Wout"])

    load_bias_row(c, block, c.dr["f_bout"], gates=((c.s_te, c.s_te.n),))

    @block.vector
    def _(v):
        v.wait_ge(c.s_ld.h, ldb + 96)
        v.scalar_tensor_tensor(c.FSM[:, 48:64], c.FSM[:, 0:16], c.FSM[:, 64:65],
                               c.FSM[:, 16:32], ALU.mult, ALU.add)
        c.s_ve.inc(v.tensor_copy(c.WS[0:64, 0:1024], c.WF[0:64, 0:1024]))

    vbW = c.s_ve.n

    @block.gpsimd
    def _(g):
        g.wait_ge(c.s_ve.h, vbW)
        dma_ser(g, c.s_ld, c.WS[64:65, 0:1024], c.RBb[0:1, 0:1024])

    ldW = c.s_ld.n

    wr_in = stream_w_bf16(c, block, c.dr["f_Win"], NKC, 1024, wc=64)

    tb0 = c.s_te.n
    vb0 = c.s_ve.n

    @block.tensor
    def _(t):
        t.wait_ge(c.s_ve.h, vb0)
        for k in range(NKC):
            t.wait_ge(wr_in[k][0].h, wr_in[k][1])
            lhsT = c.WS[:, 1024 + 64 * k:1024 + 64 * (k + 1)]
            for w in range(4):
                t.matmul(c.PS[0:64, 512 * w:512 * (w + 1)], lhsT,
                         c.XN[:, 2048 * k + 512 * w:2048 * k + 512 * (w + 1)],
                         start=(k == 0), stop=False)
        for w in range(4):
            ins = t.matmul(c.PS[0:64, 512 * w:512 * (w + 1)], c.ZSB[0:1, 0:64],
                           c.RBb[0:1, 0:512], start=False, stop=True)
        c.s_te.inc(ins)

    sb0 = c.s_se.n
    vb1 = c.s_ve.n

    @block.scalar
    def _(s):
        s.wait_ge(c.s_te.h, tb0 + 1)
        for i in range(16):
            if i >= 1:
                s.wait_ge(c.s_ve.h, vb1 + i)
            ins = s.activation(c.EPI[0:64, 0:2048], c.PS[0:64, 0:2048], AF.Relu,
                               scale=c.FSM[:, i:i + 1], bias=c.FSM[:, 48 + i:48 + i + 1])
            c.s_se.inc(ins)

    gA3 = c.A3[:].bitcast(F32)[0:64, 0:2048]
    GBB = c.A3[0:65, 4096:6144]

    @block.vector
    def _(v):
        for i in range(16):
            v.wait_ge(c.s_se.h, sb0 + i + 1)
            if i == 0:
                ins = v.tensor_scalar(gA3, c.EPI[0:64, 0:2048],
                                      c.FSM[:, 32:33], None, ALU.mult)
            else:
                ins = v.scalar_tensor_tensor(gA3, c.EPI[0:64, 0:2048],
                                             c.FSM[:, 32 + i:32 + i + 1],
                                             gA3, ALU.mult, ALU.add)
            c.s_ve.inc(ins)
        v.tensor_scalar(GBB[0:64, :], gA3, c.FSM[:, 65:66], None, ALU.add)
        c.s_ve.inc(v.memset(GBB[64:65, :], 1.0))

    vbG = c.s_ve.n

    GBB2 = c.A3[0:65, 4096:6144]

    def tile_mms(t, j, b0):
        if j == 0:
            t.wait_ge(c.s_ld.h, ldW)
        lhsT = GBB2[:, 128 * j:128 * (j + 1)]
        t.matmul(bank(c, b0)[:, 0:512], lhsT, c.WS[0:65, 0:512], start=True, stop=True)
        return t.matmul(bank(c, b0 + 1)[:, 0:512], lhsT, c.WS[0:65, 512:1024],
                        start=True, stop=True)

    emit_tmout_sandwich(c, block, tile_mms, (c.s_ve, vbG), c.STATS[:, 112:128])


def emit_tmout_sandwich(c, block, tile_mms, gate, sums_dst):
    nc = c.nc
    tb = c.s_te.n
    vb = c.s_ve.n
    sb = c.s_se.n
    negm2 = c.STATS[:, 96:97]
    r2 = c.STATS[:, 97:98]
    v1t = c.STATS[:, 98:99]
    v2t = c.STATS[:, 99:100]
    ssq = c.STATS[:, 100:101]
    ssm = c.STATS[:, 101:102]
    junk = c.EPI[:, 0:1024]
    zn = c.EPI[:, 1024:2048]

    @block.tensor
    def _(t):
        if gate is not None:
            t.wait_ge(gate[0].h, gate[1])
        if vb:
            t.wait_ge(c.s_ve.h, vb)
        for j in range(NT):
            b0 = 0 if j % 2 == 0 else 2
            if j >= 2:
                t.wait_ge(c.s_ve.h, vb + 4 * (j - 1))
            ins = tile_mms(t, j, b0)
            c.s_te.inc(ins)

    lvl = OP_LVL

    @block.vector
    def _(v):
        for j in range(NT):
            b0 = 0 if j % 2 == 0 else 2
            pz = c.PS[:, 512 * b0:512 * b0 + 1024]
            v.wait_ge(c.s_te.h, tb + j + 1)
            if lvl >= 2:
                c.s_ve.inc(v.reduce_sum(ssm, pz, axis=AX))
            else:
                c.s_ve.inc(v.tensor_copy(zn, pz[:, 0:1024]))
            if lvl >= 2:
                v.wait_ge(c.s_se.h, sb + 3 * j + 1)
            v.tensor_scalar(negm2, ssm, -1.0 / D, None, ALU.mult)
            v.tensor_mul(v2t, negm2, negm2)
            v.tensor_scalar(v1t, ssq, 1.0 / D, None, ALU.mult)
            c.s_ve.inc(v.tensor_sub(v1t, v1t, v2t))
            if lvl >= 3:
                v.wait_ge(c.s_se.h, sb + 3 * j + 3)
                v.tensor_scalar(zn, pz, negm2, r2, ALU.add, ALU.mult)
            if lvl >= 4:
                ins = v.scalar_tensor_tensor(
                    xtile(c, j), zn, 1.0, xtile(c, j), ALU.mult, ALU.add,
                    accum_out=(sums_dst[:, j:j + 1] if sums_dst is not None else None))
                c.s_ve.inc(ins)
            else:
                c.s_ve.inc(v.nop())
            v.wait_ge(c.s_ve.h, c.s_ve.n)
            c.s_ve.inc(v.nop())

    @block.scalar
    def _(s):
        for j in range(NT):
            b0 = 0 if j % 2 == 0 else 2
            pz = c.PS[:, 512 * b0:512 * b0 + 1024]
            s.wait_ge(c.s_te.h, tb + j + 1)
            s.wait_ge(c.s_ve.h, vb + 4 * j + 1)
            if lvl >= 2:
                c.s_se.inc(s.activation(junk, pz, AF.Square, accum_out=ssq))
            else:
                c.s_se.inc(s.nop())
            if lvl >= 3:
                s.wait_ge(c.s_ve.h, vb + 4 * j + 2)
                c.s_se.inc(s.activation(v1t, v1t, AF.Ln, bias=c.EPSC[:]))
                c.s_se.inc(s.activation(r2, v1t, AF.Exp, scale=-0.5))
            else:
                c.s_se.inc(s.nop())
                c.s_se.inc(s.nop())



def emit_l0_attn(c, block, upto=4):
    load_bias_fm(c, block, c.dr["l0_bq"], 0)
    load_bias_fm(c, block, c.dr["l0_bk"], 1)
    load_bias_row(c, block, c.dr["l0_bv"])
    xn = lambda k, lo, hi: c.XN[:, 2048 * k + lo:2048 * k + hi]
    for q in range(4):
        wrq = stream_w_bf16(c, block, c.dr["l0_WqT"][:, 256 * q:256 * (q + 1)], NKC, 0,
                            wc=256, gate0=(c.s_te, c.s_te.n))
        wrk = stream_w_bf16(c, block, c.dr["l0_WkT"][:, 256 * q:256 * (q + 1)], NKC, 2048, wc=256)
        wrv = stream_w_bf16(c, block, c.dr["l0_WvT"][:, 256 * q:256 * (q + 1)], NKC, 4096, wc=256)
        emit_proj_fm(c, block, c.A3[:, 0:4096], 0, 0, wrq, xn, [2 * q, 2 * q + 1], bias_col=0)
        emit_proj_fm(c, block, c.A3[:, 4096:8192], 0, 2048, wrk, xn, [2 * q, 2 * q + 1], bias_col=1)
        emit_vprime_proj(c, block, c.A3[:, 8192:12352], 4, 4096, wrv, xn, 256,
                         None, 256 * q, zb=0)
        if upto >= 3:
            emit_attention_quarter(c, block, q, 0.125, c.A2)
    if upto >= 4:
        load_bias_row(c, block, c.dr["l0_bo"], gates=((c.s_te, c.s_te.n),))
        wro = stream_w_bf16(c, block, c.dr["l0_WoT"], NKC, 0, wc=1024,
                            gate0=(c.s_te, c.s_te.n), dst=c.A3)
        if OP_LVL >= 1:
            emit_oproj_sandwich(c, block, c.A2, wro, 0, c.STATS[:, 112:128],
                                gate=(c.s_ge, c.s_ge.n))


def emit_l1_attn(c, block):
    @block.vector
    def _(v):
        c.s_ve.inc(v.memset(c.RBb[0:1, 0:1024], 0.0))

    xn = lambda k, lo, hi: c.XN[:, 2048 * k + lo:2048 * k + hi]
    for q in range(4):
        wrq = stream_masked_w(c, block, c.dr["l1_qkvT"][:, 256 * q:256 * (q + 1)],
                              c.dr["l1_qkvmT"][:, 256 * q:256 * (q + 1)], NKC, 0, 256,
                              gate0=(c.s_te, c.s_te.n))
        wrk = stream_masked_w(c, block, c.dr["l1_qkvT"][:, 1024 + 256 * q:1024 + 256 * (q + 1)],
                              c.dr["l1_qkvmT"][:, 1024 + 256 * q:1024 + 256 * (q + 1)],
                              NKC, 2048, 256)
        wrv = stream_masked_w(c, block, c.dr["l1_qkvT"][:, 2048 + 256 * q:2048 + 256 * (q + 1)],
                              c.dr["l1_qkvmT"][:, 2048 + 256 * q:2048 + 256 * (q + 1)],
                              NKC, 4096, 256)
        emit_proj_fm(c, block, c.A3[:, 0:4096], 0, 0, wrq, xn, [2 * q, 2 * q + 1], bias_col=None)
        emit_proj_fm(c, block, c.A3[:, 4096:8192], 0, 2048, wrk, xn, [2 * q, 2 * q + 1], bias_col=None)
        emit_vprime_proj(c, block, c.A3[:, 8192:12352], 4, 4096, wrv, xn, 256,
                         None, 256 * q, zb=0)
        emit_attention_quarter(c, block, q, 1.0 / 32, c.A2)
    wro = stream_masked_w(c, block, c.dr["l1_outT"], c.dr["l1_outmT"], NKC, 0, 1024,
                          gate0=(c.s_te, c.s_te.n), dst=c.A3)
    teb = c.s_te.n
    ldb = c.s_ld.n

    @block.gpsimd
    def _(g):
        g.wait_ge(c.s_te.h, teb)
        dma_ser(g, c.s_ld, c.RB[0:1, 0:1024], c.dr["l1_ob"].rearrange("(a d) -> a d", a=1))
        dma_ser(g, c.s_ld, c.EPI[0:1, 0:1024], c.dr["l1_obm"].rearrange("(a d) -> a d", a=1))
        g.wait_ge(c.s_ld.h, c.s_ld.n)
        g.tensor_scalar(c.EPI[0:1, 0:1024], c.EPI[0:1, 0:1024], 0.0, None, ALU.is_gt)
        c.s_ge.inc(g.tensor_tensor(c.RBb[0:1, 0:1024], c.EPI[0:1, 0:1024],
                                   c.RB[0:1, 0:1024], ALU.mult))

    emit_oproj_sandwich(c, block, c.A2, wro, 0, c.STATS[:, 112:128],
                        gate=(c.s_ge, c.s_ge.n))



def emit_all(c, block):
    emit_init(c, block)

    ph = 99 if DEV_PHASE is None else DEV_PHASE
    if ph >= 1:
        negm, r = emit_ln_stats(c, block, lambda j: xtile(c, j), wait_sems=(c.s_ld,))
        emit_ln_apply_transpose(c, block, lambda j: xtile(c, j), negm, r, c.XN, c.A2)
    if ph >= 2:
        emit_l0_attn(c, block, upto=min(ph, 4))
    if N_SUB >= 2:
        negm, r = emit_ln_stats(c, block, lambda j: xtile(c, j),
                                have_sums=c.STATS[:, 112:128], wait_sems=(c.s_ve,))
        emit_ln_apply_transpose(c, block, lambda j: xtile(c, j), negm, r, c.XN, c.A2)
        emit_l0ffn_slots(c, block)
        if DEBUG_DUMPS:
            vdump = c.s_ve.n

            @block.gpsimd
            def _(g):
                g.wait_ge(c.s_ve.h, vdump)
                dma_ser(g, c.s_st, c.dbg_o, c.SL[0:16, 0:1024])
                g.wait_ge(c.s_st.h, c.s_st.n)
                c.s_ge.inc(g.nop())
        emit_l0ffn_experts(c, block)
        if DEBUG_DUMPS:
            vdump2 = c.s_ve.n

            @block.gpsimd
            def _(g):
                g.wait_ge(c.s_ve.h, vdump2)
                dma_ser(g, c.s_st, c.dbg_of, c.SL[0:16, 0:1024])
                dma_ser(g, c.s_st, c.dbg_h, c.SL[0:16, 2048:3072])
                g.wait_ge(c.s_st.h, c.s_st.n)
                c.s_ge.inc(g.nop())
        emit_l0ffn_cmha(c, block)
        if DEBUG_DUMPS:
            vdump3 = c.s_ge.n

            @block.gpsimd
            def _(g):
                g.wait_ge(c.s_ge.h, vdump3)
                dma_ser(g, c.s_st, c.dbg_a2, c.A2[:])
                dma_ser(g, c.s_st, c.dbg_ks, c.QSL[:])
                g.wait_ge(c.s_st.h, c.s_st.n)
                c.s_ge.inc(g.nop())
        load_bias_row(c, block, c.dr["c_bo"], gates=((c.s_te, c.s_te.n),))
        wr_co = stream_w_bf16(c, block, c.dr["c_Wo"], NKC, 0, wc=1024,
                              gate0=(c.s_te, c.s_te.n), dst=c.A3)
        emit_oproj_sandwich(c, block, c.A2, wr_co, 0, c.STATS[:, 112:128],
                            gate=(c.s_ge, c.s_ge.n))
    if N_SUB >= 3:
        negm, r = emit_ln_stats(c, block, lambda j: xtile(c, j),
                                have_sums=c.STATS[:, 112:128], wait_sems=(c.s_ve,))
        emit_ln_apply_transpose(c, block, lambda j: xtile(c, j), negm, r, c.XN, c.A2)
        emit_l1_attn(c, block)
    if N_SUB >= 4:
        negm, r = emit_ln_stats(c, block, lambda j: xtile(c, j),
                                have_sums=c.STATS[:, 112:128], wait_sems=(c.s_ve,))
        emit_ln_apply_transpose(c, block, lambda j: xtile(c, j), negm, r, c.XN, c.A2)
        emit_l1ffn(c, block)

    vfin = c.s_ve.n

    @block.gpsimd
    def _(g):
        g.wait_ge(c.s_ve.h, vfin)
        for j in range(NT):
            dma_ser(g, c.s_st, c.out[128 * j:128 * (j + 1), :], xtile(c, j))
        g.wait_ge(c.s_st.h, c.s_st.n)


def build_program():
    nc, c = build_nc()
    with nc.Block() as block:
        emit_all(c, wrap_block(block, c))
    return nc, c



def _prep_inputs(inputs):
    f = np.ascontiguousarray
    g = {}
    g["l0_WqT"] = f(inputs["L0_attn_Wq"].T)
    g["l0_WkT"] = f(inputs["L0_attn_Wk"].T)
    g["l0_WvT"] = f(inputs["L0_attn_Wv"].T)
    g["l0_WoT"] = f(inputs["L0_attn_Wo"].T)
    g["l0_bq"] = inputs["L0_attn_bq"]
    g["l0_bk"] = inputs["L0_attn_bk"]
    g["l0_bv"] = inputs["L0_attn_bv"]
    g["l0_bo"] = inputs["L0_attn_bo"]
    g["sq"] = f(inputs["L0_ffn_slot_query"][0])
    for p in ("d", "c"):
        for m in ("Wq", "Wk", "Wv", "Wo"):
            g[f"{p}_{m}"] = f(inputs[f"L0_ffn_{p}_{m}"])
        for m in ("bq", "bk", "bv", "bo"):
            g[f"{p}_{m}"] = inputs[f"L0_ffn_{p}_{m}"]
    g["eb2"] = inputs["L0_ffn_e_b2"]
    g["l1_qkvT"] = f(inputs["L1_attn_qkv_weight"].T)
    g["l1_qkvmT"] = f(inputs["L1_attn_qkv_weight_mask"].T)
    g["l1_outT"] = f(inputs["L1_attn_out_weight"].T)
    g["l1_outmT"] = f(inputs["L1_attn_out_weight_mask"].T)
    g["l1_ob"] = inputs["L1_attn_out_bias"]
    g["l1_obm"] = inputs["L1_attn_out_bias_mask"]
    g["f_Win"] = inputs["L1_ffn_Win"]
    g["f_bin"] = inputs["L1_ffn_bin_"]
    g["f_W1"] = inputs["L1_ffn_W1"]
    g["f_b1"] = inputs["L1_ffn_b1"]
    g["f_W2"] = inputs["L1_ffn_W2"]
    g["f_b2"] = inputs["L1_ffn_b2"]
    g["f_Wout"] = inputs["L1_ffn_Wout"]
    g["f_bout"] = inputs["L1_ffn_bout"]
    y = inputs["L0_attn_y"]
    in_maps = []
    for cid in range(NCORES):
        m = dict(g)
        m["y"] = f(y[cid])
        m["e_ln_g"] = f(inputs["L0_ffn_e_ln_g"][2 * cid:2 * cid + 2])
        m["e_ln_b"] = f(inputs["L0_ffn_e_ln_b"][2 * cid:2 * cid + 2])
        m["eW1"] = f(inputs["L0_ffn_e_W1"][2 * cid:2 * cid + 2])
        m["eWg"] = f(inputs["L0_ffn_e_Wg"][2 * cid:2 * cid + 2])
        m["eW2"] = f(inputs["L0_ffn_e_W2"][2 * cid:2 * cid + 2])
        m["eb1"] = f(inputs["L0_ffn_e_b1"][2 * cid:2 * cid + 2])
        m["ebg"] = f(inputs["L0_ffn_e_bg"][2 * cid:2 * cid + 2])
        in_maps.append(m)
    return in_maps


def kernel(**inputs):
    in_maps = _prep_inputs(inputs)
    nc, _ = build_program()
    res = run_bass_kernel_spmd(nc, in_maps, core_ids=list(range(NCORES)))
    return np.stack([res.results[cid]["out"] for cid in range(NCORES)], axis=0)
